# version 05
# speedup vs baseline: 1.9727x; 1.5269x over previous
import numpy as np
import ml_dtypes
import concourse.bass as bass
import concourse.mybir as mybir
from concourse.bass_utils import run_bass_kernel_spmd

F32 = mybir.dt.float32
BF16 = mybir.dt.bfloat16
I32 = mybir.dt.int32
AF = mybir.ActivationFunctionType
ALU = mybir.AluOpType
AX = mybir.AxisListType
BFNP = ml_dtypes.bfloat16

D = 1024
S = 2048
NSEQ = 2
T = NSEQ * S
KC = 8
NCORE = 8
W = 512
NP = S // W
EPS = 1e-5
PI = float(np.pi)

ENGS = ("sync", "tensor", "vector", "scalar", "gpsimd")


class Prog:

    def __init__(self, serial=False):
        self.serial = serial
        self.items = []
        self.cnt = {}
        for e in ENGS:
            self.cnt[(e, "c")] = 0
            self.cnt[(e, "d")] = 0
        self.wr = {}
        self.rd = {}

    def op(self, eng, fn, r=(), w=(), dma=False):
        kind = "d" if dma else "c"
        inc = 16 if dma else 1
        waits = {}

        def need(e2, k2, c2, raw=True):
            if (e2, k2) == (eng, kind) and k2 == "d":
                return
            if e2 == eng and kind == "d" and k2 == "c":
                return
            if not raw and e2 == eng and kind == "c" and k2 == "c":
                return
            if waits.get((e2, k2), 0) < c2:
                waits[(e2, k2)] = c2

        for k in r:
            lw = self.wr.get(k)
            if lw:
                need(*lw)
        for k in w:
            lw = self.wr.get(k)
            if lw:
                need(*lw, raw=False)
            for (e2, k2), c2 in self.rd.get(k, {}).items():
                need(e2, k2, c2, raw=False)
        if self.serial:
            for (e2, k2), c2 in self.cnt.items():
                if (e2, k2) != (eng, kind) and c2 > 0:
                    if not (e2 == eng and kind == "d" and k2 == "c"):
                        waits[(e2, k2)] = c2
        for (e2, k2), c2 in waits.items():
            assert c2 <= self.cnt[(e2, k2)], "dependency on future op (bug)"
        end = self.cnt[(eng, kind)] + inc
        self.items.append((eng, waits, fn, inc, kind))
        self.cnt[(eng, kind)] = end
        for k in r:
            self.rd.setdefault(k, {})[(eng, kind)] = end
        for k in w:
            self.wr[k] = (eng, kind, end)
            self.rd[k] = {}

    def barrier(self):
        snap = dict(self.cnt)
        for e in ENGS:
            waits = {ek: cv for ek, cv in snap.items()
                     if ek != (e, "c") and cv > 0}
            self.items.append((e, waits, lambda eng: eng.nop(), 1, "c"))
            self.cnt[(e, "c")] += 1
        self.wr.clear()
        self.rd.clear()

    def emit(self, nc, block):
        sems = {}
        guards = []
        for e in ENGS:
            for kind in ("c", "d"):
                g = nc.semaphore(f"prog_{e}_{kind}")
                sems[(e, kind)] = g.__enter__()
                guards.append(g)
        final = dict(self.cnt)

        def emit_engine(engine_name, eng_obj):
            watermark = {ek: 0 for ek in self.cnt}
            cnt = {ek: 0 for ek in self.cnt}
            for eng, waits, fn, inc, kind in self.items:
                if eng != engine_name:
                    continue
                for ek, c2 in sorted(waits.items()):
                    if c2 > watermark[ek]:
                        eng_obj.wait_ge(sems[ek], c2)
                        watermark[ek] = c2
                inst = fn(eng_obj)
                inst.then_inc(sems[(engine_name, kind)], inc)
                cnt[(engine_name, kind)] += inc
                if kind == "d":
                    eng_obj.wait_ge(sems[(engine_name, "d")], cnt[(engine_name, "d")])
                    watermark[(engine_name, "d")] = cnt[(engine_name, "d")]
            for ek, cv in final.items():
                if ek[0] != engine_name and cv > watermark[ek]:
                    eng_obj.wait_ge(sems[ek], cv)
            if final[(engine_name, "d")] > watermark[(engine_name, "d")]:
                eng_obj.wait_ge(sems[(engine_name, "d")], final[(engine_name, "d")])

        @block.sync
        def _(e):
            emit_engine("sync", e)

        @block.tensor
        def _(e):
            emit_engine("tensor", e)

        @block.vector
        def _(e):
            emit_engine("vector", e)

        @block.scalar
        def _(e):
            emit_engine("scalar", e)

        @block.gpsimd
        def _(e):
            emit_engine("gpsimd", e)

        return guards



def _bias_pc(b):
    return np.ascontiguousarray(np.asarray(b, np.float32).reshape(-1, 128).T)


def host_layout(inp):
    f = lambda a: np.asarray(a, np.float32)
    L = {}
    L["ppT"] = np.ascontiguousarray(f(inp["L0_attn_pos_phases"])[:S].T)
    for nm in ("Wpv", "Wk", "Wkv", "Wb1", "Wo"):
        L["a_" + nm] = f(inp["L0_attn_" + nm])
    L["a_Wg1a"] = np.ascontiguousarray(f(inp["L0_attn_Wg1"])[:D])
    L["a_Wg1b"] = np.ascontiguousarray(f(inp["L0_attn_Wg1"])[D:])
    L["a_Wg2"] = np.ascontiguousarray(f(inp["L0_attn_Wg2"]).reshape(8, 128).T)
    L["a_Wb2"] = np.ascontiguousarray(
        f(inp["L0_attn_Wb2"]).reshape(4, 128, 2).transpose(1, 0, 2).reshape(128, 8))
    ab = np.zeros((128, 64), np.float32)
    ab[:, 0:8] = _bias_pc(inp["L0_attn_bpv"])
    ab[:, 8:16] = _bias_pc(inp["L0_attn_bk"])
    ab[:, 16:24] = _bias_pc(inp["L0_attn_bkv"])
    ab[:, 24:32] = _bias_pc(inp["L0_attn_bg1"])
    ab[:, 32:40] = _bias_pc(inp["L0_attn_bo"])
    ab[:, 40:44] = _bias_pc(inp["L0_attn_bb1"])
    ab[:, 44:52] = _bias_pc(inp["L0_attn_ln_g"])
    ab[:, 52:60] = _bias_pc(inp["L0_attn_ln_b"])
    ab[0, 60] = f(inp["L0_attn_bg2"]).reshape(-1)[0]
    ab[0, 61] = f(inp["L0_attn_bb2"]).reshape(-1)[0]
    ab[0, 62] = f(inp["L0_attn_bb2"]).reshape(-1)[1]
    L["a_bias"] = ab
    perm = np.concatenate([np.arange(0, 2048, 2), np.arange(1, 2048, 2)])
    L["f_gwT"] = np.ascontiguousarray(
        f(inp["L0_ffn_gate_w"]).T.reshape(8, 128, 8).transpose(1, 0, 2).reshape(128, 64))
    L["f_gb"] = f(inp["L0_ffn_gate_b"]).reshape(1, 8)
    L["f_w1p"] = np.ascontiguousarray(f(inp["L0_ffn_w1"])[:, perm, :].transpose(0, 2, 1))
    L["f_w3p"] = np.ascontiguousarray(f(inp["L0_ffn_w3"])[:, perm, :].transpose(0, 2, 1))
    L["f_w2T"] = np.ascontiguousarray(f(inp["L0_ffn_w2"]).transpose(0, 2, 1))
    L["f_b13p"] = np.concatenate(
        [np.stack([_bias_pc(f(inp["L0_ffn_b1"])[e][perm]) for e in range(8)]),
         np.stack([_bias_pc(f(inp["L0_ffn_b3"])[e][perm]) for e in range(8)])], axis=2)
    L["f_b2"] = f(inp["L0_ffn_b2"])
    L["f_sw1p"] = np.ascontiguousarray(f(inp["L0_ffn_sw1"])[perm].T)
    L["f_sw3p"] = np.ascontiguousarray(f(inp["L0_ffn_sw3"])[perm].T)
    L["f_sw2T"] = np.ascontiguousarray(f(inp["L0_ffn_sw2"]).T)
    sb = np.zeros((128, 40), np.float32)
    sb[:, 0:16] = _bias_pc(f(inp["L0_ffn_sb1"])[perm])
    sb[:, 16:32] = _bias_pc(f(inp["L0_ffn_sb3"])[perm])
    sb[:, 32:40] = _bias_pc(inp["L0_ffn_sb2"])
    L["f_sbias"] = sb
    for nm in ("Wq", "Wk", "Wv"):
        L["c_" + nm] = f(inp["L1_attn_" + nm])
    cb = np.zeros((128, 24), np.float32)
    cb[:, 0:8] = _bias_pc(inp["L1_attn_bq"])
    cb[:, 8:16] = _bias_pc(inp["L1_attn_bk"])
    cb[:, 16:24] = _bias_pc(inp["L1_attn_bv"])
    L["c_bias"] = cb
    nwT = np.zeros((D, 45), np.float32)
    nb = np.zeros((45, 1), np.float32)
    b2pad = np.zeros((3, 48, D), np.float32)
    for g, tag in enumerate(("q", "k", "v")):
        nw = f(inp[f"L1_ffn_{tag}_node_w"])
        nwT[:, np.arange(15) * 3 + g] = nw.T
        nb[np.arange(15) * 3 + g, 0] = f(inp[f"L1_ffn_{tag}_node_b"])
        b2pad[g, np.arange(16) * 3 + g, :] = f(inp[f"L1_ffn_{tag}_b2"]).reshape(16, D)
        L[f"d_W1cat_{tag}"] = np.ascontiguousarray(
            f(inp[f"L1_ffn_{tag}_w1"]).transpose(1, 0, 2).reshape(D, 1024))
        L[f"d_W2cat_{tag}"] = np.ascontiguousarray(
            f(inp[f"L1_ffn_{tag}_w2"]).reshape(1024, D))
        L[f"d_b1cat_{tag}"] = _bias_pc(f(inp[f"L1_ffn_{tag}_b1"]).reshape(1024))
    L["d_nwT"] = np.ascontiguousarray(nwT.reshape(8, 128, 45).transpose(1, 0, 2).reshape(128, 360))
    L["d_nb"] = nb
    L["d_b2pad"] = np.ascontiguousarray(b2pad)

    L["iden_f"] = np.eye(128, dtype=np.float32)
    L["iden_bf"] = np.eye(128, dtype=BFNP)
    L["iota16"] = np.broadcast_to(np.arange(16, dtype=np.float32), (128, 16)).copy()
    e_all = np.zeros((48, 3 * 1024), BFNP)
    for g in range(3):
        for l in range(16):
            e_all[l * 3 + g, g * 1024 + l * 64 : g * 1024 + (l + 1) * 64] = 1
    L["E_all"] = e_all

    x = f(inp["L0_attn_x"])
    in_maps = []
    for c in range(NCORE):
        m = dict(L)
        m["x_fm"] = np.ascontiguousarray(
            x[2 * c : 2 * c + 2].transpose(2, 0, 1).reshape(D, T))
        in_maps.append(m)
    return in_maps



class Ctx:
    pass


def build_program(dbg=False, stop_after="D", serial=False, detect_races=True, a_parts=6,
                  serial_a=False):
    nc = bass.Bass("TRN2", target_bir_lowering=False, detect_race_conditions=detect_races)
    P = Prog(serial=serial or serial_a)
    c = Ctx()
    c.nc, c.P = nc, P

    def din(name, shape):
        return nc.dram_tensor(name, list(shape), F32, kind="ExternalInput").ap()

    c.x_fm = din("x_fm", (D, T))
    names = [
        ("ppT", (D, S)), ("a_Wpv", (D, D)), ("a_Wk", (D, D)), ("a_Wkv", (D, D)),
        ("a_Wb1", (D, 512)), ("a_Wo", (D, D)), ("a_Wg1a", (D, D)), ("a_Wg1b", (D, D)),
        ("a_Wg2", (128, 8)), ("a_Wb2", (128, 8)), ("a_bias", (128, 64)),
        ("f_gwT", (128, 64)), ("f_gb", (1, 8)),
        ("f_w1p", (8, D, 2048)), ("f_w3p", (8, D, 2048)), ("f_w2T", (8, D, D)),
        ("f_b13p", (8, 128, 32)), ("f_b2", (8, D)),
        ("f_sw1p", (D, 2048)), ("f_sw3p", (D, 2048)), ("f_sw2T", (D, D)),
        ("f_sbias", (128, 40)),
        ("c_Wq", (D, D)), ("c_Wk", (D, D)), ("c_Wv", (D, D)), ("c_bias", (128, 24)),
        ("d_nwT", (128, 360)), ("d_nb", (45, 1)), ("d_b2pad", (3, 48, D)),
        ("d_W1cat_q", (D, 1024)), ("d_W1cat_k", (D, 1024)), ("d_W1cat_v", (D, 1024)),
        ("d_W2cat_q", (1024, D)), ("d_W2cat_k", (1024, D)), ("d_W2cat_v", (1024, D)),
        ("d_b1cat_q", (128, 8)), ("d_b1cat_k", (128, 8)), ("d_b1cat_v", (128, 8)),
    ]
    c.I = {nm: din(nm, shp) for nm, shp in names}
    c.I["iden_f"] = din("iden_f", (128, 128))
    c.I["iota16"] = din("iota16", (128, 16))
    c.I["iden_bf"] = nc.dram_tensor("iden_bf", [128, 128], BF16, kind="ExternalInput").ap()
    c.I["E_all"] = nc.dram_tensor("E_all", [48, 3 * 1024], BF16, kind="ExternalInput").ap()

    c.out_fm = nc.dram_tensor("out_fm", [D, T], F32, kind="ExternalOutput").ap()
    sk = "ExternalOutput" if dbg else "Internal"
    c.x1d = nc.dram_tensor("x1d", [D, T], F32, kind=sk).ap()
    c.x2d = nc.dram_tensor("x2d", [D, T], F32, kind=sk).ap()
    c.x3d = nc.dram_tensor("x3d", [D, T], F32, kind=sk).ap()
    c.ln1d = nc.dram_tensor("ln1d", [D, T], F32, kind=sk).ap()
    c.ctxd = nc.dram_tensor("ctxd", [D, T], F32, kind=sk).ap()
    c.cwdT = nc.dram_tensor("cwdT", [8, T], F32, kind=sk).ap()
    c.tapb = nc.dram_tensor("tapb", [128, 3 * 8192], BF16, kind=sk).ap() if dbg else None
    c.tapf = nc.dram_tensor("tapf", [128, 6144], F32, kind=sk).ap() if dbg else None
    c.probd = nc.dram_tensor("probd", [128, 16 * 1040], BF16, kind=sk).ap() if dbg else None

    c.qd = nc.dram_tensor("qd", [D, T], BF16, kind="Internal").ap()
    c.kd = nc.dram_tensor("kd", [D, T], BF16, kind="Internal").ap()
    c.vd = nc.dram_tensor("vd", [D, T], BF16, kind="Internal").ap()
    c.w1b = nc.dram_tensor("w1b", [8, D, 2048], BF16, kind="Internal").ap()
    c.w3b = nc.dram_tensor("w3b", [8, D, 2048], BF16, kind="Internal").ap()
    c.w2b = nc.dram_tensor("w2b", [8, D, D], BF16, kind="Internal").ap()
    c.sw1b = nc.dram_tensor("sw1b", [D, 2048], BF16, kind="Internal").ap()
    c.sw3b = nc.dram_tensor("sw3b", [D, 2048], BF16, kind="Internal").ap()
    c.sw2b = nc.dram_tensor("sw2b", [D, D], BF16, kind="Internal").ap()

    import contextlib
    with contextlib.ExitStack() as stack:
        ent = lambda g: stack.enter_context(g)
        c.iden_bf = ent(nc.sbuf_tensor("iden_bf_s", [128, 128], BF16, align_bytes=512))
        c.iden_f = ent(nc.sbuf_tensor("iden_f_s", [128, 128], F32, align_bytes=512))
        c.ones_r_bf = ent(nc.sbuf_tensor("ones_r_bf_s", [1, 128], BF16, align_bytes=512))
        c.ones_r_f = ent(nc.sbuf_tensor("ones_r_f_s", [1, 128], F32, align_bytes=512))
        c.ones_c_bf = ent(nc.sbuf_tensor("ones_c_bf_s", [128, 1], BF16, align_bytes=512))
        c.iota16 = ent(nc.sbuf_tensor("iota16_s", [128, 16], F32, align_bytes=512))
        c.ones_bf = ent(nc.sbuf_tensor("ones_bf_s", [128, S], BF16, align_bytes=512))
        c.halfpi = ent(nc.sbuf_tensor("halfpi_s", [128, 1], F32, align_bytes=512))
        c.rows = ent(nc.sbuf_tensor("rows", [1, 5 * W], F32, align_bytes=512))
        c.ps = [ent(nc.psum_tensor(f"ps_{i}", [128, 512], F32)) for i in range(5)]
        c.ps_r1 = ent(nc.psum_tensor("ps_r1", [1, 512], F32))
        c.ps_r2 = ent(nc.psum_tensor("ps_r2", [1, 512], F32))
        c.ps_tr = ent(nc.psum_tensor("ps_tr", [128, 256], BF16))
        assert nc.psum_banks_remaining >= 0

        for dst, srcap in [(c.iden_bf, c.I["iden_bf"]), (c.iden_f, c.I["iden_f"]),
                           (c.iota16, c.I["iota16"])]:
            P.op("sync", lambda e, d=dst, s=srcap: e.dma_start(out=d[:], in_=s),
                 w=[d_key(dst)], dma=True)
        P.op("vector", lambda e: e.memset(c.ones_c_bf[:], 1.0), w=["ones_c"])
        P.op("vector", lambda e: e.memset(c.ones_r_bf[:], 1.0), w=["ones_rb"])
        P.op("vector", lambda e: e.memset(c.ones_r_f[:], 1.0), w=["ones_rf"])
        P.op("vector", lambda e: e.memset(c.ones_bf[:], 1.0), w=["ones_bf"])
        P.op("vector", lambda e: e.memset(c.halfpi[:], PI / 2), w=["halfpi"])
        P.barrier()

        if a_parts >= 0 and a_parts != 90:
            emit_precast(c)
        P.barrier()
        emit_phaseA(c, a_parts)
        P.barrier()
        if serial_a:
            P.serial = serial
        if stop_after != "A":
            emit_phaseB(c)
            P.barrier()
        if stop_after not in ("A", "B"):
            emit_phaseC(c)
            P.barrier()
        if stop_after not in ("A", "B", "C"):
            emit_phaseD(c)
            P.barrier()
        if stop_after in ("A", "B", "C"):
            src = {"A": c.x1d, "B": c.x2d, "C": c.x3d}[stop_after]
            emit_dram_copy(c, src, c.out_fm)
            P.barrier()

        with nc.Block() as block:
            P.emit(nc, block)
    return nc


def d_key(t):
    return getattr(t, "name", None) or str(id(t))


import contextlib


_PAN_N = [0]
_UN = [0]


def _sb(nc, base, shape, dtype):
    _UN[0] += 1
    return nc.sbuf_tensor(f"{base}_{_UN[0]}", shape, dtype, align_bytes=512)


class Pan:
    pass


@contextlib.contextmanager
def pan_scope(c):
    _PAN_N[0] += 1
    n = _PAN_N[0]
    with (
        c.nc.sbuf_tensor(f"pan_x{n}", [128, KC * W], F32, align_bytes=512) as px,
        c.nc.sbuf_tensor(f"pan_l{n}", [128, KC * W], F32, align_bytes=512) as pl,
        c.nc.sbuf_tensor(f"pan_b{n}", [128, KC * W], BF16, align_bytes=512) as pb,
    ):
        pan = Pan()
        pan.x, pan.l, pan.b = px, pl, pb
        yield pan
    c.P.barrier()



def emit_dram_copy(c, src, dst):
    with pan_scope(c) as pan:
        _emit_dram_copy(c, pan, src, dst)


def _emit_dram_copy(c, pan, src, dst):
    P = c.P
    for ch in range(KC):
        for p in range(2 * NP):
            P.op("sync", lambda e, ch=ch, p=p: e.dma_start(
                out=pan.x[:, 0:W], in_=src[ch * 128:(ch + 1) * 128, p * W:(p + 1) * W]),
                w=["panx"], dma=True)
            P.op("sync", lambda e, ch=ch, p=p: e.dma_start(
                out=dst[ch * 128:(ch + 1) * 128, p * W:(p + 1) * W], in_=pan.x[:, 0:W]),
                r=["panx"], w=["dcopy"], dma=True)


def load_weight_bf(c, stage, dram_ap, wsb, key, ncol, stage_key="wstage"):
    P = c.P
    for k in range(KC):
        P.op("sync", lambda e, k=k: e.dma_start(
            out=stage[:, 0:ncol], in_=dram_ap[k * 128:(k + 1) * 128, 0:ncol]),
            w=[stage_key], dma=True)
        P.op("vector", lambda e, k=k: e.tensor_copy(
            wsb[:, k * ncol:(k + 1) * ncol], stage[:, 0:ncol]),
            r=[stage_key], w=[key])


def emit_ln_panel(c, pan, src_dram, col0, lnfm=None, lnfm_stride=0, lnfm_off=0,
                  store_f32=None, cb=None, cb_arg=None):
    P = c.P
    for ch in range(KC):
        P.op("sync", lambda e, ch=ch: e.dma_start(
            out=pan.x[:, ch * W:(ch + 1) * W],
            in_=src_dram[ch * 128:(ch + 1) * 128, col0:col0 + W]),
            w=[("panx", ch)], dma=True)
        P.op("vector", lambda e, ch=ch: e.tensor_copy(
            pan.b[:, ch * W:(ch + 1) * W], pan.x[:, ch * W:(ch + 1) * W]),
            r=[("panx", ch)], w=[("panb", ch)])
    for ch in range(KC):
        P.op("tensor", lambda e, ch=ch: e.matmul(
            c.ps_r1[:], c.ones_c_bf[:], pan.b[:, ch * W:(ch + 1) * W],
            start=(ch == 0), stop=(ch == KC - 1)),
            r=[("panb", ch), "ones_c"], w=["ps_r1"])
    for ch in range(KC):
        P.op("scalar", lambda e, ch=ch: e.activation(
            pan.l[:, ch * W:(ch + 1) * W], pan.b[:, ch * W:(ch + 1) * W], AF.Square),
            r=[("panb", ch)], w=[("sq", ch)])
        P.op("vector", lambda e, ch=ch: e.tensor_copy(
            pan.b[:, ch * W:(ch + 1) * W], pan.l[:, ch * W:(ch + 1) * W]),
            r=[("sq", ch)], w=[("panb", ch), ("panb_sq", ch)])
    for ch in range(KC):
        P.op("tensor", lambda e, ch=ch: e.matmul(
            c.ps_r2[:], c.ones_c_bf[:], pan.b[:, ch * W:(ch + 1) * W],
            start=(ch == 0), stop=(ch == KC - 1)),
            r=[("panb_sq", ch), "ones_c"], w=["ps_r2"])
    R = c.rows
    P.op("vector", lambda e: e.tensor_scalar(R[:, 0:W], c.ps_r1[:], 1.0 / D, None, ALU.mult),
         r=["ps_r1"], w=["row_mu"])
    P.op("vector", lambda e: e.tensor_scalar(R[:, W:2 * W], c.ps_r2[:], 1.0 / D, None, ALU.mult),
         r=["ps_r2"], w=["row_msq"])
    P.op("vector", lambda e: e.tensor_tensor(R[:, 2 * W:3 * W], R[:, 0:W], R[:, 0:W], ALU.mult),
         r=["row_mu"], w=["row_v"])
    P.op("vector", lambda e: e.tensor_tensor(R[:, W:2 * W], R[:, W:2 * W], R[:, 2 * W:3 * W], ALU.subtract),
         r=["row_msq", "row_v"], w=["row_msq"])
    P.op("vector", lambda e: e.tensor_scalar(R[:, W:2 * W], R[:, W:2 * W], EPS, None, ALU.add),
         r=["row_msq"], w=["row_msq"])
    P.op("scalar", lambda e: e.activation(R[:, 2 * W:3 * W], R[:, W:2 * W], AF.Sqrt),
         r=["row_msq"], w=["row_v"])
    P.op("vector", lambda e: e.reciprocal(R[:, 3 * W:4 * W], R[:, 2 * W:3 * W]),
         r=["row_v"], w=["row_rstd"])
    P.op("tensor", lambda e: e.matmul(c.ps[3][:], c.ones_r_f[:], R[0:1, 0:W], start=True, stop=True),
         r=["row_mu", "ones_rf"], w=["ps3"])
    P.op("tensor", lambda e: e.matmul(c.ps[4][:], c.ones_r_f[:], R[0:1, 3 * W:4 * W], start=True, stop=True),
         r=["row_rstd", "ones_rf"], w=["ps4"])
    for ch in range(KC):
        P.op("vector", lambda e, ch=ch: e.tensor_tensor(
            pan.l[:, ch * W:(ch + 1) * W], pan.x[:, ch * W:(ch + 1) * W], c.ps[3][:], ALU.subtract),
            r=[("panx", ch), "ps3"], w=[("panl", ch)])
        P.op("vector", lambda e, ch=ch: e.tensor_tensor(
            pan.l[:, ch * W:(ch + 1) * W], pan.l[:, ch * W:(ch + 1) * W], c.ps[4][:], ALU.mult),
            r=[("panl", ch), "ps4"], w=[("panl", ch)])

        if lnfm is not None:
            P.op("scalar", lambda e, ch=ch: e.copy(
                lnfm[:, ch * lnfm_stride + lnfm_off: ch * lnfm_stride + lnfm_off + W],
                pan.l[:, ch * W:(ch + 1) * W]),
                r=[("panl", ch)], w=[("lnfm", ch, lnfm_off // W)])
        if store_f32 is not None:
            P.op("sync", lambda e, ch=ch: e.dma_start(
                out=store_f32[ch * 128:(ch + 1) * 128, col0:col0 + W],
                in_=pan.l[:, ch * W:(ch + 1) * W]),
                r=[("panl", ch)], w=[("lnst", ch)], dma=True)
    if cb is not None:
        cb(cb_arg, pan)


def emit_closing(c, pan, branch_dram, stream_dram, out_dram, seq, branch_key=None):
    P = c.P
    bk = [branch_key] if branch_key else []
    for p in range(NP):
        col0 = seq * S + p * W
        for ch in range(KC):
            P.op("sync", lambda e, ch=ch, col0=col0: e.dma_start(
                out=pan.x[:, ch * W:(ch + 1) * W],
                in_=branch_dram[ch * 128:(ch + 1) * 128, col0:col0 + W]),
                r=bk, w=[("panx", ch)], dma=True)
        emit_ln_core(c, pan)
        for ch in range(KC):
            P.op("sync", lambda e, ch=ch, col0=col0: e.dma_start(
                out=pan.x[:, ch * W:(ch + 1) * W],
                in_=stream_dram[ch * 128:(ch + 1) * 128, col0:col0 + W]),
                w=[("panx", ch)], dma=True)
            P.op("vector", lambda e, ch=ch: e.tensor_tensor(
                pan.l[:, ch * W:(ch + 1) * W], pan.l[:, ch * W:(ch + 1) * W],
                pan.x[:, ch * W:(ch + 1) * W], ALU.add),
                r=[("panx", ch), ("panl", ch)], w=[("panl", ch)])
            P.op("sync", lambda e, ch=ch, col0=col0: e.dma_start(
                out=out_dram[ch * 128:(ch + 1) * 128, col0:col0 + W],
                in_=pan.l[:, ch * W:(ch + 1) * W]),
                r=[("panl", ch)], w=[("xst", ch)], dma=True)


def emit_ln_core(c, pan):
    P = c.P
    for ch in range(KC):
        P.op("vector", lambda e, ch=ch: e.tensor_copy(
            pan.b[:, ch * W:(ch + 1) * W], pan.x[:, ch * W:(ch + 1) * W]),
            r=[("panx", ch)], w=[("panb", ch)])
    for ch in range(KC):
        P.op("tensor", lambda e, ch=ch: e.matmul(
            c.ps_r1[:], c.ones_c_bf[:], pan.b[:, ch * W:(ch + 1) * W],
            start=(ch == 0), stop=(ch == KC - 1)),
            r=[("panb", ch), "ones_c"], w=["ps_r1"])
    for ch in range(KC):
        P.op("scalar", lambda e, ch=ch: e.activation(
            pan.l[:, ch * W:(ch + 1) * W], pan.b[:, ch * W:(ch + 1) * W], AF.Square),
            r=[("panb", ch)], w=[("sq", ch)])
        P.op("vector", lambda e, ch=ch: e.tensor_copy(
            pan.b[:, ch * W:(ch + 1) * W], pan.l[:, ch * W:(ch + 1) * W]),
            r=[("sq", ch)], w=[("panb", ch), ("panb_sq", ch)])
    for ch in range(KC):
        P.op("tensor", lambda e, ch=ch: e.matmul(
            c.ps_r2[:], c.ones_c_bf[:], pan.b[:, ch * W:(ch + 1) * W],
            start=(ch == 0), stop=(ch == KC - 1)),
            r=[("panb_sq", ch), "ones_c"], w=["ps_r2"])
    R = c.rows
    P.op("vector", lambda e: e.tensor_scalar(R[:, 0:W], c.ps_r1[:], 1.0 / D, None, ALU.mult),
         r=["ps_r1"], w=["row_mu"])
    P.op("vector", lambda e: e.tensor_scalar(R[:, W:2 * W], c.ps_r2[:], 1.0 / D, None, ALU.mult),
         r=["ps_r2"], w=["row_msq"])
    P.op("vector", lambda e: e.tensor_tensor(R[:, 2 * W:3 * W], R[:, 0:W], R[:, 0:W], ALU.mult),
         r=["row_mu"], w=["row_v"])
    P.op("vector", lambda e: e.tensor_tensor(R[:, W:2 * W], R[:, W:2 * W], R[:, 2 * W:3 * W], ALU.subtract),
         r=["row_msq", "row_v"], w=["row_msq"])
    P.op("vector", lambda e: e.tensor_scalar(R[:, W:2 * W], R[:, W:2 * W], EPS, None, ALU.add),
         r=["row_msq"], w=["row_msq"])
    P.op("scalar", lambda e: e.activation(R[:, 2 * W:3 * W], R[:, W:2 * W], AF.Sqrt),
         r=["row_msq"], w=["row_v"])
    P.op("vector", lambda e: e.reciprocal(R[:, 3 * W:4 * W], R[:, 2 * W:3 * W]),
         r=["row_v"], w=["row_rstd"])
    P.op("tensor", lambda e: e.matmul(c.ps[3][:], c.ones_r_f[:], R[0:1, 0:W], start=True, stop=True),
         r=["row_mu", "ones_rf"], w=["ps3"])
    P.op("tensor", lambda e: e.matmul(c.ps[4][:], c.ones_r_f[:], R[0:1, 3 * W:4 * W], start=True, stop=True),
         r=["row_rstd", "ones_rf"], w=["ps4"])
    for ch in range(KC):
        P.op("vector", lambda e, ch=ch: e.tensor_tensor(
            pan.l[:, ch * W:(ch + 1) * W], pan.x[:, ch * W:(ch + 1) * W], c.ps[3][:], ALU.subtract),
            r=[("panx", ch), "ps3"], w=[("panl", ch)])
        P.op("vector", lambda e, ch=ch: e.tensor_tensor(
            pan.l[:, ch * W:(ch + 1) * W], pan.l[:, ch * W:(ch + 1) * W], c.ps[4][:], ALU.mult),
            r=[("panl", ch), "ps4"], w=[("panl", ch)])



def emit_precast(c):
    nc, P = c.nc, c.P
    import contextlib
    with contextlib.ExitStack() as st:
        stage = st.enter_context(nc.sbuf_tensor("pc_stage", [128, 2048], F32, align_bytes=512))
        stageb = st.enter_context(nc.sbuf_tensor("pc_stageb", [128, 2048], BF16, align_bytes=512))
        jobs = []
        for e8 in range(8):
            jobs.append((c.I["f_w1p"][e8], c.w1b[e8], 2048))
            jobs.append((c.I["f_w3p"][e8], c.w3b[e8], 2048))
            jobs.append((c.I["f_w2T"][e8], c.w2b[e8], 1024))
        jobs.append((c.I["f_sw1p"], c.sw1b, 2048))
        jobs.append((c.I["f_sw3p"], c.sw3b, 2048))
        jobs.append((c.I["f_sw2T"], c.sw2b, 1024))
        for src, dst, ncol in jobs:
            for k in range(KC):
                P.op("sync", lambda e, src=src, k=k, ncol=ncol: e.dma_start(
                    out=stage[:, 0:ncol], in_=src[k * 128:(k + 1) * 128, 0:ncol]),
                    w=["pcs"], dma=True)
                P.op("vector", lambda e, k=k, ncol=ncol: e.tensor_copy(
                    stageb[:, 0:ncol], stage[:, 0:ncol]),
                    r=["pcs"], w=["pcb"])
                P.op("sync", lambda e, dst=dst, k=k, ncol=ncol: e.dma_start(
                    out=dst[k * 128:(k + 1) * 128, 0:ncol], in_=stageb[:, 0:ncol]),
                    r=["pcb"], w=["pcd"], dma=True)



def emit_phaseA(c, a_parts=6):
    nc, P = c.nc, c.P
    S2 = 1024
    NP2 = S2 // W
    import contextlib
    with contextlib.ExitStack() as st:
        ent = st.enter_context
        kvr = ent(nc.sbuf_tensor("A_kvr", [128, KC * S2], BF16, align_bytes=512))
        wsb = ent(nc.sbuf_tensor("A_wsb", [128, KC * 1024], BF16, align_bytes=512))
        wsb2 = ent(nc.sbuf_tensor("A_wsb2", [128, KC * 1024], BF16, align_bytes=512))
        wstage = ent(nc.sbuf_tensor("A_wstage", [128, 1024], F32, align_bytes=512))
        bias = ent(nc.sbuf_tensor("A_bias", [128, 72], F32, align_bytes=512))
        fsl = ent(nc.sbuf_tensor("A_f32", [128, 3 * S2], F32, align_bytes=512))
        i32t = ent(nc.sbuf_tensor("A_i32", [128, S2], I32, align_bytes=512))
        bfs = ent(nc.sbuf_tensor("A_bfs", [128, 8 * S2], BF16, align_bytes=512))
        kcb = ent(nc.sbuf_tensor("A_kcb", [128, S2 + 1], BF16, align_bytes=512))
        ksb = ent(nc.sbuf_tensor("A_ksb", [128, S2 + 1], BF16, align_bytes=512))
        grow = ent(nc.sbuf_tensor("A_grow", [1, 2 * S2], F32, align_bytes=512))
        growb = ent(nc.sbuf_tensor("A_growb", [1, S2], BF16, align_bytes=512))
        wb2sb = ent(nc.sbuf_tensor("A_wb2", [128, 8], F32, align_bytes=512))
        wb2d_bf = ent(nc.sbuf_tensor("A_wb2d_bf", [128, 4], BF16, align_bytes=512))
        wg2sb = ent(nc.sbuf_tensor("A_wg2", [128, 8], BF16, align_bytes=512))
        wg2f = ent(nc.sbuf_tensor("A_wg2f", [128, 8], F32, align_bytes=512))
        ones_bf2 = ent(nc.sbuf_tensor("A_ones2", [128, S2], BF16, align_bytes=512))
        onesr2 = ent(nc.sbuf_tensor("A_onesr2", [1, S2], F32, align_bytes=512))
        carry = ent(nc.sbuf_tensor("A_carry", [128, 32], F32, align_bytes=512))
        lncar = ent(nc.sbuf_tensor("A_lncar", [128, 8], BF16, align_bytes=512))
        kscar = ent(nc.sbuf_tensor("A_kscar", [128, 16], BF16, align_bytes=512))
        gccar = ent(nc.sbuf_tensor("A_gccar", [1, 1], F32, align_bytes=512))
        lnfm = ent(nc.sbuf_tensor("A_lnfm", [128, KC * (S2 + 1)], BF16, align_bytes=512))

        F0, F1, F2 = (fsl[:, i * S2:(i + 1) * S2] for i in range(3))
        SL = lambda i: bfs[:, i * S2:(i + 1) * S2]

        P.op("sync", lambda e: e.dma_start(out=bias[:, 0:64], in_=c.I["a_bias"]),
             w=[("Abias", k2) for k2 in (0, 8, 16, 24, 32, 40, 44, 52, 60, 61)], dma=True)
        P.op("sync", lambda e: e.dma_start(out=wb2sb[:, 0:8], in_=c.I["a_Wb2"]),
             w=["wb2sb"], dma=True)
        for cc in range(4):
            P.op("vector", lambda e, cc=cc: e.tensor_tensor(
                bias[:, 64 + cc:65 + cc], wb2sb[:, cc * 2:cc * 2 + 1],
                wb2sb[:, cc * 2 + 1:cc * 2 + 2], ALU.subtract),
                r=["wb2sb"], w=[("Abias", 64 + cc)])
        P.op("vector", lambda e: e.tensor_copy(wb2d_bf[:], bias[:, 64:68]),
             r=[("Abias", 64), ("Abias", 65), ("Abias", 66), ("Abias", 67)], w=["wb2d_bf"])
        P.op("vector", lambda e: e.tensor_tensor(
            bias[0:1, 63:64], bias[0:1, 61:62], bias[0:1, 62:63], ALU.subtract),
            r=[("Abias", 61)], w=[("Abias", 63)])
        P.op("sync", lambda e: e.dma_start(out=wg2f[:, 0:8], in_=c.I["a_Wg2"]),
             w=["wg2f"], dma=True)
        P.op("vector", lambda e: e.tensor_copy(wg2sb[:, 0:8], wg2f[:, 0:8]),
             r=["wg2f"], w=["A_wg2"])
        P.op("vector", lambda e: e.memset(ones_bf2[:], 1.0), w=["ones2"])
        P.op("vector", lambda e: e.memset(onesr2[:], 1.0), w=["onesr2"])

        def _half(hs):
            first = hs % 2 == 0
            base = hs * S2
            for ch in range(KC):
                if first:
                    P.op("vector", lambda e, ch=ch: e.memset(
                        lnfm[:, ch * (S2 + 1):ch * (S2 + 1) + 1], 0.0), w=[("lnfm", ch, "z")])
                else:
                    P.op("vector", lambda e, ch=ch: e.tensor_copy(
                        lnfm[:, ch * (S2 + 1):ch * (S2 + 1) + 1], lncar[:, ch:ch + 1]),
                        r=["lncar"], w=[("lnfm", ch, "z")])
            if first:
                P.op("vector", lambda e: e.memset(kcb[:, 0:1], 0.0), w=["kczero"])
                P.op("vector", lambda e: e.memset(ksb[:, 0:1], 0.0), w=["kszero"])
                P.op("vector", lambda e: e.memset(carry[:], 0.0), w=["carry"])
                P.op("vector", lambda e: e.memset(gccar[:], 0.0), w=["gccar"])

            with pan_scope(c) as pan1:
                for p in range(NP2):
                    emit_ln_panel(c, pan1, c.x_fm, base + p * W, lnfm=lnfm,
                                  lnfm_stride=S2 + 1, lnfm_off=1 + p * W, store_f32=c.ln1d)
            if first:
                for ch in range(KC):
                    P.op("vector", lambda e, ch=ch: e.tensor_copy(
                        lncar[:, ch:ch + 1], lnfm[:, ch * (S2 + 1) + S2:ch * (S2 + 1) + S2 + 1]),
                        r=[("lnfm", ch, NP2 - 1)], w=["lncar"])

            def lsl(k, t0, w_, sh=0):
                return lnfm[:, k * (S2 + 1) + 1 + t0 - sh: k * (S2 + 1) + 1 + t0 - sh + w_]

            if a_parts < 2 or a_parts in (90, 91):
                return
            c.P.barrier()
            load_weight_bf(c, wstage, c.I["a_Wg1a"], wsb, "A_wg1a", 1024)
            load_weight_bf(c, wstage, c.I["a_Wg1b"], wsb2, "A_wg1b", 1024)
            ghb = SL(3)
            for p in range(NP2):
                t0 = p * W
                for fc in range(KC):
                    for k in range(KC):
                        P.op("tensor", lambda e, fc=fc, k=k, t0=t0: e.matmul(
                            c.ps[0][:], wsb[:, k * 1024 + fc * 128:k * 1024 + fc * 128 + 128],
                            lsl(k, t0, W), start=(k == 0), stop=False),
                            r=["A_wg1a", ("lnfm", k, t0 // W)], w=["ps0"])
                        P.op("tensor", lambda e, fc=fc, k=k, t0=t0: e.matmul(
                            c.ps[0][:], wsb2[:, k * 1024 + fc * 128:k * 1024 + fc * 128 + 128],
                            lsl(k, t0, W, sh=1), start=False, stop=(k == KC - 1)),
                            r=["A_wg1b", ("lnfm", k, t0 // W),
                               ("lnfm", k, t0 // W - 1 if t0 else "z")], w=["ps0"])
                    P.op("scalar", lambda e, fc=fc, t0=t0: e.activation(
                        ghb[:, t0:t0 + W], c.ps[0][:], AF.Gelu, bias=bias[:, 24 + fc:25 + fc]),
                        r=["ps0", ("Abias", 24)], w=["ghb"])
                    P.op("tensor", lambda e, fc=fc, t0=t0: e.matmul(
                        c.ps_r1[:], wg2sb[:, fc:fc + 1], ghb[:, t0:t0 + W],
                        start=(fc == 0), stop=(fc == KC - 1)),
                        r=["ghb", "A_wg2"], w=["ps_r1"])
                P.op("scalar", lambda e, t0=t0: e.activation(
                    grow[0:1, t0:t0 + W], c.ps_r1[:], AF.Sigmoid, bias=bias[0:1, 60:61]),
                    r=["ps_r1", ("Abias", 60)], w=["g_row"])
            P.op("vector", lambda e: e.tensor_copy(growb[0:1, :], grow[0:1, 0:S2]),
                 r=["g_row"], w=["growb"])
            for p in range(NP2):
                P.op("tensor", lambda e, p=p: e.matmul(
                    c.ps[0][:], c.ones_r_bf[:], growb[0:1, p * W:(p + 1) * W],
                    start=True, stop=True), r=["growb", "ones_rb"], w=["ps0"])
                P.op("vector", lambda e, p=p: e.tensor_copy(
                    SL(0)[:, p * W:(p + 1) * W], c.ps[0][:]), r=["ps0"], w=[("bcast", 0)])
            P.op("vector", lambda e: e.tensor_tensor_scan(
                grow[0:1, S2:2 * S2], onesr2[0:1, :], grow[0:1, 0:S2],
                gccar[0:1, 0:1], ALU.mult, ALU.add),
                r=["g_row", "onesr2", "gccar"], w=["gc_row"])
            P.op("vector", lambda e: e.tensor_copy(gccar[0:1, 0:1], grow[0:1, 2 * S2 - 1:2 * S2]),
                 r=["gc_row"], w=["gccar"])
            P.op("vector", lambda e: e.tensor_scalar(
                grow[0:1, S2:2 * S2], grow[0:1, S2:2 * S2], 1.0, None, ALU.max),
                r=["gc_row"], w=["gc_row"])
            P.op("scalar", lambda e: e.activation(
                grow[0:1, 0:S2], grow[0:1, S2:2 * S2], AF.Sqrt),
                r=["gc_row"], w=["sqgc"])
            P.op("vector", lambda e: e.reciprocal(grow[0:1, S2:2 * S2], grow[0:1, 0:S2]),
                 r=["sqgc"], w=["r_row"])
            P.op("vector", lambda e: e.tensor_scalar(
                grow[0:1, S2:2 * S2], grow[0:1, S2:2 * S2], 1.0 / 32.0, None, ALU.mult),
                r=["r_row"], w=["r_row"])
            P.op("vector", lambda e: e.tensor_copy(growb[0:1, :], grow[0:1, S2:2 * S2]),
                 r=["r_row"], w=["growb"])
            for p in range(NP2):
                P.op("tensor", lambda e, p=p: e.matmul(
                    c.ps[0][:], c.ones_r_bf[:], growb[0:1, p * W:(p + 1) * W],
                    start=True, stop=True), r=["growb", "ones_rb"], w=["ps0"])
                P.op("vector", lambda e, p=p: e.tensor_copy(
                    SL(2)[:, p * W:(p + 1) * W], c.ps[0][:]), r=["ps0"], w=[("bcast", 2)])
            load_weight_bf(c, wstage, c.I["a_Wb1"], wsb, "A_wb1", 512)
            for p in range(NP2):
                t0 = p * W
                for fc in range(4):
                    for k in range(KC):
                        P.op("tensor", lambda e, fc=fc, k=k, t0=t0: e.matmul(
                            c.ps[0][:], wsb[:, k * 512 + fc * 128:k * 512 + fc * 128 + 128],
                            lsl(k, t0, W), start=(k == 0), stop=(k == KC - 1)),
                            r=["A_wb1", ("lnfm", k, t0 // W)], w=["ps0"])
                    P.op("scalar", lambda e, fc=fc, t0=t0: e.activation(
                        ghb[:, t0:t0 + W], c.ps[0][:], AF.Gelu, bias=bias[:, 40 + fc:41 + fc]),
                        r=["ps0", ("Abias", 40)], w=["ghb"])
                    P.op("tensor", lambda e, fc=fc, t0=t0: e.matmul(
                        c.ps_r1[:], wb2d_bf[:, fc:fc + 1], ghb[:, t0:t0 + W],
                        start=(fc == 0), stop=(fc == 3)),
                        r=["ghb", "wb2d_bf"], w=["ps_r1"])
                P.op("scalar", lambda e: e.activation(
                    c.rows[0:1, 0:W], c.ps_r1[:], AF.Sigmoid, bias=bias[0:1, 63:64]),
                    r=["ps_r1", ("Abias", 63)], w=["row_mu"])
                if hs == 0 and p == 0 and c.tapf is not None:
                    P.op("sync", lambda e: e.dma_start(out=c.tapf[0:1, 4096:4608], in_=c.rows[0:1, 0:W]),
                         r=["row_mu"], w=["tpw"], dma=True)
                    P.op("sync", lambda e: e.dma_start(out=c.tapf[:, 4608:4610], in_=wb2d_bf[:, :].bitcast(F32)),
                         r=["wb2d_bf"], w=["tpw"], dma=True)
                    P.op("sync", lambda e: e.dma_start(out=c.tapf[:, 4610:4682],
                         in_=bias[:, 0:72]), r=[("Abias", 63), ("Abias", 64)], w=["tpw"], dma=True)
                    P.op("sync", lambda e: e.dma_start(out=c.tapf[:, 4682:4690],
                         in_=wb2sb[:, 0:8]), r=["wb2sb"], w=["tpw"], dma=True)
                P.op("vector", lambda e: e.tensor_copy(growb[0:1, 0:W], c.rows[0:1, 0:W]),
                     r=["row_mu"], w=["growb"])
                P.op("tensor", lambda e: e.matmul(
                    c.ps[0][:], c.ones_r_bf[:], growb[0:1, 0:W],
                    start=True, stop=True), r=["growb", "ones_rb"], w=["ps0"])
                P.op("vector", lambda e, p=p: e.tensor_copy(
                    SL(1)[:, p * W:(p + 1) * W], c.ps[0][:]), r=["ps0"], w=[("bcast", 1)])

            if a_parts < 3:
                return
            c.P.barrier()
            load_weight_bf(c, wstage, c.I["a_Wk"], wsb, "A_wk", 1024)
            load_weight_bf(c, wstage, c.I["a_Wkv"], wsb2, "A_wkv", 1024)
            for ch in range(KC):
                for p in range(NP2):
                    t0 = p * W
                    for k in range(KC):
                        P.op("tensor", lambda e, ch=ch, k=k, t0=t0: e.matmul(
                            c.ps[0][:], wsb[:, k * 1024 + ch * 128:k * 1024 + ch * 128 + 128],
                            lsl(k, t0, W), start=(k == 0), stop=(k == KC - 1)),
                            r=["A_wk", ("lnfm", k, t0 // W)], w=["ps0"])
                    P.op("scalar", lambda e, ch=ch, t0=t0: e.activation(
                        F0[:, t0:t0 + W], c.ps[0][:], AF.Tanh, bias=bias[:, 8 + ch:9 + ch]),
                        r=["ps0", ("Abias", 8)], w=["F0t"])
                    P.op("scalar", lambda e, t0=t0: e.activation(
                        ksb[:, 1 + t0:1 + t0 + W], F0[:, t0:t0 + W], AF.Sin, scale=PI),
                        r=["F0t"], w=["ksb"])
                    P.op("scalar", lambda e, t0=t0: e.activation(
                        F1[:, t0:t0 + W], F0[:, t0:t0 + W], AF.Abs),
                        r=["F0t"], w=["F1t"])
                    P.op("scalar", lambda e, t0=t0: e.activation(
                        kcb[:, 1 + t0:1 + t0 + W], F1[:, t0:t0 + W], AF.Sin,
                        scale=-PI, bias=c.halfpi[:, 0:1]),
                        r=["F1t", "halfpi"], w=["kcb"])
                    for k in range(KC):
                        P.op("tensor", lambda e, ch=ch, k=k, t0=t0: e.matmul(
                            c.ps[1][:], wsb2[:, k * 1024 + ch * 128:k * 1024 + ch * 128 + 128],
                            lsl(k, t0, W), start=(k == 0), stop=(k == KC - 1)),
                            r=["A_wkv", ("lnfm", k, t0 // W)], w=["ps1"])
                    P.op("vector", lambda e, ch=ch, t0=t0: e.tensor_scalar(
                        SL(4)[:, t0:t0 + W], c.ps[1][:], bias[:, 16 + ch:17 + ch], None, ALU.add),
                        r=["ps1", ("Abias", 16)], w=["kvt"])
                if first:
                    P.op("vector", lambda e, ch=ch: e.tensor_copy(
                        kscar[:, ch:ch + 1], kcb[:, S2:S2 + 1]), r=["kcb"], w=["kscar"])
                    P.op("vector", lambda e, ch=ch: e.tensor_copy(
                        kscar[:, 8 + ch:9 + ch], ksb[:, S2:S2 + 1]), r=["ksb"], w=["kscar"])
                else:
                    P.op("vector", lambda e, ch=ch: e.tensor_copy(
                        kcb[:, 0:1], kscar[:, ch:ch + 1]), r=["kscar"], w=["kczero", "kcb"])
                    P.op("vector", lambda e, ch=ch: e.tensor_copy(
                        ksb[:, 0:1], kscar[:, 8 + ch:9 + ch]), r=["kscar"], w=["kszero", "ksb"])
                P.op("vector", lambda e: e.tensor_tensor(SL(4)[:], SL(4)[:], SL(0)[:], ALU.mult),
                     r=["kvt", ("bcast", 0)], w=["kvt"])
                P.op("vector", lambda e: e.tensor_tensor(SL(5)[:], kcb[:, 0:S2], SL(4)[:], ALU.mult),
                     r=["kcb", "kczero", "kvt"], w=["bct"])
                P.op("vector", lambda e: e.tensor_tensor(SL(6)[:], ksb[:, 0:S2], SL(4)[:], ALU.mult),
                     r=["ksb", "kszero", "kvt"], w=["bst"])
                P.op("vector", lambda e, ch=ch: e.tensor_tensor_scan(
                    SL(7)[:], ones_bf2[:], SL(5)[:], carry[:, ch:ch + 1], ALU.mult, ALU.add),
                    r=["bct", "ones2", "carry"], w=["mkc"])
                P.op("vector", lambda e, ch=ch: e.tensor_copy(
                    carry[:, ch:ch + 1], SL(7)[:, S2 - 1:S2]), r=["mkc"], w=["carry"])
                P.op("vector", lambda e, ch=ch: e.tensor_tensor_scan(
                    SL(4)[:], ones_bf2[:], SL(6)[:], carry[:, 8 + ch:9 + ch], ALU.mult, ALU.add),
                    r=["bst", "ones2", "carry"], w=["mks"])
                P.op("vector", lambda e, ch=ch: e.tensor_copy(
                    carry[:, 8 + ch:9 + ch], SL(4)[:, S2 - 1:S2]), r=["mks"], w=["carry"])
                P.op("vector", lambda e: e.tensor_tensor(F1[:], kcb[:, 1:S2 + 1], SL(7)[:], ALU.mult),
                     r=["kcb", "mkc"], w=["F1t"])
                P.op("vector", lambda e: e.tensor_tensor(F2[:], ksb[:, 1:S2 + 1], SL(4)[:], ALU.mult),
                     r=["ksb", "mks"], w=["F2t"])
                P.op("vector", lambda e: e.tensor_tensor(F1[:], F1[:], F2[:], ALU.add),
                     r=["F1t", "F2t"], w=["F1t"])
                P.op("vector", lambda e, ch=ch: e.tensor_tensor(
                    kvr[:, ch * S2:(ch + 1) * S2], F1[:], SL(2)[:], ALU.mult),
                    r=["F1t", ("bcast", 2)], w=[("kvr", ch)])

            if hs == 0 and c.tapb is not None:
                P.op("sync", lambda e: e.dma_start(out=c.tapb[:, 0:8192], in_=kvr[:, :]),
                     r=[("kvr", c2) for c2 in range(KC)], w=["tap1"], dma=True)
            if a_parts < 4:
                return
            c.P.barrier()
            load_weight_bf(c, wstage, c.I["a_Wpv"], wsb, "A_wpv", 1024)
            pbase = (hs % 2) * S2
            for ch in range(KC):
                for p in range(NP2):
                    t0 = p * W
                    for k in range(KC):
                        P.op("tensor", lambda e, ch=ch, k=k, t0=t0: e.matmul(
                            c.ps[0][:], wsb[:, k * 1024 + ch * 128:k * 1024 + ch * 128 + 128],
                            lsl(k, t0, W), start=(k == 0), stop=(k == KC - 1)),
                            r=["A_wpv", ("lnfm", k, t0 // W)], w=["ps0"])
                    P.op("vector", lambda e, ch=ch, t0=t0: e.tensor_scalar(
                        SL(3)[:, t0:t0 + W], c.ps[0][:], bias[:, 0 + ch:1 + ch], None, ALU.add),
                        r=["ps0", ("Abias", 0)], w=["pvt"])
                P.op("sync", lambda e, ch=ch: e.dma_start(
                    out=F0[:, 0:S2], in_=c.I["ppT"][ch * 128:(ch + 1) * 128, pbase:pbase + S2]),
                    w=["F0t"], dma=True)
                P.op("vector", lambda e: e.tensor_scalar(F1[:], F0[:], 1.0 / (2 * PI), None, ALU.mult),
                     r=["F0t"], w=["F1t"])
                P.op("vector", lambda e: e.tensor_copy(i32t[:], F1[:]), r=["F1t"], w=["i32"])
                P.op("vector", lambda e: e.tensor_copy(F1[:], i32t[:]), r=["i32"], w=["F1t"])
                P.op("vector", lambda e: e.scalar_tensor_tensor(
                    F1[:], F1[:], -(2 * PI), F0[:], ALU.mult, ALU.add), r=["F1t", "F0t"], w=["F1t"])
                P.op("scalar", lambda e: e.activation(SL(6)[:], F1[:], AF.Sin), r=["F1t"], w=["pspos"])
                P.op("vector", lambda e: e.tensor_scalar(F2[:], F0[:], PI / 2, None, ALU.add),
                     r=["F0t"], w=["F2t"])
                P.op("vector", lambda e: e.tensor_scalar(F1[:], F2[:], 1.0 / (2 * PI), None, ALU.mult),
                     r=["F2t"], w=["F1t"])
                P.op("vector", lambda e: e.tensor_copy(i32t[:], F1[:]), r=["F1t"], w=["i32"])
                P.op("vector", lambda e: e.tensor_copy(F1[:], i32t[:]), r=["i32"], w=["F1t"])
                P.op("vector", lambda e: e.scalar_tensor_tensor(
                    F1[:], F1[:], -(2 * PI), F2[:], ALU.mult, ALU.add), r=["F1t", "F2t"], w=["F1t"])
                P.op("scalar", lambda e: e.activation(SL(5)[:], F1[:], AF.Sin), r=["F1t"], w=["pcpos"])
                P.op("vector", lambda e: e.tensor_tensor(SL(7)[:], SL(5)[:], SL(3)[:], ALU.mult),
                     r=["pcpos", "pvt"], w=["ac"])
                P.op("vector", lambda e: e.tensor_tensor(SL(4)[:], SL(6)[:], SL(3)[:], ALU.mult),
                     r=["pspos", "pvt"], w=["as"])
                P.op("vector", lambda e, ch=ch: e.tensor_tensor_scan(
                    SL(3)[:], ones_bf2[:], SL(7)[:], carry[:, 16 + ch:17 + ch], ALU.mult, ALU.add),
                    r=["ac", "ones2", "carry"], w=["mpc"])
                P.op("vector", lambda e, ch=ch: e.tensor_copy(
                    carry[:, 16 + ch:17 + ch], SL(3)[:, S2 - 1:S2]), r=["mpc"], w=["carry"])
                P.op("vector", lambda e, ch=ch: e.tensor_tensor_scan(
                    SL(7)[:], ones_bf2[:], SL(4)[:], carry[:, 24 + ch:25 + ch], ALU.mult, ALU.add),
                    r=["as", "ones2", "carry"], w=["mps"])
                P.op("vector", lambda e, ch=ch: e.tensor_copy(
                    carry[:, 24 + ch:25 + ch], SL(7)[:, S2 - 1:S2]), r=["mps"], w=["carry"])
                P.op("vector", lambda e: e.tensor_tensor(F1[:], SL(5)[:], SL(3)[:], ALU.mult),
                     r=["pcpos", "mpc"], w=["F1t"])
                P.op("vector", lambda e: e.tensor_tensor(F2[:], SL(6)[:], SL(7)[:], ALU.mult),
                     r=["pspos", "mps"], w=["F2t"])
                P.op("vector", lambda e: e.tensor_tensor(F1[:], F1[:], F2[:], ALU.add),
                     r=["F1t", "F2t"], w=["F1t"])
                P.op("vector", lambda e: e.tensor_scalar(F1[:], F1[:], 1.0 / 32.0, None, ALU.mult),
                     r=["F1t"], w=["F1t"])
                if hs == 0 and ch == 0 and c.tapf is not None:
                    P.op("sync", lambda e: e.dma_start(out=c.tapf[:, 0:1024], in_=F1[:]),
                         r=["F1t"], w=["tp"], dma=True)
                    P.op("vector", lambda e: e.tensor_copy(F2[:], SL(1)[:]), r=[("bcast", 1)], w=["F2t"])
                    P.op("sync", lambda e: e.dma_start(out=c.tapf[:, 1024:2048], in_=F2[:]),
                         r=["F2t"], w=["tp"], dma=True)
                    P.op("vector", lambda e: e.tensor_copy(F2[:], SL(5)[:]), r=["pcpos"], w=["F2t"])
                    P.op("sync", lambda e: e.dma_start(out=c.tapf[:, 2048:3072], in_=F2[:]),
                         r=["F2t"], w=["tp"], dma=True)
                    P.op("vector", lambda e: e.tensor_copy(F2[:], SL(3)[:]), r=["mpc"], w=["F2t"])
                    P.op("sync", lambda e: e.dma_start(out=c.tapf[:, 3072:4096], in_=F2[:]),
                         r=["F2t", "tp"], w=["tp"], dma=True)
                P.op("vector", lambda e, ch=ch: e.tensor_tensor(
                    F1[:], F1[:], kvr[:, ch * S2:(ch + 1) * S2], ALU.subtract),
                    r=["F1t", ("kvr", ch)], w=["F1t"])
                P.op("vector", lambda e: e.tensor_tensor(F1[:], F1[:], SL(1)[:], ALU.mult),
                     r=["F1t", ("bcast", 1)], w=["F1t"])
                P.op("vector", lambda e, ch=ch: e.tensor_tensor(
                    kvr[:, ch * S2:(ch + 1) * S2], F1[:], kvr[:, ch * S2:(ch + 1) * S2], ALU.add),
                    r=["F1t", ("kvr", ch)], w=[("kvr", ch)])

            if hs == 0 and c.tapb is not None:
                P.op("sync", lambda e: e.dma_start(out=c.tapb[:, 8192:16384], in_=kvr[:, :]),
                     r=[("kvr", c2) for c2 in range(KC)], w=["tap2"], dma=True)
            if a_parts < 5:
                return
            c.P.barrier()
            for p in range(NP2):
                t0 = p * W
                for ch in range(KC):
                    P.op("scalar", lambda e, ch=ch, t0=t0: e.activation(
                        SL(5)[:, t0:t0 + W], kvr[:, ch * S2 + t0:ch * S2 + t0 + W], AF.Square),
                        r=[("kvr", ch)], w=["sqb"])
                    P.op("vector", lambda e, t0=t0: e.tensor_copy(
                        SL(6)[:, t0:t0 + W], SL(5)[:, t0:t0 + W]),
                        r=["sqb"], w=["sqc"])
                    P.op("tensor", lambda e, ch=ch, t0=t0: e.matmul(
                        c.ps_r1[:], c.ones_c_bf[:], kvr[:, ch * S2 + t0:ch * S2 + t0 + W],
                        start=(ch == 0), stop=(ch == KC - 1)),
                        r=[("kvr", ch), "ones_c"], w=["ps_r1"])
                    P.op("tensor", lambda e, ch=ch, t0=t0: e.matmul(
                        c.ps_r2[:], c.ones_c_bf[:], SL(6)[:, t0:t0 + W],
                        start=(ch == 0), stop=(ch == KC - 1)),
                        r=["sqc", "ones_c"], w=["ps_r2"])
                R = c.rows
                P.op("vector", lambda e: e.tensor_scalar(R[:, 0:W], c.ps_r1[:], 1.0 / D, None, ALU.mult),
                     r=["ps_r1"], w=["row_mu"])
                P.op("vector", lambda e: e.tensor_scalar(R[:, W:2 * W], c.ps_r2[:], 1.0 / D, None, ALU.mult),
                     r=["ps_r2"], w=["row_msq"])
                P.op("vector", lambda e: e.tensor_tensor(R[:, 2 * W:3 * W], R[:, 0:W], R[:, 0:W], ALU.mult),
                     r=["row_mu"], w=["row_v"])
                P.op("vector", lambda e: e.tensor_tensor(R[:, W:2 * W], R[:, W:2 * W], R[:, 2 * W:3 * W], ALU.subtract),
                     r=["row_msq", "row_v"], w=["row_msq"])
                P.op("vector", lambda e: e.tensor_scalar(R[:, W:2 * W], R[:, W:2 * W], EPS, None, ALU.add),
                     r=["row_msq"], w=["row_msq"])
                P.op("scalar", lambda e: e.activation(R[:, 2 * W:3 * W], R[:, W:2 * W], AF.Sqrt),
                     r=["row_msq"], w=["row_v"])
                P.op("vector", lambda e: e.reciprocal(R[:, 3 * W:4 * W], R[:, 2 * W:3 * W]),
                     r=["row_v"], w=["row_rstd"])
                P.op("tensor", lambda e: e.matmul(c.ps[3][:], c.ones_r_f[:], R[0:1, 0:W], start=True, stop=True),
                     r=["row_mu", "ones_rf"], w=["ps3"])
                P.op("tensor", lambda e: e.matmul(c.ps[4][:], c.ones_r_f[:], R[0:1, 3 * W:4 * W], start=True, stop=True),
                     r=["row_rstd", "ones_rf"], w=["ps4"])
                for ch in range(KC):
                    P.op("vector", lambda e, ch=ch, t0=t0: e.tensor_tensor(
                        F0[:, t0:t0 + W], kvr[:, ch * S2 + t0:ch * S2 + t0 + W], c.ps[3][:], ALU.subtract),
                        r=[("kvr", ch), "ps3"], w=["F0t"])
                    P.op("vector", lambda e, t0=t0: e.tensor_tensor(
                        F0[:, t0:t0 + W], F0[:, t0:t0 + W], c.ps[4][:], ALU.mult),
                        r=["F0t", "ps4"], w=["F0t"])
                    P.op("vector", lambda e, ch=ch, t0=t0: e.tensor_scalar(
                        F0[:, t0:t0 + W], F0[:, t0:t0 + W], bias[:, 44 + ch:45 + ch],
                        None, ALU.mult),
                        r=["F0t", ("Abias", 44)], w=["F0t"])
                    P.op("vector", lambda e, ch=ch, t0=t0: e.tensor_scalar(
                        kvr[:, ch * S2 + t0:ch * S2 + t0 + W], F0[:, t0:t0 + W],
                        bias[:, 52 + ch:53 + ch], None, ALU.add),
                        r=["F0t", ("Abias", 52)], w=[("kvr", ch)])

            if hs == 0 and c.tapb is not None:
                P.op("sync", lambda e: e.dma_start(out=c.tapb[:, 16384:24576], in_=kvr[:, :]),
                     r=[("kvr", c2) for c2 in range(KC)], w=["tap3"], dma=True)
            if a_parts < 6:
                return
            c.P.barrier()
            with pan_scope(c) as pan:
                load_weight_bf(c, wstage, c.I["a_Wo"], wsb, "A_wo", 1024)
                if hs == 0 and c.tapf is not None:
                    P.op("sync", lambda e: e.dma_start(
                        out=c.tapf[:, 1536:1792], in_=wsb[:, 0:512].bitcast(F32)),
                        r=["A_wo"], w=["tap5"], dma=True)
                for p in range(NP2):
                    t0 = p * W
                    col0 = base + t0
                    for oc in range(KC):
                        for k in range(KC):
                            P.op("tensor", lambda e, oc=oc, k=k, t0=t0: e.matmul(
                                c.ps[0][:], wsb[:, k * 1024 + oc * 128:k * 1024 + oc * 128 + 128],
                                kvr[:, k * S2 + t0:k * S2 + t0 + W],
                                start=(k == 0), stop=(k == KC - 1)),
                                r=["A_wo", ("kvr", k)], w=["ps0"])
                        P.op("sync", lambda e, oc=oc, col0=col0: e.dma_start(
                            out=wstage[:, 0:W],
                            in_=c.ln1d[oc * 128:(oc + 1) * 128, col0:col0 + W]),
                            r=[("lnst", oc)], w=["wstage"], dma=True)
                        P.op("vector", lambda e, oc=oc: e.tensor_scalar(
                            wstage[:, W:2 * W], c.ps[0][:], bias[:, 32 + oc:33 + oc], None, ALU.add),
                            r=["ps0", ("Abias", 32)], w=["a1t"])
                        P.op("vector", lambda e, oc=oc: e.tensor_tensor(
                            pan.x[:, oc * W:(oc + 1) * W], wstage[:, W:2 * W], wstage[:, 0:W], ALU.add),
                            r=["a1t", "wstage"], w=[("panx", oc)])
                    emit_ln_core(c, pan)
                    for ch in range(KC):
                        P.op("sync", lambda e, ch=ch, col0=col0: e.dma_start(
                            out=pan.x[:, ch * W:(ch + 1) * W],
                            in_=c.x_fm[ch * 128:(ch + 1) * 128, col0:col0 + W]),
                            w=[("panx", ch)], dma=True)
                        P.op("vector", lambda e, ch=ch: e.tensor_tensor(
                            pan.l[:, ch * W:(ch + 1) * W], pan.l[:, ch * W:(ch + 1) * W],
                            pan.x[:, ch * W:(ch + 1) * W], ALU.add),
                            r=[("panx", ch), ("panl", ch)], w=[("panl", ch)])
                        P.op("sync", lambda e, ch=ch, col0=col0: e.dma_start(
                            out=c.x1d[ch * 128:(ch + 1) * 128, col0:col0 + W],
                            in_=pan.l[:, ch * W:(ch + 1) * W]),
                            r=[("panl", ch)], w=[("x1st", ch)], dma=True)
            c.P.barrier()


        for hs in range(4):
            _half(hs)


def emit_phaseB(c):
    nc, P = c.nc, c.P
    import contextlib
    NPQ = 2
    def _tq(tq):
        seq, half = tq // 2, tq % 2
        with contextlib.ExitStack() as st:
            ent = st.enter_context
            lnq = ent(_sb(nc, "B_lnq", [128, KC * 1024], BF16))
            routed = ent(_sb(nc, "B_routed", [128, KC * 1024], F32))
            gwsb = ent(_sb(nc, "B_gw", [128, KC * 8], F32))
            b2sb = ent(_sb(nc, "B_b2", [8, D], F32))
            gbb = ent(_sb(nc, "B_gbb", [128, 8], F32))
            bias = ent(_sb(nc, "B_bias", [128, 72], F32))
            lgsb = ent(_sb(nc, "B_lg", [128, 48], F32))
            cwrow = ent(_sb(nc, "B_cwrow", [1, 1024], F32))
            cw8 = ent(_sb(nc, "B_cw8", [8, 128], F32))

            P.op("sync", lambda e: e.dma_start(out=gwsb[:, 0:64], in_=c.I["f_gwT"]),
                 w=["gw"], dma=True)
            P.op("sync", lambda e: e.dma_start(out=b2sb[:], in_=c.I["f_b2"]), w=["b2sb"], dma=True)
            P.op("sync", lambda e: e.dma_start(out=bias[:, 32:72], in_=c.I["f_sbias"]),
                 w=[("Bbias", 32), ("Bbias", 48), ("Bbias", 64)], dma=True)
            gbrow = ent(_sb(nc, "B_gbrow", [1, 8], F32))
            P.op("sync", lambda e: e.dma_start(out=gbrow[:], in_=c.I["f_gb"]),
                 w=["gbrow"], dma=True)
            P.op("tensor", lambda e: e.matmul(c.ps[0][:, 0:8], c.ones_r_f[:], gbrow[0:1, 0:8],
                                              start=True, stop=True),
                 r=["gbrow", "ones_rf"], w=["ps0"])
            P.op("vector", lambda e: e.tensor_copy(gbb[:], c.ps[0][:, 0:8]), r=["ps0"], w=["gbb"])

            def router_cb(pcol, pan):
                for sub in range(4):
                    for k in range(KC):
                        P.op("tensor", lambda e, sub=sub, k=k: e.matmul(
                            c.ps[0][:, 0:8], pan.l[:, k * W + sub * 128:k * W + sub * 128 + 128],
                            gwsb[:, k * 8:(k + 1) * 8], start=(k == 0), stop=(k == KC - 1)),
                            r=[("panl", k), "gw"], w=["ps0"])
                    sub_c = lambda i: lgsb[:, i * 8:(i + 1) * 8]
                    P.op("vector", lambda e: e.tensor_copy(sub_c(0), c.ps[0][:, 0:8]),
                         r=["ps0"], w=["lg0"])
                    P.op("vector", lambda e: e.reduce_max(lgsb[:, 40:41], sub_c(0), axis=AX.X),
                         r=["lg0"], w=["lgmx"])
                    P.op("vector", lambda e: e.tensor_scalar(
                        sub_c(1), sub_c(0), lgsb[:, 40:41], None, ALU.subtract),
                        r=["lg0", "lgmx"], w=["lg1"])
                    P.op("scalar", lambda e: e.activation(sub_c(1), sub_c(1), AF.Exp),
                         r=["lg1"], w=["lg1"])
                    P.op("vector", lambda e: e.reduce_sum(lgsb[:, 41:42], sub_c(1), axis=AX.X),
                         r=["lg1"], w=["lgsum"])
                    P.op("vector", lambda e: e.reciprocal(lgsb[:, 42:43], lgsb[:, 41:42]),
                         r=["lgsum"], w=["lginv"])
                    P.op("vector", lambda e: e.tensor_scalar(
                        sub_c(1), sub_c(1), lgsb[:, 42:43], None, ALU.mult),
                        r=["lg1", "lginv"], w=["lg1"])
                    P.op("vector", lambda e: e.tensor_tensor(sub_c(2), sub_c(1), gbb[:], ALU.add),
                         r=["lg1", "gbb"], w=["lg2"])
                    P.op("vector", lambda e: e.max(out=lgsb[:, 24:32], in_=sub_c(2)),
                         r=["lg2"], w=["lgm8"])
                    P.op("vector", lambda e: e.tensor_scalar(
                        sub_c(2), sub_c(2), lgsb[:, 25:26], None, ALU.is_ge),
                        r=["lg2", "lgm8"], w=["lg2"])
                    P.op("vector", lambda e: e.tensor_tensor(sub_c(2), sub_c(2), sub_c(1), ALU.mult),
                         r=["lg2", "lg1"], w=["lg2"])
                    P.op("tensor", lambda e: e.transpose(
                        c.ps[1][0:8, 0:128], sub_c(2), c.iden_f[:]),
                        r=["lg2", "iden_f"], w=["ps1"])
                    P.op("vector", lambda e: e.tensor_copy(cw8[0:8, :], c.ps[1][0:8, 0:128]),
                         r=["ps1"], w=["cw8"])
                    col = pcol + sub * 128
                    P.op("sync", lambda e, col=col: e.dma_start(
                        out=c.cwdT[:, col:col + 128], in_=cw8[0:8, :]),
                        r=["cw8"], w=["cwst"], dma=True)

            with pan_scope(c) as panB:
                for p in range(NPQ):
                    emit_ln_panel(c, panB, c.x1d, tq * 1024 + p * W, lnfm=None,
                                  cb=router_cb, cb_arg=tq * 1024 + p * W)
                    for ch in range(KC):
                        P.op("vector", lambda e, ch=ch, p=p, pl=panB.l: e.tensor_copy(
                            lnq[:, ch * 1024 + p * W: ch * 1024 + p * W + W],
                            pl[:, ch * W:(ch + 1) * W]),
                            r=[("panl", ch)], w=[("lnq", ch, p)])

            xst = contextlib.ExitStack()
            w1sb = xst.enter_context(_sb(nc, "B_w1", [128, KC * 2048], BF16))
            w3sb = xst.enter_context(_sb(nc, "B_w3", [128, KC * 2048], BF16))
            w2sb = xst.enter_context(_sb(nc, "B_w2", [128, KC * 1024], BF16))
            sa = xst.enter_context(_sb(nc, "B_sa", [128, 8 * W], BF16))
            hsl = xst.enter_context(_sb(nc, "B_h", [128, 6 * W], F32))
            for ex in range(9):
                shared = ex == 8
                if shared:
                    w1src, w3src, w2src = c.sw1b, c.sw3b, c.sw2b
                    b1c, b3c = 32, 48
                else:
                    w1src, w3src, w2src = c.w1b[ex], c.w3b[ex], c.w2b[ex]
                    b1c, b3c = 0, 16
                    P.op("sync", lambda e, ex=ex: e.dma_start(
                        out=bias[:, 0:32], in_=c.I["f_b13p"][ex]),
                        w=[("Bbias", 0), ("Bbias", 16)], dma=True)
                for k in range(KC):
                    P.op("sync", lambda e, s=w1src, k=k: e.dma_start(
                        out=w1sb[:, k * 2048:(k + 1) * 2048], in_=s[k * 128:(k + 1) * 128, :]),
                        w=[("w1sb", k)], dma=True)
                    P.op("sync", lambda e, s=w3src, k=k: e.dma_start(
                        out=w3sb[:, k * 2048:(k + 1) * 2048], in_=s[k * 128:(k + 1) * 128, :]),
                        w=[("w3sb", k)], dma=True)
                    P.op("sync", lambda e, s=w2src, k=k: e.dma_start(
                        out=w2sb[:, k * 1024:(k + 1) * 1024], in_=s[k * 128:(k + 1) * 128, :]),
                        w=[("w2sb", k)], dma=True)
                if not shared:
                    P.op("sync", lambda e, ex=ex: e.dma_start(
                        out=cwrow[0:1, :], in_=c.cwdT[ex:ex + 1, tq * 1024:(tq + 1) * 1024]),
                        r=["cwst"], w=["cwrow"], dma=True)
                for p in range(NPQ):
                    t0 = p * W
                    for fg in range(8):
                        specs = [(w1sb, fg, 0, ("w1sb",), c.ps[0], "ps0"),
                                 (w1sb, fg + 8, 1, ("w1sb",), c.ps[1], "ps1"),
                                 (w3sb, fg, 2, ("w3sb",), c.ps[2], "ps2"),
                                 (w3sb, fg + 8, 3, ("w3sb",), c.ps[3], "ps3")]
                        for wsb_, fc, hi, wkey, psx, pskey in specs:
                            for k in range(KC):
                                P.op("tensor", lambda e, wsb_=wsb_, fc=fc, k=k, t0=t0, psx=psx: e.matmul(
                                    psx[:], wsb_[:, k * 2048 + fc * 128:k * 2048 + fc * 128 + 128],
                                    lnq[:, k * 1024 + t0:k * 1024 + t0 + W],
                                    start=(k == 0), stop=(k == KC - 1)),
                                    r=[(wkey[0], k), ("lnq", k, p)], w=[pskey])
                        hx = lambda i: hsl[:, i * W:(i + 1) * W]
                        P.op("vector", lambda e, fg=fg, b1c=b1c: e.tensor_scalar(
                            hx(0), c.ps[0][:], bias[:, b1c + fg:b1c + fg + 1], None, ALU.add),
                            r=["ps0", ("Bbias", b1c)], w=["h0"])
                        P.op("vector", lambda e, fg=fg, b1c=b1c: e.tensor_scalar(
                            hx(1), c.ps[1][:], bias[:, b1c + 8 + fg:b1c + 9 + fg], None, ALU.add),
                            r=["ps1", ("Bbias", b1c)], w=["h1"])
                        P.op("vector", lambda e, fg=fg, b3c=b3c: e.tensor_scalar(
                            hx(2), c.ps[2][:], bias[:, b3c + fg:b3c + fg + 1], None, ALU.add),
                            r=["ps2", ("Bbias", b3c)], w=["h2"])
                        P.op("vector", lambda e, fg=fg, b3c=b3c: e.tensor_scalar(
                            hx(3), c.ps[3][:], bias[:, b3c + 8 + fg:b3c + 9 + fg], None, ALU.add),
                            r=["ps3", ("Bbias", b3c)], w=["h3"])
                        P.op("vector", lambda e: e.tensor_tensor(hx(0), hx(0), hx(2), ALU.mult),
                             r=["h0", "h2"], w=["h0"])
                        P.op("vector", lambda e: e.tensor_tensor(hx(1), hx(1), hx(3), ALU.mult),
                             r=["h1", "h3"], w=["h1"])
                        P.op("vector", lambda e: e.tensor_scalar(hx(0), hx(0), 7.0, None, ALU.min),
                             r=["h0"], w=["h0"])
                        P.op("scalar", lambda e: e.activation(hx(4), hx(0), AF.Sigmoid, scale=1.702),
                             r=["h0"], w=["h4"])
                        P.op("vector", lambda e: e.tensor_tensor(hx(0), hx(0), hx(4), ALU.mult),
                             r=["h0", "h4"], w=["h0"])
                        P.op("vector", lambda e: e.tensor_scalar(
                            hx(1), hx(1), 7.0, -7.0, ALU.min, ALU.max),
                            r=["h1"], w=["h1"])
                        P.op("vector", lambda e: e.tensor_scalar(hx(1), hx(1), 1.0, None, ALU.add),
                             r=["h1"], w=["h1"])
                        P.op("vector", lambda e: e.tensor_tensor(hx(0), hx(0), hx(1), ALU.mult),
                             r=["h0", "h1"], w=["h0"])
                        if shared:
                            P.op("vector", lambda e, fg=fg: e.tensor_copy(
                                sa[:, fg * W:(fg + 1) * W], hx(0)), r=["h0"], w=[("sa", fg)])
                        else:
                            P.op("tensor", lambda e, t0=t0: e.matmul(
                                c.ps[4][:], c.ones_r_f[:], cwrow[0:1, t0:t0 + W],
                                start=True, stop=True),
                                r=["cwrow", "ones_rf"], w=["ps4"])
                            P.op("vector", lambda e, fg=fg: e.tensor_tensor(
                                sa[:, fg * W:(fg + 1) * W], hx(0), c.ps[4][:], ALU.mult),
                                r=["h0", "ps4"], w=[("sa", fg)])
                    for dc in range(KC):
                        for fg in range(8):
                            P.op("tensor", lambda e, dc=dc, fg=fg: e.matmul(
                                c.ps[0][:], w2sb[:, fg * 1024 + dc * 128:fg * 1024 + dc * 128 + 128],
                                sa[:, fg * W:(fg + 1) * W],
                                start=(fg == 0), stop=(fg == 7)),
                                r=[("w2sb", fg), ("sa", fg)], w=["ps0"])
                        if ex == 0:
                            P.op("vector", lambda e, dc=dc, t0=t0: e.tensor_copy(
                                routed[:, dc * 1024 + t0:dc * 1024 + t0 + W], c.ps[0][:]),
                                r=["ps0"], w=[("routed", dc, p)])
                        else:
                            P.op("vector", lambda e, dc=dc, t0=t0: e.tensor_tensor(
                                routed[:, dc * 1024 + t0:dc * 1024 + t0 + W],
                                routed[:, dc * 1024 + t0:dc * 1024 + t0 + W], c.ps[0][:], ALU.add),
                                r=["ps0", ("routed", dc, p)], w=[("routed", dc, p)])

            xst.close()
            P.barrier()
            with pan_scope(c) as panC:
                emit_B_closing(c, panC, tq, routed, b2sb, cw8, bias)
            P.barrier()



    for tq in range(4):
        _tq(tq)

def emit_B_closing(c, pan, tq, routed, b2sb, cw8, bias):
    nc, P = c.nc, c.P
    seq, half = tq // 2, tq % 2
    NPQ = 2
    with _sb(nc, "B_cwq", [8, W], F32) as cwq:
        if True:
            for p in range(NPQ):
                t0 = p * W
                col0 = seq * S + half * 1024 + t0
                for dc in range(KC):
                    P.op("sync", lambda e, p=p, t0=t0: e.dma_start(
                        out=cwq[0:8, 0:W], in_=c.cwdT[0:8, tq * 1024 + t0: tq * 1024 + t0 + W]),
                        r=["cwst"], w=["cwq"], dma=True) if dc == 0 else None
                    P.op("tensor", lambda e, dc=dc, p=p: e.matmul(
                        c.ps[1][:], b2sb[0:8, dc * 128:(dc + 1) * 128], cwq[0:8, 0:W],
                        start=True, stop=True),
                        r=["b2sb", "cwq"], w=["ps1"])
                    P.op("vector", lambda e, dc=dc, t0=t0: e.scalar_tensor_tensor(
                        pan.x[:, dc * W:(dc + 1) * W],
                        routed[:, dc * 1024 + t0:dc * 1024 + t0 + W],
                        bias[:, 64 + dc:65 + dc], c.ps[1][:], ALU.add, ALU.add),
                        r=[("routed", dc, p), ("Bbias", 64), "ps1"], w=[("panx", dc)])
                emit_ln_core(c, pan)
                for ch in range(KC):
                    P.op("sync", lambda e, ch=ch, col0=col0: e.dma_start(
                        out=pan.x[:, ch * W:(ch + 1) * W],
                        in_=c.x1d[ch * 128:(ch + 1) * 128, col0:col0 + W]),
                        w=[("panx", ch)], dma=True)
                    P.op("vector", lambda e, ch=ch: e.tensor_tensor(
                        pan.l[:, ch * W:(ch + 1) * W], pan.l[:, ch * W:(ch + 1) * W],
                        pan.x[:, ch * W:(ch + 1) * W], ALU.add),
                        r=[("panx", ch), ("panl", ch)], w=[("panl", ch)])
                    P.op("sync", lambda e, ch=ch, col0=col0: e.dma_start(
                        out=c.x2d[ch * 128:(ch + 1) * 128, col0:col0 + W],
                        in_=pan.l[:, ch * W:(ch + 1) * W]),
                        r=[("panl", ch)], w=[("x2st", ch)], dma=True)



def emit_phaseC(c):
    nc, P = c.nc, c.P
    import contextlib
    def _seq(seq):
        with contextlib.ExitStack() as st:
            ent = st.enter_context
            lnfm = ent(_sb(nc, "C_lnfm", [128, KC * (S + 1)], BF16))
            qfm = ent(_sb(nc, "C_q", [128, KC * S], BF16))
            kfm = ent(_sb(nc, "C_k", [128, KC * S], BF16))
            vaug = ent(_sb(nc, "C_vaug", [128, 16 * 1040], BF16))
            bias = ent(_sb(nc, "C_bias", [128, 24], F32))
            sqb = ent(_sb(nc, "C_sqb", [128, W], BF16))
            maxn = ent(_sb(nc, "C_maxn", [1, 64], F32))
            cb_t = ent(_sb(nc, "C_cb", [128, 16], F32))
            pb = ent(_sb(nc, "C_pb", [128, 2 * W], BF16))
            ctxs = ent(_sb(nc, "C_ctx", [128, 2 * W], F32))
            inv_r = ent(_sb(nc, "C_inv", [1, 2 * W], F32))

            P.op("sync", lambda e: e.dma_start(out=bias[:, 0:24], in_=c.I["c_bias"]),
                 w=[("Cbias", 0), ("Cbias", 8), ("Cbias", 16)], dma=True)
            for ch in range(KC):
                P.op("vector", lambda e, ch=ch: e.memset(lnfm[:, ch * (S + 1):ch * (S + 1) + 1], 0.0),
                     w=[("lnfm", ch, "z")])
            with pan_scope(c) as panL:
                for p in range(NP):
                    emit_ln_panel(c, panL, c.x2d, seq * S + p * W, lnfm=lnfm,
                                  lnfm_stride=S + 1, lnfm_off=1 + p * W)

            def lsl(k, t0, w_):
                return lnfm[:, k * (S + 1) + 1 + t0: k * (S + 1) + 1 + t0 + w_]

            for kt in range(16):
                P.op("vector", lambda e, kt=kt: e.memset(
                    vaug[:, kt * 1040 + 64: kt * 1040 + 16 * 65: 65], 1.0),
                    w=[("vaug_ones", kt)])

            xstC = contextlib.ExitStack()
            wsb = xstC.enter_context(_sb(nc, "C_wsb", [128, KC * 1024], BF16))
            wstage = xstC.enter_context(_sb(nc, "C_wstage", [128, 1024], F32))
            for wi, (wnm, bcol, dest) in enumerate((("c_Wq", 0, "q"), ("c_Wk", 8, "k"), ("c_Wv", 16, "v"))):
                load_weight_bf(c, wstage, c.I[wnm], wsb, f"C_w{wi}", 1024)
                for oc in range(KC):
                    for p in range(NP):
                        t0 = p * W
                        for k in range(KC):
                            P.op("tensor", lambda e, oc=oc, k=k, t0=t0, wi=wi: e.matmul(
                                c.ps[0][:], wsb[:, k * 1024 + oc * 128:k * 1024 + oc * 128 + 128],
                                lsl(k, t0, W), start=(k == 0), stop=(k == KC - 1)),
                                r=[f"C_w{wi}", ("lnfm", k, t0 // W)], w=["ps0"])
                        if dest == "q":
                            P.op("vector", lambda e, oc=oc, t0=t0, bcol=bcol: e.tensor_scalar(
                                qfm[:, oc * S + t0: oc * S + t0 + W], c.ps[0][:],
                                bias[:, bcol + oc:bcol + oc + 1], None, ALU.add),
                                r=["ps0", ("Cbias", bcol)], w=[("qfm", oc)])
                        elif dest == "k":
                            P.op("vector", lambda e, oc=oc, t0=t0, bcol=bcol: e.tensor_scalar(
                                kfm[:, oc * S + t0: oc * S + t0 + W], c.ps[0][:],
                                bias[:, bcol + oc:bcol + oc + 1], None, ALU.add),
                                r=["ps0", ("Cbias", bcol)], w=[("kfm", oc)])
                        else:
                            P.op("vector", lambda e, oc=oc, t0=t0, bcol=bcol: e.tensor_scalar(
                                pb[:, 0:W], c.ps[0][:], bias[:, bcol + oc:bcol + oc + 1],
                                None, ALU.add),
                                r=["ps0", ("Cbias", bcol)], w=["pb0"])
                            for sub in range(W // 128):
                                kt = (t0 + sub * 128) // 128
                                P.op("tensor", lambda e, sub=sub: e.transpose(
                                    c.ps_tr[:, 0:128], pb[:, sub * 128:(sub + 1) * 128], c.iden_bf[:]),
                                    r=["pb0", "iden_b"], w=["ps_tr"])
                                for hh in range(2):
                                    h = 2 * oc + hh
                                    P.op("vector", lambda e, kt=kt, h=h, hh=hh: e.tensor_copy(
                                        vaug[:, kt * 1040 + h * 65: kt * 1040 + h * 65 + 64],
                                        c.ps_tr[:, hh * 64:hh * 64 + 64]),
                                        r=["ps_tr"], w=[("vaug", kt, h)])

            xstC.close()
            P.barrier()
            if seq == 0 and c.tapf is not None:
                P.op("sync", lambda e: e.dma_start(out=c.tapf[:, 0:1024],
                     in_=qfm[:, 0:2048].bitcast(F32)), r=[("qfm", 0)], w=["tpc"], dma=True)
                P.op("sync", lambda e: e.dma_start(out=c.tapf[:, 1024:2048],
                     in_=kfm[:, 0:2048].bitcast(F32)), r=[("kfm", 0)], w=["tpc"], dma=True)
                P.op("sync", lambda e: e.dma_start(out=c.tapf[:, 2048:2568],
                     in_=vaug[:, 0:1040].bitcast(F32)),
                     r=[("vaug", 0, 0), ("vaug", 0, 1), ("vaug_ones", 0)], w=["tpc"], dma=True)
            for which, fmref, fmkey, mcol in (("q", qfm, "qfm", 0), ("k", kfm, "kfm", 16)):
                for oc in range(KC):
                    for p in range(NP):
                        t0 = p * W
                        P.op("scalar", lambda e, oc=oc, t0=t0, fmref=fmref: e.activation(
                            sqb[:, 0:W], fmref[:, oc * S + t0:oc * S + t0 + W], AF.Square),
                            r=[(fmkey, oc)], w=["sqb"])
                        for hh in range(2):
                            P.op("tensor", lambda e, hh=hh: e.matmul(
                                c.ps_r1[:], c.ones_c_bf[hh * 64:hh * 64 + 64, 0:1],
                                sqb[hh * 64:hh * 64 + 64, 0:W], start=True, stop=True),
                                r=["sqb", "ones_c"], w=["ps_r1"])
                            P.op("vector", lambda e: e.reduce_max(
                                maxn[0:1, 32:33], c.ps_r1[:], axis=AX.X),
                                r=["ps_r1"], w=["mxs"])
                            h = 2 * oc + hh
                            if p == 0:
                                P.op("vector", lambda e, h=h, mcol=mcol: e.tensor_copy(
                                    maxn[0:1, mcol + h:mcol + h + 1], maxn[0:1, 32:33]),
                                    r=["mxs"], w=[("maxn", mcol + h)])
                            else:
                                P.op("vector", lambda e, h=h, mcol=mcol: e.tensor_tensor(
                                    maxn[0:1, mcol + h:mcol + h + 1],
                                    maxn[0:1, mcol + h:mcol + h + 1], maxn[0:1, 32:33], ALU.max),
                                    r=["mxs", ("maxn", mcol + h)], w=[("maxn", mcol + h)])
            for h in range(16):
                P.op("vector", lambda e, h=h: e.tensor_tensor(
                    maxn[0:1, 32 + h:33 + h], maxn[0:1, h:h + 1], maxn[0:1, 16 + h:17 + h], ALU.mult),
                    r=[("maxn", h), ("maxn", 16 + h)], w=[("csc", h)])
                P.op("scalar", lambda e, h=h: e.activation(
                    maxn[0:1, 32 + h:33 + h], maxn[0:1, 32 + h:33 + h], AF.Sqrt),
                    r=[("csc", h)], w=[("csc", h)])
                P.op("vector", lambda e, h=h: e.tensor_scalar(
                    maxn[0:1, 32 + h:33 + h], maxn[0:1, 32 + h:33 + h], -0.125, None, ALU.mult),
                    r=[("csc", h)], w=[("csc", h)])
                P.op("tensor", lambda e, h=h: e.matmul(
                    c.ps[0][:, 0:1], c.ones_r_f[:], maxn[0:1, 32 + h:33 + h], start=True, stop=True),
                    r=[("csc", h), "ones_rf"], w=["ps0"])
                P.op("vector", lambda e, h=h: e.tensor_copy(cb_t[:, h:h + 1], c.ps[0][:, 0:1]),
                     r=["ps0"], w=[("cb", h)])

            if seq == 0 and c.tapf is not None:
                P.op("sync", lambda e: e.dma_start(out=c.tapf[:, 2568:2584], in_=cb_t[:, 0:16]),
                     r=[("cb", h3) for h3 in range(16)], w=["tpc"], dma=True)
                P.op("sync", lambda e: e.dma_start(out=c.tapf[0:1, 2584:2648], in_=maxn[0:1, 0:64]),
                     r=[("maxn", m3) for m3 in range(32)] + [("csc", h3) for h3 in range(16)], w=["tpc"], dma=True)
            for hp in range(KC):
                h0, h1 = 2 * hp, 2 * hp + 1
                for qp in range(NP):
                    q0 = qp * W
                    for kc_i in range(16):
                        kt0 = kc_i * 128
                        P.op("tensor", lambda e, hp=hp, kt0=kt0, q0=q0: e.matmul(
                            c.ps[0][:], kfm[0:64, hp * S + kt0: hp * S + kt0 + 128],
                            qfm[0:64, hp * S + q0: hp * S + q0 + W],
                            start=True, stop=True),
                            r=[("kfm", hp), ("qfm", hp)], w=["ps0"])
                        P.op("tensor", lambda e, hp=hp, kt0=kt0, q0=q0: e.matmul(
                            c.ps[1][:], kfm[64:128, hp * S + kt0: hp * S + kt0 + 128],
                            qfm[64:128, hp * S + q0: hp * S + q0 + W],
                            start=True, stop=True, tile_position=(64, 0)),
                            r=[("kfm", hp), ("qfm", hp)], w=["ps1"])
                        P.op("scalar", lambda e, h0=h0: e.activation(
                            pb[:, 0:W], c.ps[0][:], AF.Exp, bias=cb_t[:, h0:h0 + 1], scale=0.125),
                            r=["ps0", ("cb", h0)], w=["pb0"])
                        P.op("scalar", lambda e, h1=h1: e.activation(
                            pb[:, W:2 * W], c.ps[1][:], AF.Exp, bias=cb_t[:, h1:h1 + 1], scale=0.125),
                            r=["ps1", ("cb", h1)], w=["pb1"])
                        if seq == 0 and hp == 0 and qp == 0 and c.probd is not None:
                            P.op("sync", lambda e, kc_i=kc_i: e.dma_start(
                                out=c.probd[:, kc_i * 512:(kc_i + 1) * 512], in_=pb[:, 0:W]),
                                r=["pb0"], w=["tpp"], dma=True)
                        P.op("tensor", lambda e, kc_i=kc_i, h0=h0: e.matmul(
                            c.ps[2][0:65, :], vaug[:, kc_i * 1040 + h0 * 65: kc_i * 1040 + h0 * 65 + 65],
                            pb[:, 0:W], start=(kc_i == 0), stop=(kc_i == 15)),
                            r=["pb0", ("vaug", kc_i, h0), ("vaug_ones", kc_i), "tpp"], w=["ps2"])
                        P.op("tensor", lambda e, kc_i=kc_i, h1=h1: e.matmul(
                            c.ps[3][0:65, :], vaug[:, kc_i * 1040 + h1 * 65: kc_i * 1040 + h1 * 65 + 65],
                            pb[:, W:2 * W], start=(kc_i == 0), stop=(kc_i == 15)),
                            r=["pb1", ("vaug", kc_i, h1), ("vaug_ones", kc_i)], w=["ps3"])
                    if seq == 0 and hp == 0 and qp == 0 and c.tapf is not None:
                        P.op("sync", lambda e: e.dma_start(out=c.tapf[:, 2648:2904],
                             in_=pb[:, 0:W].bitcast(F32)), r=["pb0"], w=["tpm"], dma=True)
                    for hh, psx, pskey in ((0, c.ps[2], "ps2"), (1, c.ps[3], "ps3")):
                        P.op("vector", lambda e, psx=psx, hh=hh: e.reciprocal(
                            inv_r[0:1, hh * W:(hh + 1) * W], psx[64:65, :]),
                            r=[pskey], w=[("inv", hh)])
                        P.op("tensor", lambda e, hh=hh: e.matmul(
                            c.ps[4][0:64, :], c.ones_r_f[0:1, 0:64],
                            inv_r[0:1, hh * W:(hh + 1) * W], start=True, stop=True),
                            r=[("inv", hh), "ones_rf"], w=["ps4"])
                        P.op("scalar", lambda e, psx=psx, hh=hh: e.copy(
                            ctxs[0:64, hh * W:(hh + 1) * W], psx[0:64, :]),
                            r=[pskey], w=[("ctxs", hh)])
                        if seq == 0 and hp == 0 and qp == 0 and hh == 0 and c.tapf is not None:
                            P.op("sync", lambda e: e.dma_start(out=c.tapf[0:64, 2904:3416],
                                 in_=ctxs[0:64, 0:W]), r=[("ctxs", 0)], w=["tpm"], dma=True)
                            P.op("sync", lambda e: e.dma_start(out=c.tapf[0:1, 3416:3928],
                                 in_=inv_r[0:1, 0:W]), r=[("inv", 0)], w=["tpm"], dma=True)
                        P.op("vector", lambda e, hh=hh: e.tensor_tensor(
                            ctxs[0:64, hh * W:(hh + 1) * W], ctxs[0:64, hh * W:(hh + 1) * W],
                            c.ps[4][0:64, :], ALU.mult),
                            r=[("ctxs", hh), "ps4"], w=[("ctxs", hh)])
                        h = 2 * hp + hh
                        P.op("sync", lambda e, h=h, hh=hh, qp=qp: e.dma_start(
                            out=c.ctxd[h * 64:(h + 1) * 64, seq * S + qp * W: seq * S + qp * W + W],
                            in_=ctxs[0:64, hh * W:(hh + 1) * W]),
                            r=[("ctxs", hh)], w=["ctxst"], dma=True)
            with pan_scope(c) as panZ:
                emit_closing(c, panZ, c.ctxd, c.x2d, c.x3d, seq, branch_key="ctxst")
            P.barrier()




    for seq in range(NSEQ):
        _seq(seq)

def emit_phaseD(c):
    nc, P = c.nc, c.P
    import contextlib
    def _seq(seq):
        with contextlib.ExitStack() as st:
            ent = st.enter_context
            lnfm = ent(_sb(nc, "D_lnfm", [128, KC * (S + 1)], BF16))
            nwsb = ent(_sb(nc, "D_nw", [128, KC * 45], F32))
            nbsb = ent(_sb(nc, "D_nb", [45, 1], F32))
            nsc = ent(_sb(nc, "D_nsc", [45, W], F32))
            ntl = ent(_sb(nc, "D_ntl", [128, 45], F32))
            bits = ent(_sb(nc, "D_bits", [128, 64], F32))
            selp = ent(_sb(nc, "D_sel", [128, 48], F32))
            selT = ent(_sb(nc, "D_selT", [48, S], F32))
            selTb = ent(_sb(nc, "D_selTb", [48, S], BF16))
            wstage = ent(_sb(nc, "D_wstage", [128, 2048], F32))
            hm = ent(_sb(nc, "D_hm", [128, KC * S], BF16))
            wsb = ent(_sb(nc, "D_wsb", [128, KC * 1024], BF16))
            b2p = ent(_sb(nc, "D_b2p", [48, D], F32))
            easb = ent(_sb(nc, "D_ea", [48, 3 * 1024], BF16))
            bias = ent(_sb(nc, "D_bias", [128, 8], F32))
            qtile = ent(_sb(nc, "D_qt", [128, W], F32))
            qbtile = ent(_sb(nc, "D_qbt", [128, W], BF16))

            P.op("sync", lambda e: e.dma_start(out=nwsb[:, 0:360], in_=c.I["d_nwT"]),
                 w=["nw"], dma=True)
            P.op("sync", lambda e: e.dma_start(out=nbsb[:], in_=c.I["d_nb"]), w=["nb"], dma=True)
            P.op("sync", lambda e: e.dma_start(out=easb[:], in_=c.I["E_all"]), w=["ea"], dma=True)
            for ch in range(KC):
                P.op("vector", lambda e, ch=ch: e.memset(lnfm[:, ch * (S + 1):ch * (S + 1) + 1], 0.0),
                     w=[("lnfm", ch, "z")])

            def nodes_cb(p, pan):
                t0 = p * W
                for k in range(KC):
                    P.op("tensor", lambda e, k=k: e.matmul(
                        c.ps[0][0:45, :], nwsb[:, k * 45:(k + 1) * 45],
                        pan.l[:, k * W:(k + 1) * W], start=(k == 0), stop=(k == KC - 1)),
                        r=[("panl", k), "nw"], w=["ps0"])
                P.op("vector", lambda e: e.tensor_scalar(
                    nsc[:], c.ps[0][0:45, :], nbsb[0:45, 0:1], None, ALU.add),
                    r=["ps0", "nb"], w=["nsc"])
                for sub in range(4):
                    P.op("tensor", lambda e, sub=sub: e.transpose(
                        c.ps[1][:, 0:45], nsc[0:45, sub * 128:(sub + 1) * 128], c.iden_f[0:45, 0:45]),
                        r=["nsc", "iden_f"], w=["ps1"])
                    P.op("vector", lambda e: e.tensor_scalar(
                        ntl[:], c.ps[1][:, 0:45], 0.0, None, ALU.is_ge),
                        r=["ps1"], w=["ntl"])
                    bx = lambda i: bits[:, i * 3:(i + 1) * 3]
                    gn = lambda n: ntl[:, n * 3:(n + 1) * 3]
                    P.op("vector", lambda e: e.tensor_copy(bx(0), gn(0)), r=["ntl"], w=["bits"])
                    P.op("vector", lambda e: e.tensor_tensor(bx(4), gn(2), gn(1), ALU.subtract),
                         r=["ntl"], w=["bits"])
                    P.op("vector", lambda e: e.tensor_tensor(bx(4), bx(4), bx(0), ALU.mult),
                         r=["bits"], w=["bits"])
                    P.op("vector", lambda e: e.tensor_tensor(bx(1), gn(1), bx(4), ALU.add),
                         r=["ntl", "bits"], w=["bits"])
                    P.op("vector", lambda e: e.tensor_tensor(bx(4), gn(4), gn(3), ALU.subtract),
                         r=["ntl"], w=["bits"])
                    P.op("vector", lambda e: e.tensor_tensor(bx(4), bx(4), bx(1), ALU.mult),
                         r=["bits"], w=["bits"])
                    P.op("vector", lambda e: e.tensor_tensor(bx(4), gn(3), bx(4), ALU.add),
                         r=["ntl", "bits"], w=["bits"])
                    P.op("vector", lambda e: e.tensor_tensor(bx(5), gn(6), gn(5), ALU.subtract),
                         r=["ntl"], w=["bits"])
                    P.op("vector", lambda e: e.tensor_tensor(bx(5), bx(5), bx(1), ALU.mult),
                         r=["bits"], w=["bits"])
                    P.op("vector", lambda e: e.tensor_tensor(bx(5), gn(5), bx(5), ALU.add),
                         r=["ntl", "bits"], w=["bits"])
                    P.op("vector", lambda e: e.tensor_tensor(bx(5), bx(5), bx(4), ALU.subtract),
                         r=["bits"], w=["bits"])
                    P.op("vector", lambda e: e.tensor_tensor(bx(5), bx(5), bx(0), ALU.mult),
                         r=["bits"], w=["bits"])
                    P.op("vector", lambda e: e.tensor_tensor(bx(2), bx(4), bx(5), ALU.add),
                         r=["bits"], w=["bits"])
                    for i in range(4):
                        P.op("vector", lambda e, i=i: e.tensor_tensor(
                            bits[:, (6 + i) * 3:(7 + i) * 3], gn(8 + 2 * i), gn(7 + 2 * i), ALU.subtract),
                            r=["ntl"], w=["bits"])
                        P.op("vector", lambda e, i=i: e.tensor_tensor(
                            bits[:, (6 + i) * 3:(7 + i) * 3], bits[:, (6 + i) * 3:(7 + i) * 3],
                            bx(2), ALU.mult), r=["bits"], w=["bits"])
                        P.op("vector", lambda e, i=i: e.tensor_tensor(
                            bits[:, (6 + i) * 3:(7 + i) * 3], gn(7 + 2 * i),
                            bits[:, (6 + i) * 3:(7 + i) * 3], ALU.add),
                            r=["ntl", "bits"], w=["bits"])
                    P.op("vector", lambda e: e.tensor_tensor(bx(4), bits[:, 21:24], bits[:, 18:21], ALU.subtract),
                         r=["bits"], w=["bits"])
                    P.op("vector", lambda e: e.tensor_tensor(bx(4), bx(4), bx(1), ALU.mult),
                         r=["bits"], w=["bits"])
                    P.op("vector", lambda e: e.tensor_tensor(bx(4), bits[:, 18:21], bx(4), ALU.add),
                         r=["bits"], w=["bits"])
                    P.op("vector", lambda e: e.tensor_tensor(bx(5), bits[:, 27:30], bits[:, 24:27], ALU.subtract),
                         r=["bits"], w=["bits"])
                    P.op("vector", lambda e: e.tensor_tensor(bx(5), bx(5), bx(1), ALU.mult),
                         r=["bits"], w=["bits"])
                    P.op("vector", lambda e: e.tensor_tensor(bx(5), bits[:, 24:27], bx(5), ALU.add),
                         r=["bits"], w=["bits"])
                    P.op("vector", lambda e: e.tensor_tensor(bx(5), bx(5), bx(4), ALU.subtract),
                         r=["bits"], w=["bits"])
                    P.op("vector", lambda e: e.tensor_tensor(bx(5), bx(5), bx(0), ALU.mult),
                         r=["bits"], w=["bits"])
                    P.op("vector", lambda e: e.tensor_tensor(bx(3), bx(4), bx(5), ALU.add),
                         r=["bits"], w=["bits"])
                    P.op("vector", lambda e: e.tensor_scalar(bx(4), bx(0), 2.0, None, ALU.mult),
                         r=["bits"], w=["bits"])
                    P.op("vector", lambda e: e.tensor_tensor(bx(4), bx(4), bx(1), ALU.add),
                         r=["bits"], w=["bits"])
                    P.op("vector", lambda e: e.tensor_scalar(bx(4), bx(4), 2.0, None, ALU.mult),
                         r=["bits"], w=["bits"])
                    P.op("vector", lambda e: e.tensor_tensor(bx(4), bx(4), bx(2), ALU.add),
                         r=["bits"], w=["bits"])
                    P.op("vector", lambda e: e.tensor_scalar(bx(4), bx(4), 2.0, None, ALU.mult),
                         r=["bits"], w=["bits"])
                    P.op("vector", lambda e: e.tensor_tensor(bx(4), bx(4), bx(3), ALU.add),
                         r=["bits"], w=["bits"])
                    for g in range(3):
                        P.op("vector", lambda e, g=g: e.tensor_scalar(
                            selp[:, g:48:3], c.iota16[:], bits[:, 12 + g:13 + g], None, ALU.is_equal),
                            r=["bits", "iota"], w=["selp"])
                    P.op("tensor", lambda e: e.transpose(
                        c.ps[1][0:48, 0:128], selp[:, 0:48], c.iden_f[:]),
                        r=["selp", "iden_f"], w=["ps1"])
                    tcol = t0 + sub * 128
                    P.op("vector", lambda e, tcol=tcol: e.tensor_copy(
                        selT[0:48, tcol:tcol + 128], c.ps[1][0:48, 0:128]),
                        r=["ps1"], w=["selT"])
                    P.op("vector", lambda e, tcol=tcol: e.tensor_copy(
                        selTb[0:48, tcol:tcol + 128], c.ps[1][0:48, 0:128]),
                        r=["ps1"], w=["selTb"])

            with pan_scope(c) as panL:
                for p in range(NP):
                    emit_ln_panel(c, panL, c.x3d, seq * S + p * W, lnfm=lnfm,
                                  lnfm_stride=S + 1, lnfm_off=1 + p * W,
                                  cb=nodes_cb, cb_arg=p)

            def lsl(k, t0, w_):
                return lnfm[:, k * (S + 1) + 1 + t0: k * (S + 1) + 1 + t0 + w_]

            for g, tag in enumerate(("q", "k", "v")):
                load_weight_bf(c, wstage, c.I[f"d_W1cat_{tag}"], wsb, f"D_w1_{g}", 1024)
                P.op("sync", lambda e, tag=tag: e.dma_start(
                    out=bias[:, 0:8], in_=c.I[f"d_b1cat_{tag}"]), w=["Dbias"], dma=True)
                for fc in range(KC):
                    for p in range(NP):
                        t0 = p * W
                        for k in range(KC):
                            P.op("tensor", lambda e, fc=fc, k=k, t0=t0, g=g: e.matmul(
                                c.ps[0][:], wsb[:, k * 1024 + fc * 128:k * 1024 + fc * 128 + 128],
                                lsl(k, t0, W), start=(k == 0), stop=(k == KC - 1)),
                                r=[f"D_w1_{g}", ("lnfm", k, t0 // W)], w=["ps0"])
                        P.op("scalar", lambda e, fc=fc, t0=t0: e.activation(
                            hm[:, fc * S + t0: fc * S + t0 + W], c.ps[0][:], AF.Relu,
                            bias=bias[:, fc:fc + 1]),
                            r=["ps0", "Dbias"], w=[("hm", fc)])
                        P.op("tensor", lambda e, fc=fc, t0=t0, g=g: e.matmul(
                            c.ps[1][:], easb[0:48, g * 1024 + fc * 128: g * 1024 + fc * 128 + 128],
                            selTb[0:48, t0:t0 + W], start=True, stop=True),
                            r=["selTb", "ea"], w=["ps1"])
                        P.op("vector", lambda e, fc=fc, t0=t0: e.tensor_tensor(
                            hm[:, fc * S + t0: fc * S + t0 + W],
                            hm[:, fc * S + t0: fc * S + t0 + W], c.ps[1][:], ALU.mult),
                            r=[("hm", fc), "ps1"], w=[("hm", fc)])
                load_weight_bf(c, wstage, c.I[f"d_W2cat_{tag}"], wsb, f"D_w2_{g}", 1024)
                P.op("sync", lambda e, g=g: e.dma_start(
                    out=b2p[:], in_=c.I["d_b2pad"][g]), w=["b2p"], dma=True)
                dst = (c.qd, c.kd, c.vd)[g]
                for dc in range(KC):
                    for p in range(NP):
                        t0 = p * W
                        for fc in range(KC):
                            P.op("tensor", lambda e, dc=dc, fc=fc, t0=t0, g=g: e.matmul(
                                c.ps[0][:], wsb[:, fc * 1024 + dc * 128:fc * 1024 + dc * 128 + 128],
                                hm[:, fc * S + t0: fc * S + t0 + W],
                                start=(fc == 0), stop=(fc == KC - 1)),
                                r=[f"D_w2_{g}", ("hm", fc)], w=["ps0"])
                        P.op("tensor", lambda e, dc=dc, t0=t0: e.matmul(
                            c.ps[1][:], b2p[0:48, dc * 128:(dc + 1) * 128], selT[0:48, t0:t0 + W],
                            start=True, stop=True),
                            r=["b2p", "selT"], w=["ps1"])
                        P.op("scalar", lambda e: e.copy(qtile[:, 0:W], c.ps[0][:]),
                             r=["ps0"], w=["qtile"])
                        P.op("vector", lambda e: e.tensor_tensor(
                            qbtile[:, 0:W], qtile[:, 0:W], c.ps[1][:], ALU.add),
                            r=["qtile", "ps1"], w=["qbf"])
                        P.op("sync", lambda e, dc=dc, t0=t0, dst=dst: e.dma_start(
                            out=dst[dc * 128:(dc + 1) * 128, seq * S + t0: seq * S + t0 + W],
                            in_=qbtile[:, 0:W]),
                            r=["qbf"], w=["qkvst"], dma=True)

        P.barrier()
        with contextlib.ExitStack() as st:
            ent = st.enter_context
            qfm = ent(_sb(nc, "D_qfm", [128, KC * S], BF16))
            kfm = ent(_sb(nc, "D_kfm", [128, KC * S], BF16))
            vtm = ent(_sb(nc, "D_vtm", [128, 16 * 1024], BF16))
            probs = ent(_sb(nc, "D_probs", [128, 16 * W], BF16))
            sqb = ent(_sb(nc, "D_sqb", [128, W], BF16))
            maxn = ent(_sb(nc, "D_maxn", [1, 8], F32))
            cbx = ent(_sb(nc, "D_cbx", [128, 1], F32))
            bcsb = ent(_sb(nc, "D_bcsb", [128, W], F32))
            ctxt = ent(_sb(nc, "D_ctxt", [128, W], F32))
            vstage = ent(_sb(nc, "D_vstage", [128, S], BF16))

            for ch in range(KC):
                P.op("sync", lambda e, ch=ch: e.dma_start(
                    out=qfm[:, ch * S:(ch + 1) * S],
                    in_=c.qd[ch * 128:(ch + 1) * 128, seq * S:(seq + 1) * S]),
                    r=["qkvst"], w=[("qfm", ch)], dma=True)
                P.op("sync", lambda e, ch=ch: e.dma_start(
                    out=kfm[:, ch * S:(ch + 1) * S],
                    in_=c.kd[ch * 128:(ch + 1) * 128, seq * S:(seq + 1) * S]),
                    r=["qkvst"], w=[("kfm", ch)], dma=True)
                P.op("sync", lambda e, ch=ch: e.dma_start(
                    out=vstage[:, 0:S],
                    in_=c.vd[ch * 128:(ch + 1) * 128, seq * S:(seq + 1) * S]),
                    r=["qkvst"], w=["vstage"], dma=True)
                for kt in range(16):
                    P.op("tensor", lambda e, kt=kt: e.transpose(
                        c.ps_tr[:, 0:128], vstage[:, kt * 128:(kt + 1) * 128], c.iden_bf[:]),
                        r=["vstage", "iden_b"], w=["ps_tr"])
                    P.op("vector", lambda e, kt=kt, ch=ch: e.tensor_copy(
                        vtm[:, kt * 1024 + ch * 128: kt * 1024 + ch * 128 + 128],
                        c.ps_tr[:, 0:128]),
                        r=["ps_tr"], w=[("vtm", kt, ch)])
            for which, fmref, fmkey, mi in (("q", qfm, "qfm", 0), ("k", kfm, "kfm", 1)):
                for p in range(NP):
                    t0 = p * W
                    for ch in range(KC):
                        P.op("scalar", lambda e, ch=ch, t0=t0, fmref=fmref: e.activation(
                            sqb[:, 0:W], fmref[:, ch * S + t0:ch * S + t0 + W], AF.Square),
                            r=[(fmkey, ch)], w=["sqb"])
                        P.op("tensor", lambda e, ch=ch: e.matmul(
                            c.ps_r1[:], c.ones_c_bf[:], sqb[:, 0:W],
                            start=(ch == 0), stop=(ch == KC - 1)),
                            r=["sqb", "ones_c"], w=["ps_r1"])
                    P.op("vector", lambda e: e.reduce_max(maxn[0:1, 2:3], c.ps_r1[:], axis=AX.X),
                         r=["ps_r1"], w=["mxs"])
                    if p == 0:
                        P.op("vector", lambda e, mi=mi: e.tensor_copy(
                            maxn[0:1, mi:mi + 1], maxn[0:1, 2:3]), r=["mxs"], w=[("mx", mi)])
                    else:
                        P.op("vector", lambda e, mi=mi: e.tensor_tensor(
                            maxn[0:1, mi:mi + 1], maxn[0:1, mi:mi + 1], maxn[0:1, 2:3], ALU.max),
                            r=["mxs", ("mx", mi)], w=[("mx", mi)])
            P.op("vector", lambda e: e.tensor_tensor(
                maxn[0:1, 2:3], maxn[0:1, 0:1], maxn[0:1, 1:2], ALU.mult),
                r=[("mx", 0), ("mx", 1)], w=["csc"])
            P.op("scalar", lambda e: e.activation(maxn[0:1, 2:3], maxn[0:1, 2:3], AF.Sqrt),
                 r=["csc"], w=["csc"])
            P.op("vector", lambda e: e.tensor_scalar(
                maxn[0:1, 2:3], maxn[0:1, 2:3], -1.0 / 32.0, None, ALU.mult),
                r=["csc"], w=["csc"])
            P.op("tensor", lambda e: e.matmul(
                c.ps[0][:, 0:1], c.ones_r_f[:], maxn[0:1, 2:3], start=True, stop=True),
                r=["csc", "ones_rf"], w=["ps0"])
            P.op("vector", lambda e: e.tensor_copy(cbx[:], c.ps[0][:, 0:1]), r=["ps0"], w=["cbx"])

            for qp in range(NP):
                q0 = qp * W
                for kc_i in range(16):
                    for ch in range(KC):
                        P.op("tensor", lambda e, kc_i=kc_i, ch=ch, q0=q0: e.matmul(
                            c.ps[0][:], kfm[:, ch * S + kc_i * 128: ch * S + kc_i * 128 + 128],
                            qfm[:, ch * S + q0: ch * S + q0 + W],
                            start=(ch == 0), stop=(ch == KC - 1)),
                            r=[("kfm", ch), ("qfm", ch)], w=["ps0"])
                    P.op("scalar", lambda e, kc_i=kc_i: e.activation(
                        probs[:, kc_i * W:(kc_i + 1) * W], c.ps[0][:], AF.Exp,
                        bias=cbx[:, 0:1], scale=1.0 / 32.0),
                        r=["ps0", "cbx"], w=[("probs", kc_i)])
                    P.op("tensor", lambda e, kc_i=kc_i: e.matmul(
                        c.ps_r1[:], c.ones_c_bf[:], probs[:, kc_i * W:(kc_i + 1) * W],
                        start=(kc_i == 0), stop=(kc_i == 15)),
                        r=[("probs", kc_i), "ones_c"], w=["ps_r1"])
                P.op("vector", lambda e: e.reciprocal(c.rows[0:1, 4 * W:5 * W], c.ps_r1[:]),
                     r=["ps_r1"], w=["inv"])
                P.op("tensor", lambda e: e.matmul(
                    c.ps[4][:], c.ones_r_f[:], c.rows[0:1, 4 * W:5 * W], start=True, stop=True),
                    r=["inv", "ones_rf"], w=["ps4"])
                P.op("vector", lambda e: e.tensor_copy(bcsb[:], c.ps[4][:]), r=["ps4"], w=["bcsb"])
                for dc in range(KC):
                    for kc_i in range(16):
                        P.op("tensor", lambda e, dc=dc, kc_i=kc_i: e.matmul(
                            c.ps[1][:], vtm[:, kc_i * 1024 + dc * 128: kc_i * 1024 + dc * 128 + 128],
                            probs[:, kc_i * W:(kc_i + 1) * W],
                            start=(kc_i == 0), stop=(kc_i == 15)),
                            r=[("vtm", kc_i, dc), ("probs", kc_i)], w=["ps1"])
                    P.op("scalar", lambda e: e.copy(ctxt[:, 0:W], c.ps[1][:]),
                         r=["ps1"], w=["ctxt"])
                    P.op("vector", lambda e: e.tensor_tensor(
                        ctxt[:, 0:W], ctxt[:, 0:W], bcsb[:], ALU.mult),
                        r=["ctxt", "bcsb"], w=["ctxt"])
                    P.op("sync", lambda e, dc=dc, q0=q0: e.dma_start(
                        out=c.ctxd[dc * 128:(dc + 1) * 128, seq * S + q0: seq * S + q0 + W],
                        in_=ctxt[:, 0:W]),
                        r=["ctxt"], w=["ctxst"], dma=True)
            with pan_scope(c) as panZ:
                emit_closing(c, panZ, c.ctxd, c.x3d, c.out_fm, seq, branch_key="ctxst")
        P.barrier()



    for seq in range(NSEQ):
        _seq(seq)


def kernel(**inputs):
    in_maps = host_layout(inputs)
    nc = build_program(serial=False)
    res = run_bass_kernel_spmd(nc, in_maps, core_ids=list(range(NCORE)))
    outs = []
    for ci in range(NCORE):
        o = np.asarray(res.results[ci]["out_fm"])
        outs.append(o.reshape(D, NSEQ, S).transpose(1, 2, 0))
    return np.concatenate(outs, axis=0).astype(np.float32)


# version 07 (final)
# speedup vs baseline: 2.1428x; 1.0863x over previous
import numpy as np
import ml_dtypes
import concourse.bass as bass
import concourse.mybir as mybir
from concourse.bass_utils import run_bass_kernel_spmd

F32 = mybir.dt.float32
BF16 = mybir.dt.bfloat16
I32 = mybir.dt.int32
AF = mybir.ActivationFunctionType
ALU = mybir.AluOpType
AX = mybir.AxisListType
BFNP = ml_dtypes.bfloat16

D = 1024
S = 2048
NSEQ = 2
T = NSEQ * S
KC = 8
NCORE = 8
W = 512
NP = S // W
EPS = 1e-5
PI = float(np.pi)

ENGS = ("sync", "tensor", "vector", "scalar", "gpsimd")


class Prog:

    def __init__(self, serial=False):
        self.serial = serial
        self.items = []
        self.cnt = {}
        for e in ENGS:
            self.cnt[(e, "c")] = 0
            self.cnt[(e, "d")] = 0
        self.wr = {}
        self.rd = {}

    def op(self, eng, fn, r=(), w=(), dma=False):
        kind = "d" if dma else "c"
        inc = 16 if dma else 1
        if dma and eng == "sync":
            self._rr = getattr(self, "_rr", 0) ^ 1
            eng = ("sync", "gpsimd")[self._rr]
        waits = {}

        def need(e2, k2, c2, raw=True):
            if (e2, k2) == (eng, kind) and k2 == "d":
                return
            if e2 == eng and kind == "d" and k2 == "c":
                return
            if not raw and e2 == eng and kind == "c" and k2 == "c":
                return
            if waits.get((e2, k2), 0) < c2:
                waits[(e2, k2)] = c2

        for k in r:
            lw = self.wr.get(k)
            if lw:
                need(*lw)
        for k in w:
            lw = self.wr.get(k)
            if lw:
                need(*lw, raw=False)
            for (e2, k2), c2 in self.rd.get(k, {}).items():
                need(e2, k2, c2, raw=False)
        if self.serial:
            for (e2, k2), c2 in self.cnt.items():
                if (e2, k2) != (eng, kind) and c2 > 0:
                    if not (e2 == eng and kind == "d" and k2 == "c"):
                        waits[(e2, k2)] = c2
        for (e2, k2), c2 in waits.items():
            assert c2 <= self.cnt[(e2, k2)], "dependency on future op (bug)"
        end = self.cnt[(eng, kind)] + inc
        self.items.append((eng, waits, fn, inc, kind))
        self.cnt[(eng, kind)] = end
        for k in r:
            self.rd.setdefault(k, {})[(eng, kind)] = end
        for k in w:
            self.wr[k] = (eng, kind, end)
            self.rd[k] = {}

    def barrier(self):
        snap = dict(self.cnt)
        for e in ENGS:
            waits = {ek: cv for ek, cv in snap.items()
                     if ek != (e, "c") and cv > 0}
            self.items.append((e, waits, lambda eng: eng.nop(), 1, "c"))
            self.cnt[(e, "c")] += 1
        self.wr.clear()
        self.rd.clear()

    def emit(self, nc, block):
        sems = {}
        guards = []
        for e in ENGS:
            for kind in ("c", "d"):
                g = nc.semaphore(f"prog_{e}_{kind}")
                sems[(e, kind)] = g.__enter__()
                guards.append(g)
        final = dict(self.cnt)

        def emit_engine(engine_name, eng_obj):
            watermark = {ek: 0 for ek in self.cnt}
            cnt = {ek: 0 for ek in self.cnt}
            for eng, waits, fn, inc, kind in self.items:
                if eng != engine_name:
                    continue
                for ek, c2 in sorted(waits.items()):
                    if c2 > watermark[ek]:
                        eng_obj.wait_ge(sems[ek], c2)
                        watermark[ek] = c2
                inst = fn(eng_obj)
                inst.then_inc(sems[(engine_name, kind)], inc)
                cnt[(engine_name, kind)] += inc
                if kind == "d":
                    eng_obj.wait_ge(sems[(engine_name, "d")], cnt[(engine_name, "d")])
                    watermark[(engine_name, "d")] = cnt[(engine_name, "d")]
            for ek, cv in final.items():
                if ek[0] != engine_name and cv > watermark[ek]:
                    eng_obj.wait_ge(sems[ek], cv)
            if final[(engine_name, "d")] > watermark[(engine_name, "d")]:
                eng_obj.wait_ge(sems[(engine_name, "d")], final[(engine_name, "d")])

        @block.sync
        def _(e):
            emit_engine("sync", e)

        @block.tensor
        def _(e):
            emit_engine("tensor", e)

        @block.vector
        def _(e):
            emit_engine("vector", e)

        @block.scalar
        def _(e):
            emit_engine("scalar", e)

        @block.gpsimd
        def _(e):
            emit_engine("gpsimd", e)

        return guards



def _bias_pc(b):
    return np.ascontiguousarray(np.asarray(b, np.float32).reshape(-1, 128).T)


def host_layout(inp):
    f = lambda a: np.asarray(a, np.float32)
    L = {}
    L["ppT"] = np.ascontiguousarray(f(inp["L0_attn_pos_phases"])[:S].T)
    for nm in ("Wpv", "Wk", "Wkv", "Wb1", "Wo"):
        L["a_" + nm] = f(inp["L0_attn_" + nm])
    L["a_Wg1a"] = np.ascontiguousarray(f(inp["L0_attn_Wg1"])[:D])
    L["a_Wg1b"] = np.ascontiguousarray(f(inp["L0_attn_Wg1"])[D:])
    L["a_Wg2"] = np.ascontiguousarray(f(inp["L0_attn_Wg2"]).reshape(8, 128).T)
    L["a_Wb2"] = np.ascontiguousarray(
        f(inp["L0_attn_Wb2"]).reshape(4, 128, 2).transpose(1, 0, 2).reshape(128, 8))
    ab = np.zeros((128, 64), np.float32)
    ab[:, 0:8] = _bias_pc(inp["L0_attn_bpv"])
    ab[:, 8:16] = _bias_pc(inp["L0_attn_bk"])
    ab[:, 16:24] = _bias_pc(inp["L0_attn_bkv"])
    ab[:, 24:32] = _bias_pc(inp["L0_attn_bg1"])
    ab[:, 32:40] = _bias_pc(inp["L0_attn_bo"])
    ab[:, 40:44] = _bias_pc(inp["L0_attn_bb1"])
    ab[:, 44:52] = _bias_pc(inp["L0_attn_ln_g"])
    ab[:, 52:60] = _bias_pc(inp["L0_attn_ln_b"])
    ab[0, 60] = f(inp["L0_attn_bg2"]).reshape(-1)[0]
    ab[0, 61] = f(inp["L0_attn_bb2"]).reshape(-1)[0]
    ab[0, 62] = f(inp["L0_attn_bb2"]).reshape(-1)[1]
    L["a_bias"] = ab
    perm = np.concatenate([np.arange(0, 2048, 2), np.arange(1, 2048, 2)])
    L["f_gwT"] = np.ascontiguousarray(
        f(inp["L0_ffn_gate_w"]).T.reshape(8, 128, 8).transpose(1, 0, 2).reshape(128, 64))
    L["f_gb"] = f(inp["L0_ffn_gate_b"]).reshape(1, 8)
    L["f_w1p"] = np.ascontiguousarray(f(inp["L0_ffn_w1"])[:, perm, :].transpose(0, 2, 1))
    L["f_w3p"] = np.ascontiguousarray(f(inp["L0_ffn_w3"])[:, perm, :].transpose(0, 2, 1))
    L["f_w2T"] = np.ascontiguousarray(f(inp["L0_ffn_w2"]).transpose(0, 2, 1))
    L["f_b13p"] = np.concatenate(
        [np.stack([_bias_pc(f(inp["L0_ffn_b1"])[e][perm]) for e in range(8)]),
         np.stack([_bias_pc(f(inp["L0_ffn_b3"])[e][perm]) for e in range(8)])], axis=2)
    L["f_b2"] = f(inp["L0_ffn_b2"])
    L["f_sw1p"] = np.ascontiguousarray(f(inp["L0_ffn_sw1"])[perm].T)
    L["f_sw3p"] = np.ascontiguousarray(f(inp["L0_ffn_sw3"])[perm].T)
    L["f_sw2T"] = np.ascontiguousarray(f(inp["L0_ffn_sw2"]).T)
    sb = np.zeros((128, 40), np.float32)
    sb[:, 0:16] = _bias_pc(f(inp["L0_ffn_sb1"])[perm])
    sb[:, 16:32] = _bias_pc(f(inp["L0_ffn_sb3"])[perm])
    sb[:, 32:40] = _bias_pc(inp["L0_ffn_sb2"])
    L["f_sbias"] = sb
    for nm in ("Wq", "Wk", "Wv"):
        L["c_" + nm] = f(inp["L1_attn_" + nm])
    cb = np.zeros((128, 24), np.float32)
    cb[:, 0:8] = _bias_pc(inp["L1_attn_bq"])
    cb[:, 8:16] = _bias_pc(inp["L1_attn_bk"])
    cb[:, 16:24] = _bias_pc(inp["L1_attn_bv"])
    L["c_bias"] = cb
    nwT = np.zeros((D, 45), np.float32)
    nb = np.zeros((45, 1), np.float32)
    b2pad = np.zeros((3, 48, D), np.float32)
    for g, tag in enumerate(("q", "k", "v")):
        nw = f(inp[f"L1_ffn_{tag}_node_w"])
        nwT[:, np.arange(15) * 3 + g] = nw.T
        nb[np.arange(15) * 3 + g, 0] = f(inp[f"L1_ffn_{tag}_node_b"])
        b2pad[g, np.arange(16) * 3 + g, :] = f(inp[f"L1_ffn_{tag}_b2"]).reshape(16, D)
        L[f"d_W1cat_{tag}"] = np.ascontiguousarray(
            f(inp[f"L1_ffn_{tag}_w1"]).transpose(1, 0, 2).reshape(D, 1024))
        L[f"d_W2cat_{tag}"] = np.ascontiguousarray(
            f(inp[f"L1_ffn_{tag}_w2"]).reshape(1024, D))
        L[f"d_b1cat_{tag}"] = _bias_pc(f(inp[f"L1_ffn_{tag}_b1"]).reshape(1024))
    L["d_nwT"] = np.ascontiguousarray(nwT.reshape(8, 128, 45).transpose(1, 0, 2).reshape(128, 360))
    L["d_nb"] = nb
    L["d_b2pad"] = np.ascontiguousarray(b2pad)

    L["iden_f"] = np.eye(128, dtype=np.float32)
    L["iden_bf"] = np.eye(128, dtype=BFNP)
    L["iota16"] = np.broadcast_to(np.arange(16, dtype=np.float32), (128, 16)).copy()
    e_all = np.zeros((48, 3 * 1024), BFNP)
    for g in range(3):
        for l in range(16):
            e_all[l * 3 + g, g * 1024 + l * 64 : g * 1024 + (l + 1) * 64] = 1
    L["E_all"] = e_all

    x = f(inp["L0_attn_x"])
    in_maps = []
    for c in range(NCORE):
        m = dict(L)
        m["x_fm"] = np.ascontiguousarray(
            x[2 * c : 2 * c + 2].transpose(2, 0, 1).reshape(D, T))
        in_maps.append(m)
    return in_maps



class Ctx:
    pass


def build_program(dbg=False, stop_after="D", serial=False, detect_races=True, a_parts=6,
                  serial_a=False):
    nc = bass.Bass("TRN2", target_bir_lowering=False, detect_race_conditions=detect_races)
    P = Prog(serial=serial or serial_a)
    c = Ctx()
    c.nc, c.P = nc, P

    def din(name, shape):
        return nc.dram_tensor(name, list(shape), F32, kind="ExternalInput").ap()

    c.x_fm = din("x_fm", (D, T))
    names = [
        ("ppT", (D, S)), ("a_Wpv", (D, D)), ("a_Wk", (D, D)), ("a_Wkv", (D, D)),
        ("a_Wb1", (D, 512)), ("a_Wo", (D, D)), ("a_Wg1a", (D, D)), ("a_Wg1b", (D, D)),
        ("a_Wg2", (128, 8)), ("a_Wb2", (128, 8)), ("a_bias", (128, 64)),
        ("f_gwT", (128, 64)), ("f_gb", (1, 8)),
        ("f_w1p", (8, D, 2048)), ("f_w3p", (8, D, 2048)), ("f_w2T", (8, D, D)),
        ("f_b13p", (8, 128, 32)), ("f_b2", (8, D)),
        ("f_sw1p", (D, 2048)), ("f_sw3p", (D, 2048)), ("f_sw2T", (D, D)),
        ("f_sbias", (128, 40)),
        ("c_Wq", (D, D)), ("c_Wk", (D, D)), ("c_Wv", (D, D)), ("c_bias", (128, 24)),
        ("d_nwT", (128, 360)), ("d_nb", (45, 1)), ("d_b2pad", (3, 48, D)),
        ("d_W1cat_q", (D, 1024)), ("d_W1cat_k", (D, 1024)), ("d_W1cat_v", (D, 1024)),
        ("d_W2cat_q", (1024, D)), ("d_W2cat_k", (1024, D)), ("d_W2cat_v", (1024, D)),
        ("d_b1cat_q", (128, 8)), ("d_b1cat_k", (128, 8)), ("d_b1cat_v", (128, 8)),
    ]
    c.I = {nm: din(nm, shp) for nm, shp in names}
    c.I["iden_f"] = din("iden_f", (128, 128))
    c.I["iota16"] = din("iota16", (128, 16))
    c.I["iden_bf"] = nc.dram_tensor("iden_bf", [128, 128], BF16, kind="ExternalInput").ap()
    c.I["E_all"] = nc.dram_tensor("E_all", [48, 3 * 1024], BF16, kind="ExternalInput").ap()

    c.out_fm = nc.dram_tensor("out_fm", [D, T], F32, kind="ExternalOutput").ap()
    sk = "ExternalOutput" if dbg else "Internal"
    c.x1d = nc.dram_tensor("x1d", [D, T], F32, kind=sk).ap()
    c.x2d = nc.dram_tensor("x2d", [D, T], F32, kind=sk).ap()
    c.x3d = nc.dram_tensor("x3d", [D, T], F32, kind=sk).ap()
    c.ln1d = nc.dram_tensor("ln1d", [D, T], F32, kind=sk).ap()
    c.ctxd = nc.dram_tensor("ctxd", [D, T], F32, kind=sk).ap()
    c.cwdT = nc.dram_tensor("cwdT", [8, T], F32, kind=sk).ap()
    c.tapb = nc.dram_tensor("tapb", [128, 3 * 8192], BF16, kind=sk).ap() if dbg else None
    c.tapf = nc.dram_tensor("tapf", [128, 6144], F32, kind=sk).ap() if dbg else None
    c.probd = nc.dram_tensor("probd", [128, 16 * 1040], BF16, kind=sk).ap() if dbg else None

    c.qd = nc.dram_tensor("qd", [D, T], BF16, kind="Internal").ap()
    c.kd = nc.dram_tensor("kd", [D, T], BF16, kind="Internal").ap()
    c.vd = nc.dram_tensor("vd", [D, T], BF16, kind="Internal").ap()
    c.w1b = nc.dram_tensor("w1b", [8, D, 2048], BF16, kind="Internal").ap()
    c.w3b = nc.dram_tensor("w3b", [8, D, 2048], BF16, kind="Internal").ap()
    c.w2b = nc.dram_tensor("w2b", [8, D, D], BF16, kind="Internal").ap()
    c.sw1b = nc.dram_tensor("sw1b", [D, 2048], BF16, kind="Internal").ap()
    c.sw3b = nc.dram_tensor("sw3b", [D, 2048], BF16, kind="Internal").ap()
    c.sw2b = nc.dram_tensor("sw2b", [D, D], BF16, kind="Internal").ap()

    import contextlib
    with contextlib.ExitStack() as stack:
        ent = lambda g: stack.enter_context(g)
        c.iden_bf = ent(nc.sbuf_tensor("iden_bf_s", [128, 128], BF16, align_bytes=512))
        c.iden_f = ent(nc.sbuf_tensor("iden_f_s", [128, 128], F32, align_bytes=512))
        c.ones_r_bf = ent(nc.sbuf_tensor("ones_r_bf_s", [1, 128], BF16, align_bytes=512))
        c.ones_r_f = ent(nc.sbuf_tensor("ones_r_f_s", [1, 128], F32, align_bytes=512))
        c.ones_c_bf = ent(nc.sbuf_tensor("ones_c_bf_s", [128, 1], BF16, align_bytes=512))
        c.iota16 = ent(nc.sbuf_tensor("iota16_s", [128, 16], F32, align_bytes=512))
        c.ones_bf = ent(nc.sbuf_tensor("ones_bf_s", [128, S], BF16, align_bytes=512))
        c.halfpi = ent(nc.sbuf_tensor("halfpi_s", [128, 1], F32, align_bytes=512))
        c.rows = ent(nc.sbuf_tensor("rows", [1, 5 * W], F32, align_bytes=512))
        c.ps = [ent(nc.psum_tensor(f"ps_{i}", [128, 512], F32)) for i in range(5)]
        c.ps_r1 = ent(nc.psum_tensor("ps_r1", [1, 512], F32))
        c.ps_r2 = ent(nc.psum_tensor("ps_r2", [1, 512], F32))
        c.ps_tr = ent(nc.psum_tensor("ps_tr", [128, 256], BF16))
        assert nc.psum_banks_remaining >= 0

        for dst, srcap in [(c.iden_bf, c.I["iden_bf"]), (c.iden_f, c.I["iden_f"]),
                           (c.iota16, c.I["iota16"])]:
            P.op("sync", lambda e, d=dst, s=srcap: e.dma_start(out=d[:], in_=s),
                 w=[d_key(dst)], dma=True)
        P.op("vector", lambda e: e.memset(c.ones_c_bf[:], 1.0), w=["ones_c"])
        P.op("vector", lambda e: e.memset(c.ones_r_bf[:], 1.0), w=["ones_rb"])
        P.op("vector", lambda e: e.memset(c.ones_r_f[:], 1.0), w=["ones_rf"])
        P.op("vector", lambda e: e.memset(c.ones_bf[:], 1.0), w=["ones_bf"])
        P.op("vector", lambda e: e.memset(c.halfpi[:], PI / 2), w=["halfpi"])
        P.barrier()

        if a_parts >= 0 and a_parts != 90:
            emit_precast(c)
        P.barrier()
        emit_phaseA(c, a_parts)
        P.barrier()
        if serial_a:
            P.serial = serial
        if stop_after != "A":
            emit_phaseB(c)
            P.barrier()
        if stop_after not in ("A", "B"):
            emit_phaseC(c)
            P.barrier()
        if stop_after not in ("A", "B", "C"):
            emit_phaseD(c)
            P.barrier()
        if stop_after in ("A", "B", "C"):
            src = {"A": c.x1d, "B": c.x2d, "C": c.x3d}[stop_after]
            emit_dram_copy(c, src, c.out_fm)
            P.barrier()

        with nc.Block() as block:
            P.emit(nc, block)
    return nc


def d_key(t):
    return getattr(t, "name", None) or str(id(t))


import contextlib


_PAN_N = [0]
_UN = [0]


def _sb(nc, base, shape, dtype):
    _UN[0] += 1
    return nc.sbuf_tensor(f"{base}_{_UN[0]}", shape, dtype, align_bytes=512)


class Pan:
    pass


@contextlib.contextmanager
def pan_scope(c):
    _PAN_N[0] += 1
    n = _PAN_N[0]
    with (
        c.nc.sbuf_tensor(f"pan_x{n}", [128, KC * W], F32, align_bytes=512) as px,
        c.nc.sbuf_tensor(f"pan_l{n}", [128, KC * W], F32, align_bytes=512) as pl,
        c.nc.sbuf_tensor(f"pan_b{n}", [128, KC * W], BF16, align_bytes=512) as pb,
    ):
        pan = Pan()
        pan.x, pan.l, pan.b = px, pl, pb
        yield pan
    c.P.barrier()



def emit_dram_copy(c, src, dst):
    with pan_scope(c) as pan:
        _emit_dram_copy(c, pan, src, dst)


def _emit_dram_copy(c, pan, src, dst):
    P = c.P
    for ch in range(KC):
        for p in range(2 * NP):
            P.op("sync", lambda e, ch=ch, p=p: e.dma_start(
                out=pan.x[:, 0:W], in_=src[ch * 128:(ch + 1) * 128, p * W:(p + 1) * W]),
                w=["panx"], dma=True)
            P.op("sync", lambda e, ch=ch, p=p: e.dma_start(
                out=dst[ch * 128:(ch + 1) * 128, p * W:(p + 1) * W], in_=pan.x[:, 0:W]),
                r=["panx"], w=["dcopy"], dma=True)


def load_weight_bf(c, stage, dram_ap, wsb, key, ncol, stage_key="wstage"):
    P = c.P
    for k in range(KC):
        P.op("sync", lambda e, k=k: e.dma_start(
            out=stage[:, 0:ncol], in_=dram_ap[k * 128:(k + 1) * 128, 0:ncol]),
            w=[stage_key], dma=True)
        P.op("vector", lambda e, k=k: e.tensor_copy(
            wsb[:, k * ncol:(k + 1) * ncol], stage[:, 0:ncol]),
            r=[stage_key], w=[key])


def emit_ln_panel(c, pan, src_dram, col0, lnfm=None, lnfm_stride=0, lnfm_off=0,
                  store_f32=None, cb=None, cb_arg=None):
    P = c.P
    for ch in range(KC):
        P.op("sync", lambda e, ch=ch: e.dma_start(
            out=pan.x[:, ch * W:(ch + 1) * W],
            in_=src_dram[ch * 128:(ch + 1) * 128, col0:col0 + W]),
            w=[("panx", ch)], dma=True)
        P.op("vector", lambda e, ch=ch: e.tensor_copy(
            pan.b[:, ch * W:(ch + 1) * W], pan.x[:, ch * W:(ch + 1) * W]),
            r=[("panx", ch)], w=[("panb", ch)])
    for ch in range(KC):
        P.op("tensor", lambda e, ch=ch: e.matmul(
            c.ps_r1[:], c.ones_c_bf[:], pan.b[:, ch * W:(ch + 1) * W],
            start=(ch == 0), stop=(ch == KC - 1)),
            r=[("panb", ch), "ones_c"], w=["ps_r1"])
    for ch in range(KC):
        P.op("scalar", lambda e, ch=ch: e.activation(
            pan.l[:, ch * W:(ch + 1) * W], pan.b[:, ch * W:(ch + 1) * W], AF.Square),
            r=[("panb", ch)], w=[("sq", ch)])
        P.op("vector", lambda e, ch=ch: e.tensor_copy(
            pan.b[:, ch * W:(ch + 1) * W], pan.l[:, ch * W:(ch + 1) * W]),
            r=[("sq", ch)], w=[("panb", ch), ("panb_sq", ch)])
    for ch in range(KC):
        P.op("tensor", lambda e, ch=ch: e.matmul(
            c.ps_r2[:], c.ones_c_bf[:], pan.b[:, ch * W:(ch + 1) * W],
            start=(ch == 0), stop=(ch == KC - 1)),
            r=[("panb_sq", ch), "ones_c"], w=["ps_r2"])
    R = c.rows
    P.op("vector", lambda e: e.tensor_scalar(R[:, 0:W], c.ps_r1[:], 1.0 / D, None, ALU.mult),
         r=["ps_r1"], w=["row_mu"])
    P.op("vector", lambda e: e.tensor_scalar(R[:, W:2 * W], c.ps_r2[:], 1.0 / D, None, ALU.mult),
         r=["ps_r2"], w=["row_msq"])
    P.op("vector", lambda e: e.tensor_tensor(R[:, 2 * W:3 * W], R[:, 0:W], R[:, 0:W], ALU.mult),
         r=["row_mu"], w=["row_v"])
    P.op("vector", lambda e: e.tensor_tensor(R[:, W:2 * W], R[:, W:2 * W], R[:, 2 * W:3 * W], ALU.subtract),
         r=["row_msq", "row_v"], w=["row_msq"])
    P.op("vector", lambda e: e.tensor_scalar(R[:, W:2 * W], R[:, W:2 * W], EPS, None, ALU.add),
         r=["row_msq"], w=["row_msq"])
    P.op("scalar", lambda e: e.activation(R[:, 2 * W:3 * W], R[:, W:2 * W], AF.Sqrt),
         r=["row_msq"], w=["row_v"])
    P.op("vector", lambda e: e.reciprocal(R[:, 3 * W:4 * W], R[:, 2 * W:3 * W]),
         r=["row_v"], w=["row_rstd"])
    P.op("tensor", lambda e: e.matmul(c.ps[3][:], c.ones_r_f[:], R[0:1, 0:W], start=True, stop=True),
         r=["row_mu", "ones_rf"], w=["ps3"])
    P.op("tensor", lambda e: e.matmul(c.ps[4][:], c.ones_r_f[:], R[0:1, 3 * W:4 * W], start=True, stop=True),
         r=["row_rstd", "ones_rf"], w=["ps4"])
    for ch in range(KC):
        P.op("vector", lambda e, ch=ch: e.tensor_tensor(
            pan.l[:, ch * W:(ch + 1) * W], pan.x[:, ch * W:(ch + 1) * W], c.ps[3][:], ALU.subtract),
            r=[("panx", ch), "ps3"], w=[("panl", ch)])
        P.op("vector", lambda e, ch=ch: e.tensor_tensor(
            pan.l[:, ch * W:(ch + 1) * W], pan.l[:, ch * W:(ch + 1) * W], c.ps[4][:], ALU.mult),
            r=[("panl", ch), "ps4"], w=[("panl", ch)])

        if lnfm is not None:
            P.op("scalar", lambda e, ch=ch: e.copy(
                lnfm[:, ch * lnfm_stride + lnfm_off: ch * lnfm_stride + lnfm_off + W],
                pan.l[:, ch * W:(ch + 1) * W]),
                r=[("panl", ch)], w=[("lnfm", ch, lnfm_off // W)])
        if store_f32 is not None:
            P.op("sync", lambda e, ch=ch: e.dma_start(
                out=store_f32[ch * 128:(ch + 1) * 128, col0:col0 + W],
                in_=pan.l[:, ch * W:(ch + 1) * W]),
                r=[("panl", ch)], w=[("lnst", ch)], dma=True)
    if cb is not None:
        cb(cb_arg, pan)


def emit_closing(c, pan, branch_dram, stream_dram, out_dram, seq, branch_key=None):
    P = c.P
    bk = [branch_key] if branch_key else []
    for p in range(NP):
        col0 = seq * S + p * W
        for ch in range(KC):
            P.op("sync", lambda e, ch=ch, col0=col0: e.dma_start(
                out=pan.x[:, ch * W:(ch + 1) * W],
                in_=branch_dram[ch * 128:(ch + 1) * 128, col0:col0 + W]),
                r=bk, w=[("panx", ch)], dma=True)
        emit_ln_core(c, pan)
        for ch in range(KC):
            P.op("sync", lambda e, ch=ch, col0=col0: e.dma_start(
                out=pan.x[:, ch * W:(ch + 1) * W],
                in_=stream_dram[ch * 128:(ch + 1) * 128, col0:col0 + W]),
                w=[("panx", ch)], dma=True)
            P.op("vector", lambda e, ch=ch: e.tensor_tensor(
                pan.l[:, ch * W:(ch + 1) * W], pan.l[:, ch * W:(ch + 1) * W],
                pan.x[:, ch * W:(ch + 1) * W], ALU.add),
                r=[("panx", ch), ("panl", ch)], w=[("panl", ch)])
            P.op("sync", lambda e, ch=ch, col0=col0: e.dma_start(
                out=out_dram[ch * 128:(ch + 1) * 128, col0:col0 + W],
                in_=pan.l[:, ch * W:(ch + 1) * W]),
                r=[("panl", ch)], w=[("xst", ch)], dma=True)


def emit_ln_core(c, pan):
    P = c.P
    for ch in range(KC):
        P.op("vector", lambda e, ch=ch: e.tensor_copy(
            pan.b[:, ch * W:(ch + 1) * W], pan.x[:, ch * W:(ch + 1) * W]),
            r=[("panx", ch)], w=[("panb", ch)])
    for ch in range(KC):
        P.op("tensor", lambda e, ch=ch: e.matmul(
            c.ps_r1[:], c.ones_c_bf[:], pan.b[:, ch * W:(ch + 1) * W],
            start=(ch == 0), stop=(ch == KC - 1)),
            r=[("panb", ch), "ones_c"], w=["ps_r1"])
    for ch in range(KC):
        P.op("scalar", lambda e, ch=ch: e.activation(
            pan.l[:, ch * W:(ch + 1) * W], pan.b[:, ch * W:(ch + 1) * W], AF.Square),
            r=[("panb", ch)], w=[("sq", ch)])
        P.op("vector", lambda e, ch=ch: e.tensor_copy(
            pan.b[:, ch * W:(ch + 1) * W], pan.l[:, ch * W:(ch + 1) * W]),
            r=[("sq", ch)], w=[("panb", ch), ("panb_sq", ch)])
    for ch in range(KC):
        P.op("tensor", lambda e, ch=ch: e.matmul(
            c.ps_r2[:], c.ones_c_bf[:], pan.b[:, ch * W:(ch + 1) * W],
            start=(ch == 0), stop=(ch == KC - 1)),
            r=[("panb_sq", ch), "ones_c"], w=["ps_r2"])
    R = c.rows
    P.op("vector", lambda e: e.tensor_scalar(R[:, 0:W], c.ps_r1[:], 1.0 / D, None, ALU.mult),
         r=["ps_r1"], w=["row_mu"])
    P.op("vector", lambda e: e.tensor_scalar(R[:, W:2 * W], c.ps_r2[:], 1.0 / D, None, ALU.mult),
         r=["ps_r2"], w=["row_msq"])
    P.op("vector", lambda e: e.tensor_tensor(R[:, 2 * W:3 * W], R[:, 0:W], R[:, 0:W], ALU.mult),
         r=["row_mu"], w=["row_v"])
    P.op("vector", lambda e: e.tensor_tensor(R[:, W:2 * W], R[:, W:2 * W], R[:, 2 * W:3 * W], ALU.subtract),
         r=["row_msq", "row_v"], w=["row_msq"])
    P.op("vector", lambda e: e.tensor_scalar(R[:, W:2 * W], R[:, W:2 * W], EPS, None, ALU.add),
         r=["row_msq"], w=["row_msq"])
    P.op("scalar", lambda e: e.activation(R[:, 2 * W:3 * W], R[:, W:2 * W], AF.Sqrt),
         r=["row_msq"], w=["row_v"])
    P.op("vector", lambda e: e.reciprocal(R[:, 3 * W:4 * W], R[:, 2 * W:3 * W]),
         r=["row_v"], w=["row_rstd"])
    P.op("tensor", lambda e: e.matmul(c.ps[3][:], c.ones_r_f[:], R[0:1, 0:W], start=True, stop=True),
         r=["row_mu", "ones_rf"], w=["ps3"])
    P.op("tensor", lambda e: e.matmul(c.ps[4][:], c.ones_r_f[:], R[0:1, 3 * W:4 * W], start=True, stop=True),
         r=["row_rstd", "ones_rf"], w=["ps4"])
    for ch in range(KC):
        P.op("vector", lambda e, ch=ch: e.tensor_tensor(
            pan.l[:, ch * W:(ch + 1) * W], pan.x[:, ch * W:(ch + 1) * W], c.ps[3][:], ALU.subtract),
            r=[("panx", ch), "ps3"], w=[("panl", ch)])
        P.op("vector", lambda e, ch=ch: e.tensor_tensor(
            pan.l[:, ch * W:(ch + 1) * W], pan.l[:, ch * W:(ch + 1) * W], c.ps[4][:], ALU.mult),
            r=[("panl", ch), "ps4"], w=[("panl", ch)])



def emit_precast(c):
    nc, P = c.nc, c.P
    import contextlib
    with contextlib.ExitStack() as st:
        stage = st.enter_context(nc.sbuf_tensor("pc_stage", [128, 2048], F32, align_bytes=512))
        stageb = st.enter_context(nc.sbuf_tensor("pc_stageb", [128, 2048], BF16, align_bytes=512))
        jobs = []
        for e8 in range(8):
            jobs.append((c.I["f_w1p"][e8], c.w1b[e8], 2048))
            jobs.append((c.I["f_w3p"][e8], c.w3b[e8], 2048))
            jobs.append((c.I["f_w2T"][e8], c.w2b[e8], 1024))
        jobs.append((c.I["f_sw1p"], c.sw1b, 2048))
        jobs.append((c.I["f_sw3p"], c.sw3b, 2048))
        jobs.append((c.I["f_sw2T"], c.sw2b, 1024))
        for src, dst, ncol in jobs:
            for k in range(KC):
                P.op("sync", lambda e, src=src, k=k, ncol=ncol: e.dma_start(
                    out=stage[:, 0:ncol], in_=src[k * 128:(k + 1) * 128, 0:ncol]),
                    w=["pcs"], dma=True)
                P.op("vector", lambda e, k=k, ncol=ncol: e.tensor_copy(
                    stageb[:, 0:ncol], stage[:, 0:ncol]),
                    r=["pcs"], w=["pcb"])
                P.op("sync", lambda e, dst=dst, k=k, ncol=ncol: e.dma_start(
                    out=dst[k * 128:(k + 1) * 128, 0:ncol], in_=stageb[:, 0:ncol]),
                    r=["pcb"], w=["pcd"], dma=True)



def emit_phaseA(c, a_parts=6):
    nc, P = c.nc, c.P
    S2 = 1024
    NP2 = S2 // W
    import contextlib
    with contextlib.ExitStack() as st:
        ent = st.enter_context
        kvr = ent(nc.sbuf_tensor("A_kvr", [128, KC * S2], BF16, align_bytes=512))
        wsb = ent(nc.sbuf_tensor("A_wsb", [128, KC * 1024], BF16, align_bytes=512))
        wsb2 = ent(nc.sbuf_tensor("A_wsb2", [128, KC * 1024], BF16, align_bytes=512))
        wstage = ent(nc.sbuf_tensor("A_wstage", [128, 1024], F32, align_bytes=512))
        bias = ent(nc.sbuf_tensor("A_bias", [128, 72], F32, align_bytes=512))
        fsl = ent(nc.sbuf_tensor("A_f32", [128, 3 * S2], F32, align_bytes=512))
        i32t = ent(nc.sbuf_tensor("A_i32", [128, S2], I32, align_bytes=512))
        bfs = ent(nc.sbuf_tensor("A_bfs", [128, 8 * S2], BF16, align_bytes=512))
        kcb = ent(nc.sbuf_tensor("A_kcb", [128, S2 + 1], BF16, align_bytes=512))
        ksb = ent(nc.sbuf_tensor("A_ksb", [128, S2 + 1], BF16, align_bytes=512))
        grow = ent(nc.sbuf_tensor("A_grow", [1, 2 * S2], F32, align_bytes=512))
        growb = ent(nc.sbuf_tensor("A_growb", [1, S2], BF16, align_bytes=512))
        wb2sb = ent(nc.sbuf_tensor("A_wb2", [128, 8], F32, align_bytes=512))
        wb2d_bf = ent(nc.sbuf_tensor("A_wb2d_bf", [128, 4], BF16, align_bytes=512))
        wg2sb = ent(nc.sbuf_tensor("A_wg2", [128, 8], BF16, align_bytes=512))
        wg2f = ent(nc.sbuf_tensor("A_wg2f", [128, 8], F32, align_bytes=512))
        ones_bf2 = ent(nc.sbuf_tensor("A_ones2", [128, S2], BF16, align_bytes=512))
        onesr2 = ent(nc.sbuf_tensor("A_onesr2", [1, S2], F32, align_bytes=512))
        carry = ent(nc.sbuf_tensor("A_carry", [128, 32], F32, align_bytes=512))
        lncar = ent(nc.sbuf_tensor("A_lncar", [128, 8], BF16, align_bytes=512))
        kscar = ent(nc.sbuf_tensor("A_kscar", [128, 16], BF16, align_bytes=512))
        gccar = ent(nc.sbuf_tensor("A_gccar", [1, 1], F32, align_bytes=512))
        lnfm = ent(nc.sbuf_tensor("A_lnfm", [128, KC * (S2 + 1)], BF16, align_bytes=512))

        F0, F1, F2 = (fsl[:, i * S2:(i + 1) * S2] for i in range(3))
        SL = lambda i: bfs[:, i * S2:(i + 1) * S2]

        P.op("sync", lambda e: e.dma_start(out=bias[:, 0:64], in_=c.I["a_bias"]),
             w=[("Abias", k2) for k2 in (0, 8, 16, 24, 32, 40, 44, 52, 60, 61)], dma=True)
        P.op("sync", lambda e: e.dma_start(out=wb2sb[:, 0:8], in_=c.I["a_Wb2"]),
             w=["wb2sb"], dma=True)
        for cc in range(4):
            P.op("vector", lambda e, cc=cc: e.tensor_tensor(
                bias[:, 64 + cc:65 + cc], wb2sb[:, cc * 2:cc * 2 + 1],
                wb2sb[:, cc * 2 + 1:cc * 2 + 2], ALU.subtract),
                r=["wb2sb"], w=[("Abias", 64 + cc)])
        P.op("vector", lambda e: e.tensor_copy(wb2d_bf[:], bias[:, 64:68]),
             r=[("Abias", 64), ("Abias", 65), ("Abias", 66), ("Abias", 67)], w=["wb2d_bf"])
        P.op("vector", lambda e: e.tensor_tensor(
            bias[0:1, 63:64], bias[0:1, 61:62], bias[0:1, 62:63], ALU.subtract),
            r=[("Abias", 61)], w=[("Abias", 63)])
        P.op("sync", lambda e: e.dma_start(out=wg2f[:, 0:8], in_=c.I["a_Wg2"]),
             w=["wg2f"], dma=True)
        P.op("vector", lambda e: e.tensor_copy(wg2sb[:, 0:8], wg2f[:, 0:8]),
             r=["wg2f"], w=["A_wg2"])
        P.op("vector", lambda e: e.memset(ones_bf2[:], 1.0), w=["ones2"])
        P.op("vector", lambda e: e.memset(onesr2[:], 1.0), w=["onesr2"])

        def _half(hs):
            first = hs % 2 == 0
            base = hs * S2
            for ch in range(KC):
                if first:
                    P.op("vector", lambda e, ch=ch: e.memset(
                        lnfm[:, ch * (S2 + 1):ch * (S2 + 1) + 1], 0.0), w=[("lnfm", ch, "z")])
                else:
                    P.op("vector", lambda e, ch=ch: e.tensor_copy(
                        lnfm[:, ch * (S2 + 1):ch * (S2 + 1) + 1], lncar[:, ch:ch + 1]),
                        r=["lncar"], w=[("lnfm", ch, "z")])
            if first:
                P.op("vector", lambda e: e.memset(kcb[:, 0:1], 0.0), w=["kczero"])
                P.op("vector", lambda e: e.memset(ksb[:, 0:1], 0.0), w=["kszero"])
                P.op("vector", lambda e: e.memset(carry[:], 0.0), w=["carry"])
                P.op("vector", lambda e: e.memset(gccar[:], 0.0), w=["gccar"])

            with pan_scope(c) as pan1:
                for p in range(NP2):
                    emit_ln_panel(c, pan1, c.x_fm, base + p * W, lnfm=lnfm,
                                  lnfm_stride=S2 + 1, lnfm_off=1 + p * W, store_f32=c.ln1d)
            if first:
                for ch in range(KC):
                    P.op("vector", lambda e, ch=ch: e.tensor_copy(
                        lncar[:, ch:ch + 1], lnfm[:, ch * (S2 + 1) + S2:ch * (S2 + 1) + S2 + 1]),
                        r=[("lnfm", ch, NP2 - 1)], w=["lncar"])

            def lsl(k, t0, w_, sh=0):
                return lnfm[:, k * (S2 + 1) + 1 + t0 - sh: k * (S2 + 1) + 1 + t0 - sh + w_]

            if a_parts < 2 or a_parts in (90, 91):
                return
            c.P.barrier()
            load_weight_bf(c, wstage, c.I["a_Wg1a"], wsb, "A_wg1a", 1024)
            load_weight_bf(c, wstage, c.I["a_Wg1b"], wsb2, "A_wg1b", 1024)
            ghb = SL(3)
            for p in range(NP2):
                t0 = p * W
                for fc in range(KC):
                    for k in range(KC):
                        P.op("tensor", lambda e, fc=fc, k=k, t0=t0: e.matmul(
                            c.ps[0][:], wsb[:, k * 1024 + fc * 128:k * 1024 + fc * 128 + 128],
                            lsl(k, t0, W), start=(k == 0), stop=False),
                            r=["A_wg1a", ("lnfm", k, t0 // W)], w=["ps0"])
                        P.op("tensor", lambda e, fc=fc, k=k, t0=t0: e.matmul(
                            c.ps[0][:], wsb2[:, k * 1024 + fc * 128:k * 1024 + fc * 128 + 128],
                            lsl(k, t0, W, sh=1), start=False, stop=(k == KC - 1)),
                            r=["A_wg1b", ("lnfm", k, t0 // W),
                               ("lnfm", k, t0 // W - 1 if t0 else "z")], w=["ps0"])
                    P.op("scalar", lambda e, fc=fc, t0=t0: e.activation(
                        ghb[:, t0:t0 + W], c.ps[0][:], AF.Gelu, bias=bias[:, 24 + fc:25 + fc]),
                        r=["ps0", ("Abias", 24)], w=["ghb"])
                    P.op("tensor", lambda e, fc=fc, t0=t0: e.matmul(
                        c.ps_r1[:], wg2sb[:, fc:fc + 1], ghb[:, t0:t0 + W],
                        start=(fc == 0), stop=(fc == KC - 1)),
                        r=["ghb", "A_wg2"], w=["ps_r1"])
                P.op("scalar", lambda e, t0=t0: e.activation(
                    grow[0:1, t0:t0 + W], c.ps_r1[:], AF.Sigmoid, bias=bias[0:1, 60:61]),
                    r=["ps_r1", ("Abias", 60)], w=["g_row"])
            P.op("vector", lambda e: e.tensor_copy(growb[0:1, :], grow[0:1, 0:S2]),
                 r=["g_row"], w=["growb"])
            for p in range(NP2):
                P.op("tensor", lambda e, p=p: e.matmul(
                    c.ps[0][:], c.ones_r_bf[:], growb[0:1, p * W:(p + 1) * W],
                    start=True, stop=True), r=["growb", "ones_rb"], w=["ps0"])
                P.op("vector", lambda e, p=p: e.tensor_copy(
                    SL(0)[:, p * W:(p + 1) * W], c.ps[0][:]), r=["ps0"], w=[("bcast", 0)])
            P.op("vector", lambda e: e.tensor_tensor_scan(
                grow[0:1, S2:2 * S2], onesr2[0:1, :], grow[0:1, 0:S2],
                gccar[0:1, 0:1], ALU.mult, ALU.add),
                r=["g_row", "onesr2", "gccar"], w=["gc_row"])
            P.op("vector", lambda e: e.tensor_copy(gccar[0:1, 0:1], grow[0:1, 2 * S2 - 1:2 * S2]),
                 r=["gc_row"], w=["gccar"])
            P.op("vector", lambda e: e.tensor_scalar(
                grow[0:1, S2:2 * S2], grow[0:1, S2:2 * S2], 1.0, None, ALU.max),
                r=["gc_row"], w=["gc_row"])
            P.op("scalar", lambda e: e.activation(
                grow[0:1, 0:S2], grow[0:1, S2:2 * S2], AF.Sqrt),
                r=["gc_row"], w=["sqgc"])
            P.op("vector", lambda e: e.reciprocal(grow[0:1, S2:2 * S2], grow[0:1, 0:S2]),
                 r=["sqgc"], w=["r_row"])
            P.op("vector", lambda e: e.tensor_scalar(
                grow[0:1, S2:2 * S2], grow[0:1, S2:2 * S2], 1.0 / 32.0, None, ALU.mult),
                r=["r_row"], w=["r_row"])
            P.op("vector", lambda e: e.tensor_copy(growb[0:1, :], grow[0:1, S2:2 * S2]),
                 r=["r_row"], w=["growb"])
            for p in range(NP2):
                P.op("tensor", lambda e, p=p: e.matmul(
                    c.ps[0][:], c.ones_r_bf[:], growb[0:1, p * W:(p + 1) * W],
                    start=True, stop=True), r=["growb", "ones_rb"], w=["ps0"])
                P.op("vector", lambda e, p=p: e.tensor_copy(
                    SL(2)[:, p * W:(p + 1) * W], c.ps[0][:]), r=["ps0"], w=[("bcast", 2)])
            load_weight_bf(c, wstage, c.I["a_Wb1"], wsb, "A_wb1", 512)
            for p in range(NP2):
                t0 = p * W
                for fc in range(4):
                    for k in range(KC):
                        P.op("tensor", lambda e, fc=fc, k=k, t0=t0: e.matmul(
                            c.ps[0][:], wsb[:, k * 512 + fc * 128:k * 512 + fc * 128 + 128],
                            lsl(k, t0, W), start=(k == 0), stop=(k == KC - 1)),
                            r=["A_wb1", ("lnfm", k, t0 // W)], w=["ps0"])
                    P.op("scalar", lambda e, fc=fc, t0=t0: e.activation(
                        ghb[:, t0:t0 + W], c.ps[0][:], AF.Gelu, bias=bias[:, 40 + fc:41 + fc]),
                        r=["ps0", ("Abias", 40)], w=["ghb"])
                    P.op("tensor", lambda e, fc=fc, t0=t0: e.matmul(
                        c.ps_r1[:], wb2d_bf[:, fc:fc + 1], ghb[:, t0:t0 + W],
                        start=(fc == 0), stop=(fc == 3)),
                        r=["ghb", "wb2d_bf"], w=["ps_r1"])
                P.op("scalar", lambda e: e.activation(
                    c.rows[0:1, 0:W], c.ps_r1[:], AF.Sigmoid, bias=bias[0:1, 63:64]),
                    r=["ps_r1", ("Abias", 63)], w=["row_mu"])
                if hs == 0 and p == 0 and c.tapf is not None:
                    P.op("sync", lambda e: e.dma_start(out=c.tapf[0:1, 4096:4608], in_=c.rows[0:1, 0:W]),
                         r=["row_mu"], w=["tpw"], dma=True)
                    P.op("sync", lambda e: e.dma_start(out=c.tapf[:, 4608:4610], in_=wb2d_bf[:, :].bitcast(F32)),
                         r=["wb2d_bf"], w=["tpw"], dma=True)
                    P.op("sync", lambda e: e.dma_start(out=c.tapf[:, 4610:4682],
                         in_=bias[:, 0:72]), r=[("Abias", 63), ("Abias", 64)], w=["tpw"], dma=True)
                    P.op("sync", lambda e: e.dma_start(out=c.tapf[:, 4682:4690],
                         in_=wb2sb[:, 0:8]), r=["wb2sb"], w=["tpw"], dma=True)
                P.op("vector", lambda e: e.tensor_copy(growb[0:1, 0:W], c.rows[0:1, 0:W]),
                     r=["row_mu"], w=["growb"])
                P.op("tensor", lambda e: e.matmul(
                    c.ps[0][:], c.ones_r_bf[:], growb[0:1, 0:W],
                    start=True, stop=True), r=["growb", "ones_rb"], w=["ps0"])
                P.op("vector", lambda e, p=p: e.tensor_copy(
                    SL(1)[:, p * W:(p + 1) * W], c.ps[0][:]), r=["ps0"], w=[("bcast", 1)])

            if a_parts < 3:
                return
            c.P.barrier()
            load_weight_bf(c, wstage, c.I["a_Wk"], wsb, "A_wk", 1024)
            load_weight_bf(c, wstage, c.I["a_Wkv"], wsb2, "A_wkv", 1024)
            for ch in range(KC):
                for p in range(NP2):
                    t0 = p * W
                    for k in range(KC):
                        P.op("tensor", lambda e, ch=ch, k=k, t0=t0: e.matmul(
                            c.ps[0][:], wsb[:, k * 1024 + ch * 128:k * 1024 + ch * 128 + 128],
                            lsl(k, t0, W), start=(k == 0), stop=(k == KC - 1)),
                            r=["A_wk", ("lnfm", k, t0 // W)], w=["ps0"])
                    P.op("scalar", lambda e, ch=ch, t0=t0: e.activation(
                        F0[:, t0:t0 + W], c.ps[0][:], AF.Tanh, bias=bias[:, 8 + ch:9 + ch]),
                        r=["ps0", ("Abias", 8)], w=["F0t"])
                    P.op("scalar", lambda e, t0=t0: e.activation(
                        ksb[:, 1 + t0:1 + t0 + W], F0[:, t0:t0 + W], AF.Sin, scale=PI),
                        r=["F0t"], w=["ksb"])
                    P.op("scalar", lambda e, t0=t0: e.activation(
                        F1[:, t0:t0 + W], F0[:, t0:t0 + W], AF.Abs),
                        r=["F0t"], w=["F1t"])
                    P.op("scalar", lambda e, t0=t0: e.activation(
                        kcb[:, 1 + t0:1 + t0 + W], F1[:, t0:t0 + W], AF.Sin,
                        scale=-PI, bias=c.halfpi[:, 0:1]),
                        r=["F1t", "halfpi"], w=["kcb"])
                    for k in range(KC):
                        P.op("tensor", lambda e, ch=ch, k=k, t0=t0: e.matmul(
                            c.ps[1][:], wsb2[:, k * 1024 + ch * 128:k * 1024 + ch * 128 + 128],
                            lsl(k, t0, W), start=(k == 0), stop=(k == KC - 1)),
                            r=["A_wkv", ("lnfm", k, t0 // W)], w=["ps1"])
                    P.op("vector", lambda e, ch=ch, t0=t0: e.tensor_scalar(
                        SL(4)[:, t0:t0 + W], c.ps[1][:], bias[:, 16 + ch:17 + ch], None, ALU.add),
                        r=["ps1", ("Abias", 16)], w=["kvt"])
                if first:
                    P.op("vector", lambda e, ch=ch: e.tensor_copy(
                        kscar[:, ch:ch + 1], kcb[:, S2:S2 + 1]), r=["kcb"], w=["kscar"])
                    P.op("vector", lambda e, ch=ch: e.tensor_copy(
                        kscar[:, 8 + ch:9 + ch], ksb[:, S2:S2 + 1]), r=["ksb"], w=["kscar"])
                else:
                    P.op("vector", lambda e, ch=ch: e.tensor_copy(
                        kcb[:, 0:1], kscar[:, ch:ch + 1]), r=["kscar"], w=["kczero", "kcb"])
                    P.op("vector", lambda e, ch=ch: e.tensor_copy(
                        ksb[:, 0:1], kscar[:, 8 + ch:9 + ch]), r=["kscar"], w=["kszero", "ksb"])
                P.op("vector", lambda e: e.tensor_tensor(SL(4)[:], SL(4)[:], SL(0)[:], ALU.mult),
                     r=["kvt", ("bcast", 0)], w=["kvt"])
                P.op("vector", lambda e: e.tensor_tensor(SL(5)[:], kcb[:, 0:S2], SL(4)[:], ALU.mult),
                     r=["kcb", "kczero", "kvt"], w=["bct"])
                P.op("vector", lambda e: e.tensor_tensor(SL(6)[:], ksb[:, 0:S2], SL(4)[:], ALU.mult),
                     r=["ksb", "kszero", "kvt"], w=["bst"])
                P.op("vector", lambda e, ch=ch: e.tensor_tensor_scan(
                    SL(7)[:], ones_bf2[:], SL(5)[:], carry[:, ch:ch + 1], ALU.mult, ALU.add),
                    r=["bct", "ones2", "carry"], w=["mkc"])
                P.op("vector", lambda e, ch=ch: e.tensor_copy(
                    carry[:, ch:ch + 1], SL(7)[:, S2 - 1:S2]), r=["mkc"], w=["carry"])
                P.op("vector", lambda e, ch=ch: e.tensor_tensor_scan(
                    SL(4)[:], ones_bf2[:], SL(6)[:], carry[:, 8 + ch:9 + ch], ALU.mult, ALU.add),
                    r=["bst", "ones2", "carry"], w=["mks"])
                P.op("vector", lambda e, ch=ch: e.tensor_copy(
                    carry[:, 8 + ch:9 + ch], SL(4)[:, S2 - 1:S2]), r=["mks"], w=["carry"])
                P.op("vector", lambda e: e.tensor_tensor(F1[:], kcb[:, 1:S2 + 1], SL(7)[:], ALU.mult),
                     r=["kcb", "mkc"], w=["F1t"])
                P.op("vector", lambda e: e.tensor_tensor(F2[:], ksb[:, 1:S2 + 1], SL(4)[:], ALU.mult),
                     r=["ksb", "mks"], w=["F2t"])
                P.op("vector", lambda e: e.tensor_tensor(F1[:], F1[:], F2[:], ALU.add),
                     r=["F1t", "F2t"], w=["F1t"])
                P.op("vector", lambda e, ch=ch: e.tensor_tensor(
                    kvr[:, ch * S2:(ch + 1) * S2], F1[:], SL(2)[:], ALU.mult),
                    r=["F1t", ("bcast", 2)], w=[("kvr", ch)])

            if hs == 0 and c.tapb is not None:
                P.op("sync", lambda e: e.dma_start(out=c.tapb[:, 0:8192], in_=kvr[:, :]),
                     r=[("kvr", c2) for c2 in range(KC)], w=["tap1"], dma=True)
            if a_parts < 4:
                return
            c.P.barrier()
            load_weight_bf(c, wstage, c.I["a_Wpv"], wsb, "A_wpv", 1024)
            pbase = (hs % 2) * S2
            for ch in range(KC):
                for p in range(NP2):
                    t0 = p * W
                    for k in range(KC):
                        P.op("tensor", lambda e, ch=ch, k=k, t0=t0: e.matmul(
                            c.ps[0][:], wsb[:, k * 1024 + ch * 128:k * 1024 + ch * 128 + 128],
                            lsl(k, t0, W), start=(k == 0), stop=(k == KC - 1)),
                            r=["A_wpv", ("lnfm", k, t0 // W)], w=["ps0"])
                    P.op("vector", lambda e, ch=ch, t0=t0: e.tensor_scalar(
                        SL(3)[:, t0:t0 + W], c.ps[0][:], bias[:, 0 + ch:1 + ch], None, ALU.add),
                        r=["ps0", ("Abias", 0)], w=["pvt"])
                P.op("sync", lambda e, ch=ch: e.dma_start(
                    out=F0[:, 0:S2], in_=c.I["ppT"][ch * 128:(ch + 1) * 128, pbase:pbase + S2]),
                    w=["F0t"], dma=True)
                P.op("vector", lambda e: e.tensor_scalar(F1[:], F0[:], 1.0 / (2 * PI), None, ALU.mult),
                     r=["F0t"], w=["F1t"])
                P.op("vector", lambda e: e.tensor_copy(i32t[:], F1[:]), r=["F1t"], w=["i32"])
                P.op("vector", lambda e: e.tensor_copy(F1[:], i32t[:]), r=["i32"], w=["F1t"])
                P.op("vector", lambda e: e.scalar_tensor_tensor(
                    F1[:], F1[:], -(2 * PI), F0[:], ALU.mult, ALU.add), r=["F1t", "F0t"], w=["F1t"])
                P.op("scalar", lambda e: e.activation(SL(6)[:], F1[:], AF.Sin), r=["F1t"], w=["pspos"])
                P.op("vector", lambda e: e.tensor_scalar(F2[:], F0[:], PI / 2, None, ALU.add),
                     r=["F0t"], w=["F2t"])
                P.op("vector", lambda e: e.tensor_scalar(F1[:], F2[:], 1.0 / (2 * PI), None, ALU.mult),
                     r=["F2t"], w=["F1t"])
                P.op("vector", lambda e: e.tensor_copy(i32t[:], F1[:]), r=["F1t"], w=["i32"])
                P.op("vector", lambda e: e.tensor_copy(F1[:], i32t[:]), r=["i32"], w=["F1t"])
                P.op("vector", lambda e: e.scalar_tensor_tensor(
                    F1[:], F1[:], -(2 * PI), F2[:], ALU.mult, ALU.add), r=["F1t", "F2t"], w=["F1t"])
                P.op("scalar", lambda e: e.activation(SL(5)[:], F1[:], AF.Sin), r=["F1t"], w=["pcpos"])
                P.op("vector", lambda e: e.tensor_tensor(SL(7)[:], SL(5)[:], SL(3)[:], ALU.mult),
                     r=["pcpos", "pvt"], w=["ac"])
                P.op("vector", lambda e: e.tensor_tensor(SL(4)[:], SL(6)[:], SL(3)[:], ALU.mult),
                     r=["pspos", "pvt"], w=["as"])
                P.op("vector", lambda e, ch=ch: e.tensor_tensor_scan(
                    SL(3)[:], ones_bf2[:], SL(7)[:], carry[:, 16 + ch:17 + ch], ALU.mult, ALU.add),
                    r=["ac", "ones2", "carry"], w=["mpc"])
                P.op("vector", lambda e, ch=ch: e.tensor_copy(
                    carry[:, 16 + ch:17 + ch], SL(3)[:, S2 - 1:S2]), r=["mpc"], w=["carry"])
                P.op("vector", lambda e, ch=ch: e.tensor_tensor_scan(
                    SL(7)[:], ones_bf2[:], SL(4)[:], carry[:, 24 + ch:25 + ch], ALU.mult, ALU.add),
                    r=["as", "ones2", "carry"], w=["mps"])
                P.op("vector", lambda e, ch=ch: e.tensor_copy(
                    carry[:, 24 + ch:25 + ch], SL(7)[:, S2 - 1:S2]), r=["mps"], w=["carry"])
                P.op("vector", lambda e: e.tensor_tensor(F1[:], SL(5)[:], SL(3)[:], ALU.mult),
                     r=["pcpos", "mpc"], w=["F1t"])
                P.op("vector", lambda e: e.tensor_tensor(F2[:], SL(6)[:], SL(7)[:], ALU.mult),
                     r=["pspos", "mps"], w=["F2t"])
                P.op("vector", lambda e: e.tensor_tensor(F1[:], F1[:], F2[:], ALU.add),
                     r=["F1t", "F2t"], w=["F1t"])
                P.op("vector", lambda e: e.tensor_scalar(F1[:], F1[:], 1.0 / 32.0, None, ALU.mult),
                     r=["F1t"], w=["F1t"])
                if hs == 0 and ch == 0 and c.tapf is not None:
                    P.op("sync", lambda e: e.dma_start(out=c.tapf[:, 0:1024], in_=F1[:]),
                         r=["F1t"], w=["tp"], dma=True)
                    P.op("vector", lambda e: e.tensor_copy(F2[:], SL(1)[:]), r=[("bcast", 1)], w=["F2t"])
                    P.op("sync", lambda e: e.dma_start(out=c.tapf[:, 1024:2048], in_=F2[:]),
                         r=["F2t"], w=["tp"], dma=True)
                    P.op("vector", lambda e: e.tensor_copy(F2[:], SL(5)[:]), r=["pcpos"], w=["F2t"])
                    P.op("sync", lambda e: e.dma_start(out=c.tapf[:, 2048:3072], in_=F2[:]),
                         r=["F2t"], w=["tp"], dma=True)
                    P.op("vector", lambda e: e.tensor_copy(F2[:], SL(3)[:]), r=["mpc"], w=["F2t"])
                    P.op("sync", lambda e: e.dma_start(out=c.tapf[:, 3072:4096], in_=F2[:]),
                         r=["F2t", "tp"], w=["tp"], dma=True)
                P.op("vector", lambda e, ch=ch: e.tensor_tensor(
                    F1[:], F1[:], kvr[:, ch * S2:(ch + 1) * S2], ALU.subtract),
                    r=["F1t", ("kvr", ch)], w=["F1t"])
                P.op("vector", lambda e: e.tensor_tensor(F1[:], F1[:], SL(1)[:], ALU.mult),
                     r=["F1t", ("bcast", 1)], w=["F1t"])
                P.op("vector", lambda e, ch=ch: e.tensor_tensor(
                    kvr[:, ch * S2:(ch + 1) * S2], F1[:], kvr[:, ch * S2:(ch + 1) * S2], ALU.add),
                    r=["F1t", ("kvr", ch)], w=[("kvr", ch)])

            if hs == 0 and c.tapb is not None:
                P.op("sync", lambda e: e.dma_start(out=c.tapb[:, 8192:16384], in_=kvr[:, :]),
                     r=[("kvr", c2) for c2 in range(KC)], w=["tap2"], dma=True)
            if a_parts < 5:
                return
            c.P.barrier()
            for p in range(NP2):
                t0 = p * W
                for ch in range(KC):
                    P.op("scalar", lambda e, ch=ch, t0=t0: e.activation(
                        SL(5)[:, t0:t0 + W], kvr[:, ch * S2 + t0:ch * S2 + t0 + W], AF.Square),
                        r=[("kvr", ch)], w=["sqb"])
                    P.op("vector", lambda e, t0=t0: e.tensor_copy(
                        SL(6)[:, t0:t0 + W], SL(5)[:, t0:t0 + W]),
                        r=["sqb"], w=["sqc"])
                    P.op("tensor", lambda e, ch=ch, t0=t0: e.matmul(
                        c.ps_r1[:], c.ones_c_bf[:], kvr[:, ch * S2 + t0:ch * S2 + t0 + W],
                        start=(ch == 0), stop=(ch == KC - 1)),
                        r=[("kvr", ch), "ones_c"], w=["ps_r1"])
                    P.op("tensor", lambda e, ch=ch, t0=t0: e.matmul(
                        c.ps_r2[:], c.ones_c_bf[:], SL(6)[:, t0:t0 + W],
                        start=(ch == 0), stop=(ch == KC - 1)),
                        r=["sqc", "ones_c"], w=["ps_r2"])
                R = c.rows
                P.op("vector", lambda e: e.tensor_scalar(R[:, 0:W], c.ps_r1[:], 1.0 / D, None, ALU.mult),
                     r=["ps_r1"], w=["row_mu"])
                P.op("vector", lambda e: e.tensor_scalar(R[:, W:2 * W], c.ps_r2[:], 1.0 / D, None, ALU.mult),
                     r=["ps_r2"], w=["row_msq"])
                P.op("vector", lambda e: e.tensor_tensor(R[:, 2 * W:3 * W], R[:, 0:W], R[:, 0:W], ALU.mult),
                     r=["row_mu"], w=["row_v"])
                P.op("vector", lambda e: e.tensor_tensor(R[:, W:2 * W], R[:, W:2 * W], R[:, 2 * W:3 * W], ALU.subtract),
                     r=["row_msq", "row_v"], w=["row_msq"])
                P.op("vector", lambda e: e.tensor_scalar(R[:, W:2 * W], R[:, W:2 * W], EPS, None, ALU.add),
                     r=["row_msq"], w=["row_msq"])
                P.op("scalar", lambda e: e.activation(R[:, 2 * W:3 * W], R[:, W:2 * W], AF.Sqrt),
                     r=["row_msq"], w=["row_v"])
                P.op("vector", lambda e: e.reciprocal(R[:, 3 * W:4 * W], R[:, 2 * W:3 * W]),
                     r=["row_v"], w=["row_rstd"])
                P.op("tensor", lambda e: e.matmul(c.ps[3][:], c.ones_r_f[:], R[0:1, 0:W], start=True, stop=True),
                     r=["row_mu", "ones_rf"], w=["ps3"])
                P.op("tensor", lambda e: e.matmul(c.ps[4][:], c.ones_r_f[:], R[0:1, 3 * W:4 * W], start=True, stop=True),
                     r=["row_rstd", "ones_rf"], w=["ps4"])
                for ch in range(KC):
                    P.op("vector", lambda e, ch=ch, t0=t0: e.tensor_tensor(
                        F0[:, t0:t0 + W], kvr[:, ch * S2 + t0:ch * S2 + t0 + W], c.ps[3][:], ALU.subtract),
                        r=[("kvr", ch), "ps3"], w=["F0t"])
                    P.op("vector", lambda e, t0=t0: e.tensor_tensor(
                        F0[:, t0:t0 + W], F0[:, t0:t0 + W], c.ps[4][:], ALU.mult),
                        r=["F0t", "ps4"], w=["F0t"])
                    P.op("vector", lambda e, ch=ch, t0=t0: e.tensor_scalar(
                        F0[:, t0:t0 + W], F0[:, t0:t0 + W], bias[:, 44 + ch:45 + ch],
                        None, ALU.mult),
                        r=["F0t", ("Abias", 44)], w=["F0t"])
                    P.op("vector", lambda e, ch=ch, t0=t0: e.tensor_scalar(
                        kvr[:, ch * S2 + t0:ch * S2 + t0 + W], F0[:, t0:t0 + W],
                        bias[:, 52 + ch:53 + ch], None, ALU.add),
                        r=["F0t", ("Abias", 52)], w=[("kvr", ch)])

            if hs == 0 and c.tapb is not None:
                P.op("sync", lambda e: e.dma_start(out=c.tapb[:, 16384:24576], in_=kvr[:, :]),
                     r=[("kvr", c2) for c2 in range(KC)], w=["tap3"], dma=True)
            if a_parts < 6:
                return
            c.P.barrier()
            with pan_scope(c) as pan:
                load_weight_bf(c, wstage, c.I["a_Wo"], wsb, "A_wo", 1024)
                if hs == 0 and c.tapf is not None:
                    P.op("sync", lambda e: e.dma_start(
                        out=c.tapf[:, 1536:1792], in_=wsb[:, 0:512].bitcast(F32)),
                        r=["A_wo"], w=["tap5"], dma=True)
                for p in range(NP2):
                    t0 = p * W
                    col0 = base + t0
                    for oc in range(KC):
                        for k in range(KC):
                            P.op("tensor", lambda e, oc=oc, k=k, t0=t0: e.matmul(
                                c.ps[0][:], wsb[:, k * 1024 + oc * 128:k * 1024 + oc * 128 + 128],
                                kvr[:, k * S2 + t0:k * S2 + t0 + W],
                                start=(k == 0), stop=(k == KC - 1)),
                                r=["A_wo", ("kvr", k)], w=["ps0"])
                        P.op("sync", lambda e, oc=oc, col0=col0: e.dma_start(
                            out=wstage[:, 0:W],
                            in_=c.ln1d[oc * 128:(oc + 1) * 128, col0:col0 + W]),
                            r=[("lnst", oc)], w=["wstage"], dma=True)
                        P.op("vector", lambda e, oc=oc: e.tensor_scalar(
                            wstage[:, W:2 * W], c.ps[0][:], bias[:, 32 + oc:33 + oc], None, ALU.add),
                            r=["ps0", ("Abias", 32)], w=["a1t"])
                        P.op("vector", lambda e, oc=oc: e.tensor_tensor(
                            pan.x[:, oc * W:(oc + 1) * W], wstage[:, W:2 * W], wstage[:, 0:W], ALU.add),
                            r=["a1t", "wstage"], w=[("panx", oc)])
                    emit_ln_core(c, pan)
                    for ch in range(KC):
                        P.op("sync", lambda e, ch=ch, col0=col0: e.dma_start(
                            out=pan.x[:, ch * W:(ch + 1) * W],
                            in_=c.x_fm[ch * 128:(ch + 1) * 128, col0:col0 + W]),
                            w=[("panx", ch)], dma=True)
                        P.op("vector", lambda e, ch=ch: e.tensor_tensor(
                            pan.l[:, ch * W:(ch + 1) * W], pan.l[:, ch * W:(ch + 1) * W],
                            pan.x[:, ch * W:(ch + 1) * W], ALU.add),
                            r=[("panx", ch), ("panl", ch)], w=[("panl", ch)])
                        P.op("sync", lambda e, ch=ch, col0=col0: e.dma_start(
                            out=c.x1d[ch * 128:(ch + 1) * 128, col0:col0 + W],
                            in_=pan.l[:, ch * W:(ch + 1) * W]),
                            r=[("panl", ch)], w=[("x1st", ch)], dma=True)
            c.P.barrier()


        for hs in range(4):
            _half(hs)


def emit_phaseB(c):
    nc, P = c.nc, c.P
    import contextlib
    NPQ = 2
    def _tq(tq):
        seq, half = tq // 2, tq % 2
        with contextlib.ExitStack() as st:
            ent = st.enter_context
            lnq = ent(_sb(nc, "B_lnq", [128, KC * 1024], BF16))
            routed = ent(_sb(nc, "B_routed", [128, KC * 1024], F32))
            gwsb = ent(_sb(nc, "B_gw", [128, KC * 8], F32))
            b2sb = ent(_sb(nc, "B_b2", [8, D], F32))
            gbb = ent(_sb(nc, "B_gbb", [128, 8], F32))
            bias = ent(_sb(nc, "B_bias", [128, 72], F32))
            lgsb = ent(_sb(nc, "B_lg", [128, 48], F32))
            cwrow = ent(_sb(nc, "B_cwrow", [1, 1024], F32))
            cw8 = ent(_sb(nc, "B_cw8", [8, 128], F32))

            P.op("sync", lambda e: e.dma_start(out=gwsb[:, 0:64], in_=c.I["f_gwT"]),
                 w=["gw"], dma=True)
            P.op("sync", lambda e: e.dma_start(out=b2sb[:], in_=c.I["f_b2"]), w=["b2sb"], dma=True)
            P.op("sync", lambda e: e.dma_start(out=bias[:, 32:72], in_=c.I["f_sbias"]),
                 w=[("Bbias", 32), ("Bbias", 48), ("Bbias", 64)], dma=True)
            gbrow = ent(_sb(nc, "B_gbrow", [1, 8], F32))
            P.op("sync", lambda e: e.dma_start(out=gbrow[:], in_=c.I["f_gb"]),
                 w=["gbrow"], dma=True)
            P.op("tensor", lambda e: e.matmul(c.ps[0][:, 0:8], c.ones_r_f[:], gbrow[0:1, 0:8],
                                              start=True, stop=True),
                 r=["gbrow", "ones_rf"], w=["ps0"])
            P.op("vector", lambda e: e.tensor_copy(gbb[:], c.ps[0][:, 0:8]), r=["ps0"], w=["gbb"])

            def router_cb(pcol, pan):
                for sub in range(4):
                    for k in range(KC):
                        P.op("tensor", lambda e, sub=sub, k=k: e.matmul(
                            c.ps[0][:, 0:8], pan.l[:, k * W + sub * 128:k * W + sub * 128 + 128],
                            gwsb[:, k * 8:(k + 1) * 8], start=(k == 0), stop=(k == KC - 1)),
                            r=[("panl", k), "gw"], w=["ps0"])
                    sub_c = lambda i: lgsb[:, i * 8:(i + 1) * 8]
                    P.op("vector", lambda e: e.tensor_copy(sub_c(0), c.ps[0][:, 0:8]),
                         r=["ps0"], w=["lg0"])
                    P.op("vector", lambda e: e.reduce_max(lgsb[:, 40:41], sub_c(0), axis=AX.X),
                         r=["lg0"], w=["lgmx"])
                    P.op("vector", lambda e: e.tensor_scalar(
                        sub_c(1), sub_c(0), lgsb[:, 40:41], None, ALU.subtract),
                        r=["lg0", "lgmx"], w=["lg1"])
                    P.op("scalar", lambda e: e.activation(sub_c(1), sub_c(1), AF.Exp),
                         r=["lg1"], w=["lg1"])
                    P.op("vector", lambda e: e.reduce_sum(lgsb[:, 41:42], sub_c(1), axis=AX.X),
                         r=["lg1"], w=["lgsum"])
                    P.op("vector", lambda e: e.reciprocal(lgsb[:, 42:43], lgsb[:, 41:42]),
                         r=["lgsum"], w=["lginv"])
                    P.op("vector", lambda e: e.tensor_scalar(
                        sub_c(1), sub_c(1), lgsb[:, 42:43], None, ALU.mult),
                        r=["lg1", "lginv"], w=["lg1"])
                    P.op("vector", lambda e: e.tensor_tensor(sub_c(2), sub_c(1), gbb[:], ALU.add),
                         r=["lg1", "gbb"], w=["lg2"])
                    P.op("vector", lambda e: e.max(out=lgsb[:, 24:32], in_=sub_c(2)),
                         r=["lg2"], w=["lgm8"])
                    P.op("vector", lambda e: e.tensor_scalar(
                        sub_c(2), sub_c(2), lgsb[:, 25:26], None, ALU.is_ge),
                        r=["lg2", "lgm8"], w=["lg2"])
                    P.op("vector", lambda e: e.tensor_tensor(sub_c(2), sub_c(2), sub_c(1), ALU.mult),
                         r=["lg2", "lg1"], w=["lg2"])
                    P.op("tensor", lambda e: e.transpose(
                        c.ps[1][0:8, 0:128], sub_c(2), c.iden_f[:]),
                        r=["lg2", "iden_f"], w=["ps1"])
                    P.op("vector", lambda e: e.tensor_copy(cw8[0:8, :], c.ps[1][0:8, 0:128]),
                         r=["ps1"], w=["cw8"])
                    col = pcol + sub * 128
                    P.op("sync", lambda e, col=col: e.dma_start(
                        out=c.cwdT[:, col:col + 128], in_=cw8[0:8, :]),
                        r=["cw8"], w=["cwst"], dma=True)

            with pan_scope(c) as panB:
                for p in range(NPQ):
                    emit_ln_panel(c, panB, c.x1d, tq * 1024 + p * W, lnfm=None,
                                  cb=router_cb, cb_arg=tq * 1024 + p * W)
                    for ch in range(KC):
                        P.op("vector", lambda e, ch=ch, p=p, pl=panB.l: e.tensor_copy(
                            lnq[:, ch * 1024 + p * W: ch * 1024 + p * W + W],
                            pl[:, ch * W:(ch + 1) * W]),
                            r=[("panl", ch)], w=[("lnq", ch, p)])

            xst = contextlib.ExitStack()
            w1sb = xst.enter_context(_sb(nc, "B_w1", [128, KC * 2048], BF16))
            w3sb = xst.enter_context(_sb(nc, "B_w3", [128, KC * 2048], BF16))
            w2sb = xst.enter_context(_sb(nc, "B_w2", [128, KC * 1024], BF16))
            sa = xst.enter_context(_sb(nc, "B_sa", [128, 8 * W], BF16))
            hsl = xst.enter_context(_sb(nc, "B_h", [128, 6 * W], F32))
            for ex in range(9):
                shared = ex == 8
                if shared:
                    w1src, w3src, w2src = c.sw1b, c.sw3b, c.sw2b
                    b1c, b3c = 32, 48
                else:
                    w1src, w3src, w2src = c.w1b[ex], c.w3b[ex], c.w2b[ex]
                    b1c, b3c = 0, 16
                    P.op("sync", lambda e, ex=ex: e.dma_start(
                        out=bias[:, 0:32], in_=c.I["f_b13p"][ex]),
                        w=[("Bbias", 0), ("Bbias", 16)], dma=True)
                for k in range(KC):
                    P.op("sync", lambda e, s=w1src, k=k: e.dma_start(
                        out=w1sb[:, k * 2048:(k + 1) * 2048], in_=s[k * 128:(k + 1) * 128, :]),
                        w=[("w1sb", k)], dma=True)
                    P.op("sync", lambda e, s=w3src, k=k: e.dma_start(
                        out=w3sb[:, k * 2048:(k + 1) * 2048], in_=s[k * 128:(k + 1) * 128, :]),
                        w=[("w3sb", k)], dma=True)
                    P.op("sync", lambda e, s=w2src, k=k: e.dma_start(
                        out=w2sb[:, k * 1024:(k + 1) * 1024], in_=s[k * 128:(k + 1) * 128, :]),
                        w=[("w2sb", k)], dma=True)
                if not shared:
                    P.op("sync", lambda e, ex=ex: e.dma_start(
                        out=cwrow[0:1, :], in_=c.cwdT[ex:ex + 1, tq * 1024:(tq + 1) * 1024]),
                        r=["cwst"], w=["cwrow"], dma=True)
                for p in range(NPQ):
                    t0 = p * W
                    for fg in range(8):
                        specs = [(w1sb, fg, 0, ("w1sb",), c.ps[0], "ps0"),
                                 (w1sb, fg + 8, 1, ("w1sb",), c.ps[1], "ps1"),
                                 (w3sb, fg, 2, ("w3sb",), c.ps[2], "ps2"),
                                 (w3sb, fg + 8, 3, ("w3sb",), c.ps[3], "ps3")]
                        for wsb_, fc, hi, wkey, psx, pskey in specs:
                            for k in range(KC):
                                P.op("tensor", lambda e, wsb_=wsb_, fc=fc, k=k, t0=t0, psx=psx: e.matmul(
                                    psx[:], wsb_[:, k * 2048 + fc * 128:k * 2048 + fc * 128 + 128],
                                    lnq[:, k * 1024 + t0:k * 1024 + t0 + W],
                                    start=(k == 0), stop=(k == KC - 1)),
                                    r=[(wkey[0], k), ("lnq", k, p)], w=[pskey])
                        hx = lambda i: hsl[:, i * W:(i + 1) * W]
                        P.op("vector", lambda e, fg=fg, b1c=b1c: e.tensor_scalar(
                            hx(0), c.ps[0][:], bias[:, b1c + fg:b1c + fg + 1], None, ALU.add),
                            r=["ps0", ("Bbias", b1c)], w=["h0"])
                        P.op("vector", lambda e, fg=fg, b1c=b1c: e.tensor_scalar(
                            hx(1), c.ps[1][:], bias[:, b1c + 8 + fg:b1c + 9 + fg], None, ALU.add),
                            r=["ps1", ("Bbias", b1c)], w=["h1"])
                        P.op("vector", lambda e, fg=fg, b3c=b3c: e.tensor_scalar(
                            hx(2), c.ps[2][:], bias[:, b3c + fg:b3c + fg + 1], None, ALU.add),
                            r=["ps2", ("Bbias", b3c)], w=["h2"])
                        P.op("vector", lambda e, fg=fg, b3c=b3c: e.tensor_scalar(
                            hx(3), c.ps[3][:], bias[:, b3c + 8 + fg:b3c + 9 + fg], None, ALU.add),
                            r=["ps3", ("Bbias", b3c)], w=["h3"])
                        P.op("vector", lambda e: e.tensor_tensor(hx(0), hx(0), hx(2), ALU.mult),
                             r=["h0", "h2"], w=["h0"])
                        P.op("vector", lambda e: e.tensor_tensor(hx(1), hx(1), hx(3), ALU.mult),
                             r=["h1", "h3"], w=["h1"])
                        P.op("vector", lambda e: e.tensor_scalar(hx(0), hx(0), 7.0, None, ALU.min),
                             r=["h0"], w=["h0"])
                        P.op("scalar", lambda e: e.activation(hx(4), hx(0), AF.Sigmoid, scale=1.702),
                             r=["h0"], w=["h4"])
                        P.op("vector", lambda e: e.tensor_tensor(hx(0), hx(0), hx(4), ALU.mult),
                             r=["h0", "h4"], w=["h0"])
                        P.op("vector", lambda e: e.tensor_scalar(
                            hx(1), hx(1), 7.0, -7.0, ALU.min, ALU.max),
                            r=["h1"], w=["h1"])
                        P.op("vector", lambda e: e.tensor_scalar(hx(1), hx(1), 1.0, None, ALU.add),
                             r=["h1"], w=["h1"])
                        P.op("vector", lambda e: e.tensor_tensor(hx(0), hx(0), hx(1), ALU.mult),
                             r=["h0", "h1"], w=["h0"])
                        if shared:
                            P.op("vector", lambda e, fg=fg: e.tensor_copy(
                                sa[:, fg * W:(fg + 1) * W], hx(0)), r=["h0"], w=[("sa", fg)])
                        else:
                            P.op("tensor", lambda e, t0=t0: e.matmul(
                                c.ps[4][:], c.ones_r_f[:], cwrow[0:1, t0:t0 + W],
                                start=True, stop=True),
                                r=["cwrow", "ones_rf"], w=["ps4"])
                            P.op("vector", lambda e, fg=fg: e.tensor_tensor(
                                sa[:, fg * W:(fg + 1) * W], hx(0), c.ps[4][:], ALU.mult),
                                r=["h0", "ps4"], w=[("sa", fg)])
                    for dc in range(KC):
                        for fg in range(8):
                            P.op("tensor", lambda e, dc=dc, fg=fg: e.matmul(
                                c.ps[0][:], w2sb[:, fg * 1024 + dc * 128:fg * 1024 + dc * 128 + 128],
                                sa[:, fg * W:(fg + 1) * W],
                                start=(fg == 0), stop=(fg == 7)),
                                r=[("w2sb", fg), ("sa", fg)], w=["ps0"])
                        if ex == 0:
                            P.op("vector", lambda e, dc=dc, t0=t0: e.tensor_copy(
                                routed[:, dc * 1024 + t0:dc * 1024 + t0 + W], c.ps[0][:]),
                                r=["ps0"], w=[("routed", dc, p)])
                        else:
                            P.op("vector", lambda e, dc=dc, t0=t0: e.tensor_tensor(
                                routed[:, dc * 1024 + t0:dc * 1024 + t0 + W],
                                routed[:, dc * 1024 + t0:dc * 1024 + t0 + W], c.ps[0][:], ALU.add),
                                r=["ps0", ("routed", dc, p)], w=[("routed", dc, p)])

            xst.close()
            P.barrier()
            with pan_scope(c) as panC:
                emit_B_closing(c, panC, tq, routed, b2sb, cw8, bias)
            P.barrier()



    for tq in range(4):
        _tq(tq)

def emit_B_closing(c, pan, tq, routed, b2sb, cw8, bias):
    nc, P = c.nc, c.P
    seq, half = tq // 2, tq % 2
    NPQ = 2
    with _sb(nc, "B_cwq", [8, W], F32) as cwq:
        if True:
            for p in range(NPQ):
                t0 = p * W
                col0 = seq * S + half * 1024 + t0
                for dc in range(KC):
                    P.op("sync", lambda e, p=p, t0=t0: e.dma_start(
                        out=cwq[0:8, 0:W], in_=c.cwdT[0:8, tq * 1024 + t0: tq * 1024 + t0 + W]),
                        r=["cwst"], w=["cwq"], dma=True) if dc == 0 else None
                    P.op("tensor", lambda e, dc=dc, p=p: e.matmul(
                        c.ps[1][:], b2sb[0:8, dc * 128:(dc + 1) * 128], cwq[0:8, 0:W],
                        start=True, stop=True),
                        r=["b2sb", "cwq"], w=["ps1"])
                    P.op("vector", lambda e, dc=dc, t0=t0: e.scalar_tensor_tensor(
                        pan.x[:, dc * W:(dc + 1) * W],
                        routed[:, dc * 1024 + t0:dc * 1024 + t0 + W],
                        bias[:, 64 + dc:65 + dc], c.ps[1][:], ALU.add, ALU.add),
                        r=[("routed", dc, p), ("Bbias", 64), "ps1"], w=[("panx", dc)])
                emit_ln_core(c, pan)
                for ch in range(KC):
                    P.op("sync", lambda e, ch=ch, col0=col0: e.dma_start(
                        out=pan.x[:, ch * W:(ch + 1) * W],
                        in_=c.x1d[ch * 128:(ch + 1) * 128, col0:col0 + W]),
                        w=[("panx", ch)], dma=True)
                    P.op("vector", lambda e, ch=ch: e.tensor_tensor(
                        pan.l[:, ch * W:(ch + 1) * W], pan.l[:, ch * W:(ch + 1) * W],
                        pan.x[:, ch * W:(ch + 1) * W], ALU.add),
                        r=[("panx", ch), ("panl", ch)], w=[("panl", ch)])
                    P.op("sync", lambda e, ch=ch, col0=col0: e.dma_start(
                        out=c.x2d[ch * 128:(ch + 1) * 128, col0:col0 + W],
                        in_=pan.l[:, ch * W:(ch + 1) * W]),
                        r=[("panl", ch)], w=[("x2st", ch)], dma=True)



def emit_phaseC(c):
    nc, P = c.nc, c.P
    import contextlib
    def _seq(seq):
        with contextlib.ExitStack() as st:
            ent = st.enter_context
            lnfm = ent(_sb(nc, "C_lnfm", [128, KC * (S + 1)], BF16))
            qfm = ent(_sb(nc, "C_q", [128, KC * S], BF16))
            kfm = ent(_sb(nc, "C_k", [128, KC * S], BF16))
            vaug = ent(_sb(nc, "C_vaug", [128, 16 * 1040], BF16))
            bias = ent(_sb(nc, "C_bias", [128, 24], F32))
            sqb = ent(_sb(nc, "C_sqb", [128, W], BF16))
            maxn = ent(_sb(nc, "C_maxn", [1, 64], F32))
            cb_t = ent(_sb(nc, "C_cb", [128, 16], F32))
            pb = ent(_sb(nc, "C_pb", [128, 2 * W], BF16))
            ctxs = ent(_sb(nc, "C_ctx", [128, 2 * W], F32))
            inv_r = ent(_sb(nc, "C_inv", [1, 2 * W], F32))

            P.op("sync", lambda e: e.dma_start(out=bias[:, 0:24], in_=c.I["c_bias"]),
                 w=[("Cbias", 0), ("Cbias", 8), ("Cbias", 16)], dma=True)
            for ch in range(KC):
                P.op("vector", lambda e, ch=ch: e.memset(lnfm[:, ch * (S + 1):ch * (S + 1) + 1], 0.0),
                     w=[("lnfm", ch, "z")])
            with pan_scope(c) as panL:
                for p in range(NP):
                    emit_ln_panel(c, panL, c.x2d, seq * S + p * W, lnfm=lnfm,
                                  lnfm_stride=S + 1, lnfm_off=1 + p * W)

            def lsl(k, t0, w_):
                return lnfm[:, k * (S + 1) + 1 + t0: k * (S + 1) + 1 + t0 + w_]

            for kt in range(16):
                P.op("vector", lambda e, kt=kt: e.memset(
                    vaug[:, kt * 1040 + 64: kt * 1040 + 16 * 65: 65], 1.0),
                    w=[("vaug_ones", kt)])

            xstC = contextlib.ExitStack()
            wsb = xstC.enter_context(_sb(nc, "C_wsb", [128, KC * 1024], BF16))
            wstage = xstC.enter_context(_sb(nc, "C_wstage", [128, 1024], F32))
            for wi, (wnm, bcol, dest) in enumerate((("c_Wq", 0, "q"), ("c_Wk", 8, "k"), ("c_Wv", 16, "v"))):
                load_weight_bf(c, wstage, c.I[wnm], wsb, f"C_w{wi}", 1024)
                for oc in range(KC):
                    for p in range(NP):
                        t0 = p * W
                        for k in range(KC):
                            P.op("tensor", lambda e, oc=oc, k=k, t0=t0, wi=wi: e.matmul(
                                c.ps[0][:], wsb[:, k * 1024 + oc * 128:k * 1024 + oc * 128 + 128],
                                lsl(k, t0, W), start=(k == 0), stop=(k == KC - 1)),
                                r=[f"C_w{wi}", ("lnfm", k, t0 // W)], w=["ps0"])
                        if dest == "q":
                            P.op("vector", lambda e, oc=oc, t0=t0, bcol=bcol: e.tensor_scalar(
                                qfm[:, oc * S + t0: oc * S + t0 + W], c.ps[0][:],
                                bias[:, bcol + oc:bcol + oc + 1], None, ALU.add),
                                r=["ps0", ("Cbias", bcol)], w=[("qfm", oc)])
                        elif dest == "k":
                            P.op("vector", lambda e, oc=oc, t0=t0, bcol=bcol: e.tensor_scalar(
                                kfm[:, oc * S + t0: oc * S + t0 + W], c.ps[0][:],
                                bias[:, bcol + oc:bcol + oc + 1], None, ALU.add),
                                r=["ps0", ("Cbias", bcol)], w=[("kfm", oc)])
                        else:
                            P.op("vector", lambda e, oc=oc, t0=t0, bcol=bcol: e.tensor_scalar(
                                pb[:, 0:W], c.ps[0][:], bias[:, bcol + oc:bcol + oc + 1],
                                None, ALU.add),
                                r=["ps0", ("Cbias", bcol)], w=["pb0"])
                            for sub in range(W // 128):
                                kt = (t0 + sub * 128) // 128
                                P.op("tensor", lambda e, sub=sub: e.transpose(
                                    c.ps_tr[:, 0:128], pb[:, sub * 128:(sub + 1) * 128], c.iden_bf[:]),
                                    r=["pb0", "iden_b"], w=["ps_tr"])
                                for hh in range(2):
                                    h = 2 * oc + hh
                                    P.op("vector", lambda e, kt=kt, h=h, hh=hh: e.tensor_copy(
                                        vaug[:, kt * 1040 + h * 65: kt * 1040 + h * 65 + 64],
                                        c.ps_tr[:, hh * 64:hh * 64 + 64]),
                                        r=["ps_tr"], w=[("vaug", kt, h)])

            xstC.close()
            P.barrier()
            if seq == 0 and c.tapf is not None:
                P.op("sync", lambda e: e.dma_start(out=c.tapf[:, 0:1024],
                     in_=qfm[:, 0:2048].bitcast(F32)), r=[("qfm", 0)], w=["tpc"], dma=True)
                P.op("sync", lambda e: e.dma_start(out=c.tapf[:, 1024:2048],
                     in_=kfm[:, 0:2048].bitcast(F32)), r=[("kfm", 0)], w=["tpc"], dma=True)
                P.op("sync", lambda e: e.dma_start(out=c.tapf[:, 2048:2568],
                     in_=vaug[:, 0:1040].bitcast(F32)),
                     r=[("vaug", 0, 0), ("vaug", 0, 1), ("vaug_ones", 0)], w=["tpc"], dma=True)
            for which, fmref, fmkey, mcol in (("q", qfm, "qfm", 0), ("k", kfm, "kfm", 16)):
                for oc in range(KC):
                    for p in range(NP):
                        t0 = p * W
                        P.op("scalar", lambda e, oc=oc, t0=t0, fmref=fmref: e.activation(
                            sqb[:, 0:W], fmref[:, oc * S + t0:oc * S + t0 + W], AF.Square),
                            r=[(fmkey, oc)], w=["sqb"])
                        for hh in range(2):
                            P.op("tensor", lambda e, hh=hh: e.matmul(
                                c.ps_r1[:], c.ones_c_bf[hh * 64:hh * 64 + 64, 0:1],
                                sqb[hh * 64:hh * 64 + 64, 0:W], start=True, stop=True),
                                r=["sqb", "ones_c"], w=["ps_r1"])
                            P.op("vector", lambda e: e.reduce_max(
                                maxn[0:1, 32:33], c.ps_r1[:], axis=AX.X),
                                r=["ps_r1"], w=["mxs"])
                            h = 2 * oc + hh
                            if p == 0:
                                P.op("vector", lambda e, h=h, mcol=mcol: e.tensor_copy(
                                    maxn[0:1, mcol + h:mcol + h + 1], maxn[0:1, 32:33]),
                                    r=["mxs"], w=[("maxn", mcol + h)])
                            else:
                                P.op("vector", lambda e, h=h, mcol=mcol: e.tensor_tensor(
                                    maxn[0:1, mcol + h:mcol + h + 1],
                                    maxn[0:1, mcol + h:mcol + h + 1], maxn[0:1, 32:33], ALU.max),
                                    r=["mxs", ("maxn", mcol + h)], w=[("maxn", mcol + h)])
            for h in range(16):
                P.op("vector", lambda e, h=h: e.tensor_tensor(
                    maxn[0:1, 32 + h:33 + h], maxn[0:1, h:h + 1], maxn[0:1, 16 + h:17 + h], ALU.mult),
                    r=[("maxn", h), ("maxn", 16 + h)], w=[("csc", h)])
                P.op("scalar", lambda e, h=h: e.activation(
                    maxn[0:1, 32 + h:33 + h], maxn[0:1, 32 + h:33 + h], AF.Sqrt),
                    r=[("csc", h)], w=[("csc", h)])
                P.op("vector", lambda e, h=h: e.tensor_scalar(
                    maxn[0:1, 32 + h:33 + h], maxn[0:1, 32 + h:33 + h], -0.125, None, ALU.mult),
                    r=[("csc", h)], w=[("csc", h)])
                P.op("tensor", lambda e, h=h: e.matmul(
                    c.ps[0][:, 0:1], c.ones_r_f[:], maxn[0:1, 32 + h:33 + h], start=True, stop=True),
                    r=[("csc", h), "ones_rf"], w=["ps0"])
                P.op("vector", lambda e, h=h: e.tensor_copy(cb_t[:, h:h + 1], c.ps[0][:, 0:1]),
                     r=["ps0"], w=[("cb", h)])

            if seq == 0 and c.tapf is not None:
                P.op("sync", lambda e: e.dma_start(out=c.tapf[:, 2568:2584], in_=cb_t[:, 0:16]),
                     r=[("cb", h3) for h3 in range(16)], w=["tpc"], dma=True)
                P.op("sync", lambda e: e.dma_start(out=c.tapf[0:1, 2584:2648], in_=maxn[0:1, 0:64]),
                     r=[("maxn", m3) for m3 in range(32)] + [("csc", h3) for h3 in range(16)], w=["tpc"], dma=True)
            for hp in range(KC):
                h0, h1 = 2 * hp, 2 * hp + 1
                for qp in range(NP):
                    q0 = qp * W
                    for kc_i in range(16):
                        kt0 = kc_i * 128
                        P.op("tensor", lambda e, hp=hp, kt0=kt0, q0=q0: e.matmul(
                            c.ps[0][:], kfm[0:64, hp * S + kt0: hp * S + kt0 + 128],
                            qfm[0:64, hp * S + q0: hp * S + q0 + W],
                            start=True, stop=True),
                            r=[("kfm", hp), ("qfm", hp)], w=["ps0"])
                        P.op("tensor", lambda e, hp=hp, kt0=kt0, q0=q0: e.matmul(
                            c.ps[1][:], kfm[64:128, hp * S + kt0: hp * S + kt0 + 128],
                            qfm[64:128, hp * S + q0: hp * S + q0 + W],
                            start=True, stop=True, tile_position=(64, 0)),
                            r=[("kfm", hp), ("qfm", hp)], w=["ps1"])
                        P.op("scalar", lambda e, h0=h0: e.activation(
                            pb[:, 0:W], c.ps[0][:], AF.Exp, bias=cb_t[:, h0:h0 + 1], scale=0.125),
                            r=["ps0", ("cb", h0)], w=["pb0"])
                        P.op("scalar", lambda e, h1=h1: e.activation(
                            pb[:, W:2 * W], c.ps[1][:], AF.Exp, bias=cb_t[:, h1:h1 + 1], scale=0.125),
                            r=["ps1", ("cb", h1)], w=["pb1"])
                        if seq == 0 and hp == 0 and qp == 0 and c.probd is not None:
                            P.op("sync", lambda e, kc_i=kc_i: e.dma_start(
                                out=c.probd[:, kc_i * 512:(kc_i + 1) * 512], in_=pb[:, 0:W]),
                                r=["pb0"], w=["tpp"], dma=True)
                        P.op("tensor", lambda e, kc_i=kc_i, h0=h0: e.matmul(
                            c.ps[2][0:65, :], vaug[:, kc_i * 1040 + h0 * 65: kc_i * 1040 + h0 * 65 + 65],
                            pb[:, 0:W], start=(kc_i == 0), stop=(kc_i == 15)),
                            r=["pb0", ("vaug", kc_i, h0), ("vaug_ones", kc_i), "tpp"], w=["ps2"])
                        P.op("tensor", lambda e, kc_i=kc_i, h1=h1: e.matmul(
                            c.ps[3][0:65, :], vaug[:, kc_i * 1040 + h1 * 65: kc_i * 1040 + h1 * 65 + 65],
                            pb[:, W:2 * W], start=(kc_i == 0), stop=(kc_i == 15)),
                            r=["pb1", ("vaug", kc_i, h1), ("vaug_ones", kc_i)], w=["ps3"])
                    if seq == 0 and hp == 0 and qp == 0 and c.tapf is not None:
                        P.op("sync", lambda e: e.dma_start(out=c.tapf[:, 2648:2904],
                             in_=pb[:, 0:W].bitcast(F32)), r=["pb0"], w=["tpm"], dma=True)
                    for hh, psx, pskey in ((0, c.ps[2], "ps2"), (1, c.ps[3], "ps3")):
                        P.op("vector", lambda e, psx=psx, hh=hh: e.reciprocal(
                            inv_r[0:1, hh * W:(hh + 1) * W], psx[64:65, :]),
                            r=[pskey], w=[("inv", hh)])
                        P.op("tensor", lambda e, hh=hh: e.matmul(
                            c.ps[4][0:64, :], c.ones_r_f[0:1, 0:64],
                            inv_r[0:1, hh * W:(hh + 1) * W], start=True, stop=True),
                            r=[("inv", hh), "ones_rf"], w=["ps4"])
                        P.op("scalar", lambda e, psx=psx, hh=hh: e.copy(
                            ctxs[0:64, hh * W:(hh + 1) * W], psx[0:64, :]),
                            r=[pskey], w=[("ctxs", hh)])
                        if seq == 0 and hp == 0 and qp == 0 and hh == 0 and c.tapf is not None:
                            P.op("sync", lambda e: e.dma_start(out=c.tapf[0:64, 2904:3416],
                                 in_=ctxs[0:64, 0:W]), r=[("ctxs", 0)], w=["tpm"], dma=True)
                            P.op("sync", lambda e: e.dma_start(out=c.tapf[0:1, 3416:3928],
                                 in_=inv_r[0:1, 0:W]), r=[("inv", 0)], w=["tpm"], dma=True)
                        P.op("vector", lambda e, hh=hh: e.tensor_tensor(
                            ctxs[0:64, hh * W:(hh + 1) * W], ctxs[0:64, hh * W:(hh + 1) * W],
                            c.ps[4][0:64, :], ALU.mult),
                            r=[("ctxs", hh), "ps4"], w=[("ctxs", hh)])
                        h = 2 * hp + hh
                        P.op("sync", lambda e, h=h, hh=hh, qp=qp: e.dma_start(
                            out=c.ctxd[h * 64:(h + 1) * 64, seq * S + qp * W: seq * S + qp * W + W],
                            in_=ctxs[0:64, hh * W:(hh + 1) * W]),
                            r=[("ctxs", hh)], w=["ctxst"], dma=True)
            with pan_scope(c) as panZ:
                emit_closing(c, panZ, c.ctxd, c.x2d, c.x3d, seq, branch_key="ctxst")
            P.barrier()




    for seq in range(NSEQ):
        _seq(seq)

def emit_phaseD(c):
    nc, P = c.nc, c.P
    import contextlib
    def _seq(seq):
        with contextlib.ExitStack() as st:
            ent = st.enter_context
            lnfm = ent(_sb(nc, "D_lnfm", [128, KC * (S + 1)], BF16))
            nwsb = ent(_sb(nc, "D_nw", [128, KC * 45], F32))
            nbsb = ent(_sb(nc, "D_nb", [45, 1], F32))
            nsc = ent(_sb(nc, "D_nsc", [45, W], F32))
            ntl = ent(_sb(nc, "D_ntl", [128, 45], F32))
            bits = ent(_sb(nc, "D_bits", [128, 64], F32))
            selp = ent(_sb(nc, "D_sel", [128, 48], F32))
            selT = ent(_sb(nc, "D_selT", [48, S], F32))
            selTb = ent(_sb(nc, "D_selTb", [48, S], BF16))
            wstage = ent(_sb(nc, "D_wstage", [128, 2048], F32))
            hm = ent(_sb(nc, "D_hm", [128, KC * S], BF16))
            wsb = ent(_sb(nc, "D_wsb", [128, KC * 1024], BF16))
            b2p = ent(_sb(nc, "D_b2p", [48, D], F32))
            easb = ent(_sb(nc, "D_ea", [48, 3 * 1024], BF16))
            bias = ent(_sb(nc, "D_bias", [128, 8], F32))
            qtile = ent(_sb(nc, "D_qt", [128, W], F32))
            qbtile = ent(_sb(nc, "D_qbt", [128, W], BF16))

            P.op("sync", lambda e: e.dma_start(out=nwsb[:, 0:360], in_=c.I["d_nwT"]),
                 w=["nw"], dma=True)
            P.op("sync", lambda e: e.dma_start(out=nbsb[:], in_=c.I["d_nb"]), w=["nb"], dma=True)
            P.op("sync", lambda e: e.dma_start(out=easb[:], in_=c.I["E_all"]), w=["ea"], dma=True)
            for ch in range(KC):
                P.op("vector", lambda e, ch=ch: e.memset(lnfm[:, ch * (S + 1):ch * (S + 1) + 1], 0.0),
                     w=[("lnfm", ch, "z")])

            def nodes_cb(p, pan):
                t0 = p * W
                for k in range(KC):
                    P.op("tensor", lambda e, k=k: e.matmul(
                        c.ps[0][0:45, :], nwsb[:, k * 45:(k + 1) * 45],
                        pan.l[:, k * W:(k + 1) * W], start=(k == 0), stop=(k == KC - 1)),
                        r=[("panl", k), "nw"], w=["ps0"])
                P.op("vector", lambda e: e.tensor_scalar(
                    nsc[:], c.ps[0][0:45, :], nbsb[0:45, 0:1], None, ALU.add),
                    r=["ps0", "nb"], w=["nsc"])
                for sub in range(4):
                    P.op("tensor", lambda e, sub=sub: e.transpose(
                        c.ps[1][:, 0:45], nsc[0:45, sub * 128:(sub + 1) * 128], c.iden_f[0:45, 0:45]),
                        r=["nsc", "iden_f"], w=["ps1"])
                    P.op("vector", lambda e: e.tensor_scalar(
                        ntl[:], c.ps[1][:, 0:45], 0.0, None, ALU.is_ge),
                        r=["ps1"], w=["ntl"])
                    bx = lambda i: bits[:, i * 3:(i + 1) * 3]
                    gn = lambda n: ntl[:, n * 3:(n + 1) * 3]
                    P.op("vector", lambda e: e.tensor_copy(bx(0), gn(0)), r=["ntl"], w=["bits"])
                    P.op("vector", lambda e: e.tensor_tensor(bx(4), gn(2), gn(1), ALU.subtract),
                         r=["ntl"], w=["bits"])
                    P.op("vector", lambda e: e.tensor_tensor(bx(4), bx(4), bx(0), ALU.mult),
                         r=["bits"], w=["bits"])
                    P.op("vector", lambda e: e.tensor_tensor(bx(1), gn(1), bx(4), ALU.add),
                         r=["ntl", "bits"], w=["bits"])
                    P.op("vector", lambda e: e.tensor_tensor(bx(4), gn(4), gn(3), ALU.subtract),
                         r=["ntl"], w=["bits"])
                    P.op("vector", lambda e: e.tensor_tensor(bx(4), bx(4), bx(1), ALU.mult),
                         r=["bits"], w=["bits"])
                    P.op("vector", lambda e: e.tensor_tensor(bx(4), gn(3), bx(4), ALU.add),
                         r=["ntl", "bits"], w=["bits"])
                    P.op("vector", lambda e: e.tensor_tensor(bx(5), gn(6), gn(5), ALU.subtract),
                         r=["ntl"], w=["bits"])
                    P.op("vector", lambda e: e.tensor_tensor(bx(5), bx(5), bx(1), ALU.mult),
                         r=["bits"], w=["bits"])
                    P.op("vector", lambda e: e.tensor_tensor(bx(5), gn(5), bx(5), ALU.add),
                         r=["ntl", "bits"], w=["bits"])
                    P.op("vector", lambda e: e.tensor_tensor(bx(5), bx(5), bx(4), ALU.subtract),
                         r=["bits"], w=["bits"])
                    P.op("vector", lambda e: e.tensor_tensor(bx(5), bx(5), bx(0), ALU.mult),
                         r=["bits"], w=["bits"])
                    P.op("vector", lambda e: e.tensor_tensor(bx(2), bx(4), bx(5), ALU.add),
                         r=["bits"], w=["bits"])
                    for i in range(4):
                        P.op("vector", lambda e, i=i: e.tensor_tensor(
                            bits[:, (6 + i) * 3:(7 + i) * 3], gn(8 + 2 * i), gn(7 + 2 * i), ALU.subtract),
                            r=["ntl"], w=["bits"])
                        P.op("vector", lambda e, i=i: e.tensor_tensor(
                            bits[:, (6 + i) * 3:(7 + i) * 3], bits[:, (6 + i) * 3:(7 + i) * 3],
                            bx(2), ALU.mult), r=["bits"], w=["bits"])
                        P.op("vector", lambda e, i=i: e.tensor_tensor(
                            bits[:, (6 + i) * 3:(7 + i) * 3], gn(7 + 2 * i),
                            bits[:, (6 + i) * 3:(7 + i) * 3], ALU.add),
                            r=["ntl", "bits"], w=["bits"])
                    P.op("vector", lambda e: e.tensor_tensor(bx(4), bits[:, 21:24], bits[:, 18:21], ALU.subtract),
                         r=["bits"], w=["bits"])
                    P.op("vector", lambda e: e.tensor_tensor(bx(4), bx(4), bx(1), ALU.mult),
                         r=["bits"], w=["bits"])
                    P.op("vector", lambda e: e.tensor_tensor(bx(4), bits[:, 18:21], bx(4), ALU.add),
                         r=["bits"], w=["bits"])
                    P.op("vector", lambda e: e.tensor_tensor(bx(5), bits[:, 27:30], bits[:, 24:27], ALU.subtract),
                         r=["bits"], w=["bits"])
                    P.op("vector", lambda e: e.tensor_tensor(bx(5), bx(5), bx(1), ALU.mult),
                         r=["bits"], w=["bits"])
                    P.op("vector", lambda e: e.tensor_tensor(bx(5), bits[:, 24:27], bx(5), ALU.add),
                         r=["bits"], w=["bits"])
                    P.op("vector", lambda e: e.tensor_tensor(bx(5), bx(5), bx(4), ALU.subtract),
                         r=["bits"], w=["bits"])
                    P.op("vector", lambda e: e.tensor_tensor(bx(5), bx(5), bx(0), ALU.mult),
                         r=["bits"], w=["bits"])
                    P.op("vector", lambda e: e.tensor_tensor(bx(3), bx(4), bx(5), ALU.add),
                         r=["bits"], w=["bits"])
                    P.op("vector", lambda e: e.tensor_scalar(bx(4), bx(0), 2.0, None, ALU.mult),
                         r=["bits"], w=["bits"])
                    P.op("vector", lambda e: e.tensor_tensor(bx(4), bx(4), bx(1), ALU.add),
                         r=["bits"], w=["bits"])
                    P.op("vector", lambda e: e.tensor_scalar(bx(4), bx(4), 2.0, None, ALU.mult),
                         r=["bits"], w=["bits"])
                    P.op("vector", lambda e: e.tensor_tensor(bx(4), bx(4), bx(2), ALU.add),
                         r=["bits"], w=["bits"])
                    P.op("vector", lambda e: e.tensor_scalar(bx(4), bx(4), 2.0, None, ALU.mult),
                         r=["bits"], w=["bits"])
                    P.op("vector", lambda e: e.tensor_tensor(bx(4), bx(4), bx(3), ALU.add),
                         r=["bits"], w=["bits"])
                    for g in range(3):
                        P.op("vector", lambda e, g=g: e.tensor_scalar(
                            selp[:, g:48:3], c.iota16[:], bits[:, 12 + g:13 + g], None, ALU.is_equal),
                            r=["bits", "iota"], w=["selp"])
                    P.op("tensor", lambda e: e.transpose(
                        c.ps[1][0:48, 0:128], selp[:, 0:48], c.iden_f[:]),
                        r=["selp", "iden_f"], w=["ps1"])
                    tcol = t0 + sub * 128
                    P.op("vector", lambda e, tcol=tcol: e.tensor_copy(
                        selT[0:48, tcol:tcol + 128], c.ps[1][0:48, 0:128]),
                        r=["ps1"], w=["selT"])
                    P.op("vector", lambda e, tcol=tcol: e.tensor_copy(
                        selTb[0:48, tcol:tcol + 128], c.ps[1][0:48, 0:128]),
                        r=["ps1"], w=["selTb"])

            with pan_scope(c) as panL:
                for p in range(NP):
                    emit_ln_panel(c, panL, c.x3d, seq * S + p * W, lnfm=lnfm,
                                  lnfm_stride=S + 1, lnfm_off=1 + p * W,
                                  cb=nodes_cb, cb_arg=p)

            def lsl(k, t0, w_):
                return lnfm[:, k * (S + 1) + 1 + t0: k * (S + 1) + 1 + t0 + w_]

            for g, tag in enumerate(("q", "k", "v")):
                load_weight_bf(c, wstage, c.I[f"d_W1cat_{tag}"], wsb, f"D_w1_{g}", 1024)
                P.op("sync", lambda e, tag=tag: e.dma_start(
                    out=bias[:, 0:8], in_=c.I[f"d_b1cat_{tag}"]), w=["Dbias"], dma=True)
                for fc in range(KC):
                    for p in range(NP):
                        t0 = p * W
                        for k in range(KC):
                            P.op("tensor", lambda e, fc=fc, k=k, t0=t0, g=g: e.matmul(
                                c.ps[0][:], wsb[:, k * 1024 + fc * 128:k * 1024 + fc * 128 + 128],
                                lsl(k, t0, W), start=(k == 0), stop=(k == KC - 1)),
                                r=[f"D_w1_{g}", ("lnfm", k, t0 // W)], w=["ps0"])
                        P.op("scalar", lambda e, fc=fc, t0=t0: e.activation(
                            hm[:, fc * S + t0: fc * S + t0 + W], c.ps[0][:], AF.Relu,
                            bias=bias[:, fc:fc + 1]),
                            r=["ps0", "Dbias"], w=[("hm", fc)])
                        P.op("tensor", lambda e, fc=fc, t0=t0, g=g: e.matmul(
                            c.ps[1][:], easb[0:48, g * 1024 + fc * 128: g * 1024 + fc * 128 + 128],
                            selTb[0:48, t0:t0 + W], start=True, stop=True),
                            r=["selTb", "ea"], w=["ps1"])
                        P.op("vector", lambda e, fc=fc, t0=t0: e.tensor_tensor(
                            hm[:, fc * S + t0: fc * S + t0 + W],
                            hm[:, fc * S + t0: fc * S + t0 + W], c.ps[1][:], ALU.mult),
                            r=[("hm", fc), "ps1"], w=[("hm", fc)])
                load_weight_bf(c, wstage, c.I[f"d_W2cat_{tag}"], wsb, f"D_w2_{g}", 1024)
                P.op("sync", lambda e, g=g: e.dma_start(
                    out=b2p[:], in_=c.I["d_b2pad"][g]), w=["b2p"], dma=True)
                dst = (c.qd, c.kd, c.vd)[g]
                for dc in range(KC):
                    for p in range(NP):
                        t0 = p * W
                        for fc in range(KC):
                            P.op("tensor", lambda e, dc=dc, fc=fc, t0=t0, g=g: e.matmul(
                                c.ps[0][:], wsb[:, fc * 1024 + dc * 128:fc * 1024 + dc * 128 + 128],
                                hm[:, fc * S + t0: fc * S + t0 + W],
                                start=(fc == 0), stop=(fc == KC - 1)),
                                r=[f"D_w2_{g}", ("hm", fc)], w=["ps0"])
                        P.op("tensor", lambda e, dc=dc, t0=t0: e.matmul(
                            c.ps[1][:], b2p[0:48, dc * 128:(dc + 1) * 128], selT[0:48, t0:t0 + W],
                            start=True, stop=True),
                            r=["b2p", "selT"], w=["ps1"])
                        P.op("scalar", lambda e: e.copy(qtile[:, 0:W], c.ps[0][:]),
                             r=["ps0"], w=["qtile"])
                        P.op("vector", lambda e: e.tensor_tensor(
                            qbtile[:, 0:W], qtile[:, 0:W], c.ps[1][:], ALU.add),
                            r=["qtile", "ps1"], w=["qbf"])
                        P.op("sync", lambda e, dc=dc, t0=t0, dst=dst: e.dma_start(
                            out=dst[dc * 128:(dc + 1) * 128, seq * S + t0: seq * S + t0 + W],
                            in_=qbtile[:, 0:W]),
                            r=["qbf"], w=["qkvst"], dma=True)

        P.barrier()
        with contextlib.ExitStack() as st:
            ent = st.enter_context
            qfm = ent(_sb(nc, "D_qfm", [128, KC * S], BF16))
            kfm = ent(_sb(nc, "D_kfm", [128, KC * S], BF16))
            vtm = ent(_sb(nc, "D_vtm", [128, 16 * 1024], BF16))
            probs = ent(_sb(nc, "D_probs", [128, 16 * W], BF16))
            sqb = ent(_sb(nc, "D_sqb", [128, W], BF16))
            maxn = ent(_sb(nc, "D_maxn", [1, 8], F32))
            cbx = ent(_sb(nc, "D_cbx", [128, 1], F32))
            bcsb = ent(_sb(nc, "D_bcsb", [128, W], F32))
            ctxt = ent(_sb(nc, "D_ctxt", [128, W], F32))
            vstage = ent(_sb(nc, "D_vstage", [128, S], BF16))

            for ch in range(KC):
                P.op("sync", lambda e, ch=ch: e.dma_start(
                    out=qfm[:, ch * S:(ch + 1) * S],
                    in_=c.qd[ch * 128:(ch + 1) * 128, seq * S:(seq + 1) * S]),
                    r=["qkvst"], w=[("qfm", ch)], dma=True)
                P.op("sync", lambda e, ch=ch: e.dma_start(
                    out=kfm[:, ch * S:(ch + 1) * S],
                    in_=c.kd[ch * 128:(ch + 1) * 128, seq * S:(seq + 1) * S]),
                    r=["qkvst"], w=[("kfm", ch)], dma=True)
                P.op("sync", lambda e, ch=ch: e.dma_start(
                    out=vstage[:, 0:S],
                    in_=c.vd[ch * 128:(ch + 1) * 128, seq * S:(seq + 1) * S]),
                    r=["qkvst"], w=["vstage"], dma=True)
                for kt in range(16):
                    P.op("tensor", lambda e, kt=kt: e.transpose(
                        c.ps_tr[:, 0:128], vstage[:, kt * 128:(kt + 1) * 128], c.iden_bf[:]),
                        r=["vstage", "iden_b"], w=["ps_tr"])
                    P.op("vector", lambda e, kt=kt, ch=ch: e.tensor_copy(
                        vtm[:, kt * 1024 + ch * 128: kt * 1024 + ch * 128 + 128],
                        c.ps_tr[:, 0:128]),
                        r=["ps_tr"], w=[("vtm", kt, ch)])
            for which, fmref, fmkey, mi in (("q", qfm, "qfm", 0), ("k", kfm, "kfm", 1)):
                for p in range(NP):
                    t0 = p * W
                    for ch in range(KC):
                        P.op("scalar", lambda e, ch=ch, t0=t0, fmref=fmref: e.activation(
                            sqb[:, 0:W], fmref[:, ch * S + t0:ch * S + t0 + W], AF.Square),
                            r=[(fmkey, ch)], w=["sqb"])
                        P.op("tensor", lambda e, ch=ch: e.matmul(
                            c.ps_r1[:], c.ones_c_bf[:], sqb[:, 0:W],
                            start=(ch == 0), stop=(ch == KC - 1)),
                            r=["sqb", "ones_c"], w=["ps_r1"])
                    P.op("vector", lambda e: e.reduce_max(maxn[0:1, 2:3], c.ps_r1[:], axis=AX.X),
                         r=["ps_r1"], w=["mxs"])
                    if p == 0:
                        P.op("vector", lambda e, mi=mi: e.tensor_copy(
                            maxn[0:1, mi:mi + 1], maxn[0:1, 2:3]), r=["mxs"], w=[("mx", mi)])
                    else:
                        P.op("vector", lambda e, mi=mi: e.tensor_tensor(
                            maxn[0:1, mi:mi + 1], maxn[0:1, mi:mi + 1], maxn[0:1, 2:3], ALU.max),
                            r=["mxs", ("mx", mi)], w=[("mx", mi)])
            P.op("vector", lambda e: e.tensor_tensor(
                maxn[0:1, 2:3], maxn[0:1, 0:1], maxn[0:1, 1:2], ALU.mult),
                r=[("mx", 0), ("mx", 1)], w=["csc"])
            P.op("scalar", lambda e: e.activation(maxn[0:1, 2:3], maxn[0:1, 2:3], AF.Sqrt),
                 r=["csc"], w=["csc"])
            P.op("vector", lambda e: e.tensor_scalar(
                maxn[0:1, 2:3], maxn[0:1, 2:3], -1.0 / 32.0, None, ALU.mult),
                r=["csc"], w=["csc"])
            P.op("tensor", lambda e: e.matmul(
                c.ps[0][:, 0:1], c.ones_r_f[:], maxn[0:1, 2:3], start=True, stop=True),
                r=["csc", "ones_rf"], w=["ps0"])
            P.op("vector", lambda e: e.tensor_copy(cbx[:], c.ps[0][:, 0:1]), r=["ps0"], w=["cbx"])

            for qp in range(NP):
                q0 = qp * W
                for kc_i in range(16):
                    for ch in range(KC):
                        P.op("tensor", lambda e, kc_i=kc_i, ch=ch, q0=q0: e.matmul(
                            c.ps[0][:], kfm[:, ch * S + kc_i * 128: ch * S + kc_i * 128 + 128],
                            qfm[:, ch * S + q0: ch * S + q0 + W],
                            start=(ch == 0), stop=(ch == KC - 1)),
                            r=[("kfm", ch), ("qfm", ch)], w=["ps0"])
                    P.op("scalar", lambda e, kc_i=kc_i: e.activation(
                        probs[:, kc_i * W:(kc_i + 1) * W], c.ps[0][:], AF.Exp,
                        bias=cbx[:, 0:1], scale=1.0 / 32.0),
                        r=["ps0", "cbx"], w=[("probs", kc_i)])
                    P.op("tensor", lambda e, kc_i=kc_i: e.matmul(
                        c.ps_r1[:], c.ones_c_bf[:], probs[:, kc_i * W:(kc_i + 1) * W],
                        start=(kc_i == 0), stop=(kc_i == 15)),
                        r=[("probs", kc_i), "ones_c"], w=["ps_r1"])
                P.op("vector", lambda e: e.reciprocal(c.rows[0:1, 4 * W:5 * W], c.ps_r1[:]),
                     r=["ps_r1"], w=["inv"])
                P.op("tensor", lambda e: e.matmul(
                    c.ps[4][:], c.ones_r_f[:], c.rows[0:1, 4 * W:5 * W], start=True, stop=True),
                    r=["inv", "ones_rf"], w=["ps4"])
                P.op("vector", lambda e: e.tensor_copy(bcsb[:], c.ps[4][:]), r=["ps4"], w=["bcsb"])
                for dc in range(KC):
                    for kc_i in range(16):
                        P.op("tensor", lambda e, dc=dc, kc_i=kc_i: e.matmul(
                            c.ps[1][:], vtm[:, kc_i * 1024 + dc * 128: kc_i * 1024 + dc * 128 + 128],
                            probs[:, kc_i * W:(kc_i + 1) * W],
                            start=(kc_i == 0), stop=(kc_i == 15)),
                            r=[("vtm", kc_i, dc), ("probs", kc_i)], w=["ps1"])
                    P.op("scalar", lambda e: e.copy(ctxt[:, 0:W], c.ps[1][:]),
                         r=["ps1"], w=["ctxt"])
                    P.op("vector", lambda e: e.tensor_tensor(
                        ctxt[:, 0:W], ctxt[:, 0:W], bcsb[:], ALU.mult),
                        r=["ctxt", "bcsb"], w=["ctxt"])
                    P.op("sync", lambda e, dc=dc, q0=q0: e.dma_start(
                        out=c.ctxd[dc * 128:(dc + 1) * 128, seq * S + q0: seq * S + q0 + W],
                        in_=ctxt[:, 0:W]),
                        r=["ctxt"], w=["ctxst"], dma=True)
            with pan_scope(c) as panZ:
                emit_closing(c, panZ, c.ctxd, c.x3d, c.out_fm, seq, branch_key="ctxst")
        P.barrier()



    for seq in range(NSEQ):
        _seq(seq)


def kernel(**inputs):
    in_maps = host_layout(inputs)
    nc = build_program(serial=False)
    res = run_bass_kernel_spmd(nc, in_maps, core_ids=list(range(NCORE)))
    outs = []
    for ci in range(NCORE):
        o = np.asarray(res.results[ci]["out_fm"])
        outs.append(o.reshape(D, NSEQ, S).transpose(1, 2, 0))
    return np.concatenate(outs, axis=0).astype(np.float32)
